# version 04
# baseline (speedup 1.0000x reference)
import numpy as np
import contextlib
import concourse.bass as bass
import concourse.mybir as mybir
from concourse.bass_utils import run_bass_kernel_spmd

F32 = mybir.dt.float32
FP8 = mybir.dt.float8e4
BF16 = mybir.dt.bfloat16
I32 = mybir.dt.int32
AF = mybir.ActivationFunctionType
ALU = mybir.AluOpType

NC = 8


def derived(T, D, HPC, I):
    c = dict(T=T, D=D, HPC=HPC, I=I)
    c["ROWS"] = T // NC
    c["NT"] = T // 128
    c["DCH"] = D // 128
    c["DS"] = D // 512
    c["DPC"] = D // NC
    c["HCC"] = c["DPC"] // 128
    c["ICH"] = I // 128
    c["IPC"] = -(-c["ICH"] // NC)
    c["IPAD"] = c["IPC"] * 128 * NC
    c["NIB"] = -(-I // 512)
    c["ICHP"] = c["IPC"] * NC
    c["QS"] = 512
    c["NQS"] = T // 512
    c["QSC"] = 4
    c["AG1"] = 256
    c["NAG1"] = T // 256
    c["AGD"] = 2
    c["TT"] = c["ROWS"] // 128
    c["TG"] = min(512, c["ROWS"])
    c["NTG"] = c["ROWS"] // c["TG"]
    c["TGT"] = c["TG"] // 128
    c["OSL"] = HPC * 128
    assert T % 512 == 0 and T % (NC * 128) == 0 and D % 512 == 0 and I % 128 == 0
    assert c["TT"] <= 8
    return c


def build(T, D, HPC, I, dbg=False, upto=None):
    c = derived(T, D, HPC, I)
    ROWS, DCH, DS, DPC, HCC = c["ROWS"], c["DCH"], c["DS"], c["DPC"], c["HCC"]
    ICH, IPC, IPAD, NIB = c["ICH"], c["IPC"], c["IPAD"], c["NIB"]
    ICHP = c["ICHP"]
    NQS, QSC, AG1, NAG1, AGD = c["NQS"], c["QSC"], c["AG1"], c["NAG1"], c["AGD"]
    TT, TG, NTG, TGT, OSL = c["TT"], c["TG"], c["NTG"], c["TGT"], c["OSL"]
    INV_D = 1.0 / D
    QSCALE = float(1.0 / np.sqrt(128.0))
    EPS = 1e-5

    nc = bass.Bass("TRN2", target_bir_lowering=False, debug=False)

    x_full = nc.declare_dram_parameter("x_full", [T, D], F32, isOutput=False)
    x_my = nc.declare_dram_parameter("x_my", [ROWS, D], F32, isOutput=False)
    wqT_sl = nc.declare_dram_parameter("wqT_sl", [D, OSL], F32, isOutput=False)
    wkT_sl = nc.declare_dram_parameter("wkT_sl", [D, OSL], F32, isOutput=False)
    wvT_sl = nc.declare_dram_parameter("wvT_sl", [D, OSL], F32, isOutput=False)
    bq_sl = nc.declare_dram_parameter("bq_sl", [1, OSL], F32, isOutput=False)
    bk_sl = nc.declare_dram_parameter("bk_sl", [1, OSL], F32, isOutput=False)
    bv_sl = nc.declare_dram_parameter("bv_sl", [1, OSL], F32, isOutput=False)
    woT_sl = nc.declare_dram_parameter("woT_sl", [DPC, D], F32, isOutput=False)
    bo_in = nc.declare_dram_parameter("bo_in", [1, D], F32, isOutput=False)
    wgT_blk = nc.declare_dram_parameter("wgT_blk", [HCC * ICHP * 128, 128], I32,
                                        isOutput=False)
    wuT_blk = nc.declare_dram_parameter("wuT_blk", [HCC * ICHP * 128, 128], I32,
                                        isOutput=False)
    wdT_sl = nc.declare_dram_parameter("wdT_sl", [IPC * 128, D], I32, isOutput=False)
    sg_in = nc.declare_dram_parameter("sg_in", [1, IPAD], F32, isOutput=False)
    su_in = nc.declare_dram_parameter("su_in", [1, IPAD], F32, isOutput=False)
    sd_in = nc.declare_dram_parameter("sd_in", [1, D], F32, isOutput=False)
    cid_in = nc.declare_dram_parameter("cid_in", [1, 1], I32, isOutput=False)
    mask_in = nc.declare_dram_parameter("mask_in", [128, 4 * 512], BF16, isOutput=False)
    out = nc.declare_dram_parameter("out", [ROWS, D], F32, isOutput=True)

    def idram(name, shape, dt_):
        if dbg:
            return nc.declare_dram_parameter(name, shape, dt_, isOutput=True)
        return nc.dram_tensor(name, shape, dt_)

    xn_tm = idram("xn_tm", [T, D], BF16)
    xn_fm = idram("xn_fm", [D, T], BF16)
    wqT_bnc = nc.dram_tensor("wqT_bnc", [D, OSL], BF16)
    wkT_bnc = nc.dram_tensor("wkT_bnc", [D, OSL], BF16)
    wvT_bnc = nc.dram_tensor("wvT_bnc", [D, OSL], BF16)
    woT_loc = nc.dram_tensor("woT_loc", [DPC, D], BF16)
    woT_all = nc.dram_tensor("woT_all", [D, D], BF16, addr_space="Shared")
    wgT_loc = nc.dram_tensor("wgT_loc", [HCC * ICHP * 128, 128], BF16)
    wuT_loc = nc.dram_tensor("wuT_loc", [HCC * ICHP * 128, 128], BF16)
    wdT_loc = nc.dram_tensor("wdT_loc", [IPC * 128, D], BF16)
    wgT_all = nc.dram_tensor("wgT_all", [DCH * ICHP * 128, 128], BF16, addr_space="Shared")
    wuT_all = nc.dram_tensor("wuT_all", [DCH * ICHP * 128, 128], BF16, addr_space="Shared")
    wdT_all = nc.dram_tensor("wdT_all", [IPAD, D], BF16, addr_space="Shared")
    v_dram = idram("v_dram", [HPC * T, 128], BF16)
    o_loc = nc.dram_tensor("o_loc", [OSL, T], BF16)
    o_all = nc.dram_tensor("o_all", [D, T], BF16, addr_space="Shared")
    o_my = idram("o_my", [D, ROWS], BF16)
    x1_d = idram("x1_d", [ROWS, D], F32)
    xn1_tm = idram("xn1_tm", [ROWS, D], BF16)

    sem_n = {}
    sem_h = {}

    with contextlib.ExitStack() as st0:
        E = st0.enter_context
        block = E(nc.Block())

        def sget(name):
            if name not in sem_h:
                sem_h[name] = E(nc.semaphore(name))
                sem_n[name] = 0
            return sem_h[name]

        def sinc(instr, name, k=1):
            instr.then_inc(sget(name), k)
            sem_n[name] += k
            return sem_n[name]

        def sval(name):
            sget(name)
            return sem_n[name]

        def swait(eng, name, v):
            if v > 0:
                eng.wait_ge(sget(name), v)

        def swaitm(eng, mark):
            if mark is not None:
                swait(eng, mark[0], mark[1])

        ones_r = E(nc.sbuf_tensor("ones_r", [1, 128], F32))
        ones_c = E(nc.sbuf_tensor("ones_c", [128, 1], F32))
        ones_s = E(nc.sbuf_tensor("ones_s", [1, 1], F32))
        bqc = E(nc.sbuf_tensor("bqc", [128, HPC], F32))
        bkc = E(nc.sbuf_tensor("bkc", [128, HPC], F32))
        bv_bc = E(nc.sbuf_tensor("bv_bc", [128, OSL], F32))
        sgc = E(nc.sbuf_tensor("sgc", [128, ICH], F32))
        suc = E(nc.sbuf_tensor("suc", [128, ICH], F32))
        masks = E(nc.sbuf_tensor("masks", [128, QSC * 512], BF16))
        cid_sb = E(nc.sbuf_tensor("cid_sb", [1, 1], I32))

        @block.sync
        def _(sync):
            sinc(sync.dma_start(out=cid_sb[:], in_=cid_in[:, :]), "cid", 16)

        @block.sync
        def _(sync):
            sinc(sync.dma_start(out=masks[:], in_=mask_in[:, :]), "mskl", 16)

        @block.gpsimd
        def _(g):
            sinc(g.memset(ones_r[:], 1.0), "p0g")
            sinc(g.memset(ones_c[:], 1.0), "p0g")
            sinc(g.memset(ones_s[:], 1.0), "p0g")
        p0g_all = sval("p0g")

        jobs = [
            (bq_sl[:, :], OSL, "col", bqc),
            (bk_sl[:, :], OSL, "col", bkc),
            (bv_sl[:, :], OSL, "bc", bv_bc),
            (sg_in[0:1, 0:I], I, "col", sgc),
            (su_in[0:1, 0:I], I, "col", suc),
        ]
        maxw = max(j[1] for j in jobs)
        chunks_before = [0]
        for (_src, w, kind, _d) in jobs:
            chunks_before.append(chunks_before[-1] + (w // 128 if kind == "col" else -(-w // 512)))

        with nc.sbuf_tensor("rowbuf", [1, maxw], F32) as rowbuf, \
             nc.psum_tensor("pbc", [128, 512], F32) as pbc, \
             nc.psum_tensor("pbc2", [128, 512], F32) as pbc2:
            pb = [pbc, pbc2]

            @block.scalar
            def _(sc):
                for ji, (src, w, kind, dest) in enumerate(jobs):
                    swait(sc, "p0pe", chunks_before[ji])
                    sinc(sc.dma_start(out=rowbuf[0:1, 0:w], in_=src), "p0row", 16)

            @block.tensor
            def _(t):
                kg = 0
                swait(t, "p0g", 3)
                for ji, (src, w, kind, dest) in enumerate(jobs):
                    nch = w // 128 if kind == "col" else -(-w // 512)
                    swait(t, "p0row", 16 * (ji + 1))
                    for k in range(nch):
                        swait(t, "p0ev", max(0, kg - 1))
                        if kind == "col":
                            mm = t.matmul(pb[kg % 2][:, 0:1],
                                          rowbuf[0:1, k * 128:(k + 1) * 128],
                                          ones_s[:], start=True, stop=True)
                        else:
                            lo, hi = k * 512, min(w, k * 512 + 512)
                            mm = t.matmul(pb[kg % 2][:, 0:hi - lo], ones_r[:],
                                          rowbuf[0:1, lo:hi], start=True, stop=True)
                        sinc(mm, "p0pe")
                        kg += 1

            @block.vector
            def _(v):
                kg = 0
                for ji, (src, w, kind, dest) in enumerate(jobs):
                    nch = w // 128 if kind == "col" else -(-w // 512)
                    for k in range(nch):
                        swait(v, "p0pe", kg + 1)
                        if kind == "col":
                            cp = v.tensor_copy(out=dest[:, k:k + 1],
                                               in_=pb[kg % 2][:, 0:1])
                        else:
                            lo, hi = k * 512, min(w, k * 512 + 512)
                            cp = v.tensor_copy(out=dest[:, lo:hi],
                                               in_=pb[kg % 2][:, 0:hi - lo])
                        sinc(cp, "p0ev")
                        kg += 1

        def flat_aps(t, total_elems, nchunks):
            per = total_elems // 128 // nchunks
            return [t.ap().rearrange("a i -> (a i)").rearrange("(p x) -> p x", p=128)[
                :, k * per:(k + 1) * per] for k in range(nchunks)]

        cast_jobs = []
        for t_in, t_out_, nch in ((woT_sl, woT_loc, 8),
                                  (wqT_sl, wqT_bnc, 8), (wkT_sl, wkT_bnc, 8),
                                  (wvT_sl, wvT_bnc, 8)):
            n = 1
            for d_ in t_in.shape:
                n *= d_
            ins_l = flat_aps(t_in, n, nch)
            outs_l = flat_aps(t_out_, n, nch)
            cast_jobs += list(zip(ins_l, outs_l, [n // 128 // nch] * nch))
        cw_qkv_jobs = len(cast_jobs)
        for t_in, t_out_, nch in ((wgT_blk, wgT_loc, 16), (wuT_blk, wuT_loc, 16),
                                  (wdT_sl, wdT_loc, 16)):
            n = 1
            for d_ in t_in.shape:
                n *= d_
            ins_l = flat_aps(t_in, n, nch)
            outs_l = flat_aps(t_out_, n, nch)
            cast_jobs += list(zip(ins_l, outs_l, [n // 128 // nch] * nch))
        cw_wg_jobs = cw_qkv_jobs + 16
        cw_wu_jobs = cw_qkv_jobs + 32
        maxper = max(w for (_i, _o, w) in cast_jobs)

        cnvb = [E(nc.sbuf_tensor(f"cnv{i}", [128, maxper], BF16, side="right"))
                for i in range(2)]

        @block.gpsimd
        def _(g):
            for kk, (iap, oap, w) in enumerate(cast_jobs):
                swait(g, "cw", 16 * (kk - 1))
                sinc(g.dma_start(out=cnvb[kk % 2][:, 0:w], in_=iap), "cl", 16)
                swait(g, "cl", 16 * (kk + 1))
                swait(g, "cw", 16 * kk)
                sinc(g.dma_start(out=oap, in_=cnvb[kk % 2][:, 0:w]), "cw", 16)

        cw_qkv = 16 * cw_qkv_jobs
        cw_wg = 16 * cw_wg_jobs
        cw_wu = 16 * cw_wu_jobs
        cw_all = sval("cw")

        st_LN = E(contextlib.ExitStack())
        EL = st_LN.enter_context
        lb = {
            "xb": [EL(nc.sbuf_tensor(f"xb{i}", [128, D], F32, side="right")) for i in range(2)],
            "junk_d": [EL(nc.sbuf_tensor(f"junkd{i}", [128, D], FP8, side="right")) for i in range(2)],
            "junk_a": [EL(nc.sbuf_tensor(f"junka{i}", [128, D], FP8, side="right")) for i in range(2)],
            "sum": [EL(nc.sbuf_tensor(f"lsum{i}", [128, 1], F32, side="right")) for i in range(2)],
            "sq": [EL(nc.sbuf_tensor(f"lsq{i}", [128, 1], F32, side="right")) for i in range(2)],
            "m2": [EL(nc.sbuf_tensor(f"lm2{i}", [128, 1], F32, side="right")) for i in range(2)],
            "var": [EL(nc.sbuf_tensor(f"lvar{i}", [128, 1], F32, side="right")) for i in range(2)],
            "lnv": [EL(nc.sbuf_tensor(f"llnv{i}", [128, 1], F32, side="right")) for i in range(2)],
            "var2": [EL(nc.sbuf_tensor(f"lvar2{i}", [128, 1], F32, side="right")) for i in range(2)],
            "rstd": [EL(nc.sbuf_tensor(f"lrstd{i}", [128, 1], F32, side="right")) for i in range(2)],
            "negm": [EL(nc.sbuf_tensor(f"lnegm{i}", [128, 1], F32, side="right")) for i in range(2)],
            "xnb": [EL(nc.sbuf_tensor(f"lxnb{i}", [128, D], BF16, side="right")) for i in range(2)],
        }
        xfs = [EL(nc.sbuf_tensor(f"xfs{i}", [128, 512], BF16, is_transpose_dest=True, side="right"))
               for i in range(2)]

        def emit_ln_tile(trow, t, src, dst, gate_marks=()):
            pfx = "L"
            q = t % 2

            @block.sync
            def _(sync):
                swait(sync, pfx + "an", t - 1)
                for m in gate_marks:
                    swaitm(sync, m)
                sinc(sync.dma_start(out=lb["xb"][q][:],
                                    in_=src[trow * 128:(trow + 1) * 128, :]),
                     pfx + f"lx{q}", 16)

            @block.vector
            def _(v):
                swait(v, pfx + f"lx{q}", 16 * (t // 2 + 1))
                swait(v, pfx + "d1", t - 1)
                sinc(v.tensor_scalar(out=lb["junk_d"][q][:], in0=lb["xb"][q][:],
                                     scalar1=1.0, scalar2=0.0, op0=ALU.mult,
                                     op1=ALU.add,
                                     accum_out=lb["sum"][q][:]), pfx + "d1")

            @block.scalar
            def _(sc):
                swait(sc, pfx + f"lx{q}", 16 * (t // 2 + 1))
                swait(sc, pfx + "a2", t - 1)
                sinc(sc.activation(out=lb["junk_a"][q][:], in_=lb["xb"][q][:],
                                   func=AF.Square, accum_out=lb["sq"][q][:]),
                     pfx + "a2")

            @block.vector
            def _(v):
                swait(v, pfx + "d1", t + 1)
                swait(v, pfx + "dv1", t - 1)
                sinc(v.tensor_mul(lb["m2"][q][:], lb["sum"][q][:], lb["sum"][q][:]),
                     pfx + "dv1")
                swait(v, pfx + "dv1", t + 1)
                swait(v, pfx + "a2", t + 1)
                swait(v, pfx + "dv2", t - 1)
                sinc(v.scalar_tensor_tensor(out=lb["var"][q][:], in0=lb["m2"][q][:],
                                            scalar=-INV_D, in1=lb["sq"][q][:],
                                            op0=ALU.mult, op1=ALU.add), pfx + "dv2")
                swait(v, pfx + "dv2", t + 1)
                swait(v, pfx + "dv2b", t - 1)
                sinc(v.tensor_scalar(out=lb["var2"][q][:], in0=lb["var"][q][:],
                                     scalar1=INV_D, scalar2=EPS,
                                     op0=ALU.mult, op1=ALU.add), pfx + "dv2b")

            @block.scalar
            def _(sc):
                swait(sc, pfx + "dv2b", t + 1)
                swait(sc, pfx + "aln", t - 1)
                sinc(sc.activation(out=lb["lnv"][q][:], in_=lb["var2"][q][:],
                                   func=AF.Ln), pfx + "aln")
                swait(sc, pfx + "aln", t + 1)
                swait(sc, pfx + "aex", t - 1)
                sinc(sc.activation(out=lb["rstd"][q][:], in_=lb["lnv"][q][:],
                                   func=AF.Exp, scale=-0.5), pfx + "aex")

            @block.vector
            def _(v):
                swait(v, pfx + "aex", t + 1)
                swait(v, pfx + "dv3", t - 1)
                sinc(v.tensor_scalar(out=lb["negm"][q][:], in0=lb["sum"][q][:],
                                     scalar1=lb["rstd"][q][:], scalar2=-INV_D,
                                     op0=ALU.mult, op1=ALU.mult), pfx + "dv3")

            @block.scalar
            def _(sc):
                swait(sc, pfx + "dv3", t + 1)
                if t >= 2:
                    swait(sc, pfx + f"xnw{q}", 16 * ((t - 2) // 2 + 1))
                sinc(sc.activation(out=lb["xnb"][q][:], in_=lb["xb"][q][:],
                                   func=AF.Identity, scale=lb["rstd"][q][:],
                                   bias=lb["negm"][q][:]), pfx + "an")

            @block.sync
            def _(sync):
                swait(sync, pfx + "an", t + 1)
                sinc(sync.dma_start(out=dst[trow * 128:(trow + 1) * 128, :],
                                    in_=lb["xnb"][q][:]), pfx + f"xnw{q}", 16)

        xf_last = {0: 0, 1: 0}
        xf_k = [0]

        def emit_xbar_bounce(sync, in_ap, out_ap, out_3d=False):
            sl = xf_k[0] % 2
            xf_k[0] += 1
            n = in_ap.shape[0]
            swait(sync, f"xfw{sl}", xf_last[sl])
            v = sinc(sync.dma_start(out=xfs[sl][:, 0:n],
                                    in_=in_ap, transpose=True), f"xfl{sl}", 16)
            swait(sync, f"xfl{sl}", v)
            src_ap = (xfs[sl][:, 0:n].rearrange("p (q i) -> p q i", i=128)
                      if out_3d else xfs[sl][:, 0:n])
            xf_last[sl] = sinc(sync.dma_start(out=out_ap, in_=src_ap),
                               f"xfw{sl}", 16)

        def emit_xnfm_strip(s):
            @block.sync
            def _(sync):
                swait(sync, "Lxnw0", 16 * (2 * s + 2))
                swait(sync, "Lxnw1", 16 * (2 * s + 2))
                for dc in range(DCH):
                    emit_xbar_bounce(sync,
                                     xn_tm[s * 512:(s + 1) * 512,
                                           dc * 128:(dc + 1) * 128],
                                     xn_fm[dc * 128:(dc + 1) * 128,
                                           s * 512:(s + 1) * 512])

        st_A = E(contextlib.ExitStack())
        EA = st_A.enter_context
        stp = [EA(nc.sbuf_tensor(f"stp{i}", [128, DCH * AG1], BF16)) for i in range(2)]
        K_h = EA(nc.sbuf_tensor("K_h", [128, T], BF16))
        Q_h = EA(nc.sbuf_tensor("Q_h", [128, T], BF16))
        wqT_sb = EA(nc.sbuf_tensor("wqT_sb", [128, DCH * 128], BF16,
                                   is_transpose_dest=True))
        wkT_sb = EA(nc.sbuf_tensor("wkT_sb", [128, DCH * 128], BF16,
                                   is_transpose_dest=True))
        v_sb = [EA(nc.sbuf_tensor(f"v_sb{i}", [128, OSL], BF16)) for i in range(2)]

        strip_free = []
        a1v = {"v": 0}
        a2st = {"s": 0, "chunk": 0, "ptmark": [], "lm_mark": [], "on_mark": [],
                "av_mark": [], "lacp_prev": None, "psl_free": []}

        if dbg:
            bnc_dbg = nc.declare_dram_parameter("bnc_dbg", [128, D], BF16,
                                                isOutput=True)

        def emit_wqkT_load(h, prev_head_mark):
            @block.scalar
            def _(sc):
                swait(sc, "cw", cw_qkv)
                if prev_head_mark is not None:
                    swaitm(sc, prev_head_mark)
                for wsrc, dst in ((wqT_bnc, wqT_sb), (wkT_bnc, wkT_sb)):
                    for cg in range(0, DCH, 8):
                        sinc(sc.dma_start(
                            out=dst[:, cg * 128:(cg + 8) * 128].rearrange(
                                "p (c o) -> p c o", c=8),
                            in_=wsrc[cg * 128:(cg + 8) * 128,
                                     h * 128:(h + 1) * 128].rearrange(
                                "(c p) o -> p c o", p=128)), "wqkx", 16)

        a1ev = {"K": 0, "Q": 0, "V": 0}

        def emit_a1_group(h, g, gctr, pKap, pQap, pVap=None, wvT_sb=None,
                          head_first_wait=None):
            sl = gctr % 2

            @block.scalar
            def _(sc):
                s_idx = g // 2
                swait(sc, "xfw0", 16 * (DCH // 2) * (s_idx + 1))
                swait(sc, "xfw1", 16 * (DCH // 2) * (s_idx + 1))
                if gctr >= 2:
                    swaitm(sc, strip_free[gctr - 2])
                for cg in range(0, DCH, 8):
                    sinc(sc.dma_start(
                        out=stp[sl][:, cg * AG1:(cg + 8) * AG1].rearrange(
                            "p (c t) -> p c t", c=8),
                        in_=xn_fm[cg * 128:(cg + 8) * 128, :].rearrange(
                            "(c p) t -> p c t", p=128)[
                            :, :, g * AG1:(g + 1) * AG1]), f"stl{sl}", 16)

            @block.tensor
            def _(t):
                swait(t, f"stl{sl}", 16 * (DCH // 8) * (gctr // 2 + 1))
                if g == 0:
                    swait(t, "wqkx", sval("wqkx"))
                    if head_first_wait is not None:
                        for m in head_first_wait:
                            swaitm(t, m)
                swait(t, "evA1", a1ev["K"])
                for dc in range(DCH):
                    mmk = t.matmul(pKap, wkT_sb[:, dc * 128:(dc + 1) * 128],
                                   stp[sl][:, dc * AG1:(dc + 1) * AG1],
                                   start=(dc == 0), stop=(dc == DCH - 1))
                sinc(mmk, "pA1")

            @block.vector
            def _(v):
                swait(v, "pA1", sval("pA1"))
                a1ev["K"] = sinc(
                    v.tensor_scalar(out=K_h[:, g * AG1:(g + 1) * AG1], in0=pKap,
                                    scalar1=bkc[:, h:h + 1], scalar2=None,
                                    op0=ALU.add), "evA1")

            @block.tensor
            def _(t):
                swait(t, "evA1", a1ev["Q"])
                for dc in range(DCH):
                    mmq = t.matmul(pQap, wqT_sb[:, dc * 128:(dc + 1) * 128],
                                   stp[sl][:, dc * AG1:(dc + 1) * AG1],
                                   start=(dc == 0), stop=(dc == DCH - 1))
                sinc(mmq, "pA1")

            @block.vector
            def _(v):
                swait(v, "pA1", sval("pA1"))
                a1ev["Q"] = sinc(
                    v.tensor_scalar(out=Q_h[:, g * AG1:(g + 1) * AG1], in0=pQap,
                                    scalar1=bqc[:, h:h + 1], scalar2=QSCALE,
                                    op0=ALU.add, op1=ALU.mult), "evA1")

            if h == 0:
                for tt in range(AGD):
                    vctr = a1v["v"]
                    a1v["v"] += 1

                    @block.tensor
                    def _(t):
                        swait(t, "evA1", a1ev["V"])
                        if g == 0 and tt == 0:
                            swait(t, "wqkx", sval("wqkx"))
                        for dc in range(DCH):
                            mmv = t.matmul(pVap,
                                           stp[sl][:, dc * AG1 + tt * 128:
                                                   dc * AG1 + (tt + 1) * 128],
                                           wvT_sb[:, dc * OSL:(dc + 1) * OSL],
                                           start=(dc == 0), stop=(dc == DCH - 1))
                        sinc(mmv, "pA1")

                    @block.vector
                    def _(v):
                        swait(v, "pA1", sval("pA1"))
                        swait(v, f"vw{vctr % 2}", 16 * HPC * (vctr // 2))
                        a1ev["V"] = sinc(
                            v.tensor_add(v_sb[vctr % 2][:, :], pVap, bv_bc[:, :]),
                            "evA1")

                    @block.scalar
                    def _(sc):
                        swait(sc, "evA1", a1ev["V"])
                        row = g * AG1 + tt * 128
                        for hh in range(HPC):
                            sinc(sc.dma_start(
                                out=v_dram[hh * T + row:hh * T + row + 128, :],
                                in_=v_sb[vctr % 2][:, hh * 128:(hh + 1) * 128]),
                                f"vw{vctr % 2}", 16)

            strip_free.append(("pA1", sval("pA1")))

        def emit_vh_load(h):
            @block.scalar
            def _(sc):
                swait(sc, "vw0", 16 * HPC * (T // 256))
                swait(sc, "vw1", 16 * HPC * (T // 256))
                if a2st["av_mark"]:
                    swaitm(sc, a2st["av_mark"][-1])
                KT = T // 128
                for kg in range(0, KT, 8):
                    sinc(sc.dma_start(
                        out=vbuf[:, kg * 128:(kg + 8) * 128].rearrange(
                            "p (k f) -> p k f", k=8),
                        in_=v_dram[h * T + kg * 128:h * T + (kg + 8) * 128, :]
                        .rearrange("(k p) f -> p k f", p=128)), "vhl", 16)

        def emit_a2_strip(h, s, use_pool, psS, psO, psL, psBc):
            kmax = QSC * (s + 1)
            sctr = a2st["s"]
            a2st["s"] += 1

            for kc in range(kmax):
                diag = kc >= QSC * s
                j = kc - QSC * s
                eng_pool = use_pool and (kc % 2 == 1)
                cglob = a2st["chunk"]
                a2st["chunk"] += 1

                @block.tensor
                def _(t):
                    if s == 0 and kc == 0:
                        swait(t, "p0g", p0g_all)
                        swait(t, "mskl", 16)
                        swait(t, "evA1", sval("evA1"))
                    swait(t, "exp", max(0, sval("st") - 1))
                    sinc(t.matmul(psS[kc % 2][:, :], K_h[:, kc * 128:(kc + 1) * 128],
                                  Q_h[:, s * 512:(s + 1) * 512],
                                  start=True, stop=True), "st")

                @block.scalar
                def _(sc):
                    swait(sc, "st", sval("st"))
                    if cglob >= 3:
                        for m in a2st["ptmark"][cglob - 3]:
                            swaitm(sc, m)
                    sinc(sc.activation(out=pt[cglob % 3][:, :],
                                       in_=psS[kc % 2][:, :], func=AF.Exp), "exp")

                if diag:
                    @block.vector
                    def _(v):
                        if sval("msk") == 0:
                            swait(v, "mskl", 16)
                        swait(v, "exp", sval("exp"))
                        swait(v, "msk", max(0, sval("msk") - 1))
                        sinc(v.tensor_mul(pt[cglob % 3][:, :], pt[cglob % 3][:, :],
                                          masks[:, j * 512:(j + 1) * 512]), "msk")
                ready = ("msk", sval("msk")) if diag else ("exp", sval("exp"))

                if eng_pool:
                    @block.gpsimd
                    def _(g):
                        swaitm(g, ready)
                        if kc == 1:
                            swaitm(g, a2st["lacp_prev"])
                            op = g.tensor_copy(out=accP[:, :], in_=pt[cglob % 3][:, :])
                        else:
                            swait(g, "lacP", sval("lacP"))
                            op = g.tensor_add(accP[:, :], accP[:, :],
                                              pt[cglob % 3][:, :])
                        sinc(op, "lacP")
                    lmark = ("lacP", sval("lacP"))
                else:
                    @block.vector
                    def _(v):
                        swaitm(v, ready)
                        if kc == 0:
                            if a2st["lm_mark"]:
                                swaitm(v, a2st["lm_mark"][-1])
                            op = v.tensor_copy(out=accD[:, :], in_=pt[cglob % 3][:, :])
                        else:
                            swait(v, "lacD", sval("lacD"))
                            op = v.tensor_add(accD[:, :], accD[:, :],
                                              pt[cglob % 3][:, :])
                        sinc(op, "lacD")
                    lmark = ("lacD", sval("lacD"))

                @block.tensor
                def _(t):
                    swaitm(t, ready)
                    swait(t, "vhl", 16 * (T // 1024) * (h + 1))
                    if kc == 0 and sctr >= 2:
                        swaitm(t, a2st["on_mark"][sctr - 2])
                    sinc(t.matmul(psO[sctr % 2][:, :],
                                  vbuf[:, kc * 128:(kc + 1) * 128],
                                  pt[cglob % 3][:, :],
                                  start=(kc == 0), stop=(kc == kmax - 1)), "av")
                a2st["ptmark"].append([("av", sval("av")), lmark])

            a2st["av_mark"].append(("av", sval("av")))

            if use_pool:
                @block.vector
                def _(v):
                    swait(v, "lacP", sval("lacP"))
                    swait(v, "lacD", sval("lacD"))
                    sinc(v.tensor_add(accD[:, :], accD[:, :], accP[:, :]), "lacD")
                a2st["lacp_prev"] = ("lacD", sval("lacD"))

            @block.tensor
            def _(t):
                swait(t, "lacD", sval("lacD"))
                if sctr >= 2:
                    swaitm(t, a2st["psl_free"][sctr - 2])
                sinc(t.matmul(psL[sctr % 2][0:1, :], ones_c[:, :], accD[:, :],
                              start=True, stop=True), "lm")
            a2st["lm_mark"].append(("lm", sval("lm")))

            @block.scalar
            def _(sc):
                swait(sc, "lm", sval("lm"))
                swait(sc, "lin", max(0, sval("lin") - 2))
                sinc(sc.activation(out=lnl_r[sctr % 2][:, :],
                                   in_=psL[sctr % 2][0:1, :], func=AF.Ln), "lin")
                a2st["psl_free"].append(("lin", sval("lin")))
                swait(sc, "lin", sval("lin"))
                swait(sc, "bcm", sval("bcm"))
                sinc(sc.activation(out=linv_r[sctr % 2][:, :],
                                   in_=lnl_r[sctr % 2][:, :],
                                   func=AF.Exp, scale=-1.0), "lin")

            @block.tensor
            def _(t):
                swait(t, "lin", sval("lin"))
                swait(t, "bce", max(0, sval("bcm")))
                sinc(t.matmul(psBc[:, :], ones_r[:, :], linv_r[sctr % 2][:, :],
                              start=True, stop=True), "bcm")

            @block.vector
            def _(v):
                swait(v, "bcm", sval("bcm"))
                swait(v, "on", sval("on"))
                sinc(v.tensor_copy(out=linv_bc[:, :], in_=psBc[:, :]), "bce")
                swait(v, "bce", sval("bce"))
                swait(v, "av", sval("av"))
                swait(v, f"ow{sctr % 2}", 16 * (sctr // 2))
                sinc(v.tensor_mul(o_sb[sctr % 2][:, :], psO[sctr % 2][:, :],
                                  linv_bc[:, :]), "on")
            a2st["on_mark"].append(("on", sval("on")))

            @block.scalar
            def _(sc):
                swait(sc, "on", sval("on"))
                sinc(sc.dma_start(out=o_loc[h * 128:(h + 1) * 128,
                                            s * 512:(s + 1) * 512],
                                  in_=o_sb[sctr % 2][:, :]), f"ow{sctr % 2}", 16)

        a1_gctr = 0
        with nc.psum_tensor("psK", [128, AG1], F32) as psK, \
             nc.psum_tensor("psQ", [128, AG1], F32) as psQ, \
             nc.psum_tensor("psV", [128, OSL], F32) as psV, \
             nc.sbuf_tensor("wvT_sb", [128, DCH * OSL], BF16,
                            is_transpose_dest=True) as wvT_sb:

            @block.scalar
            def _(sc):
                swait(sc, "cw", cw_qkv)
                for cg in range(0, DCH, 8):
                    sinc(sc.dma_start(
                        out=wvT_sb[:, cg * OSL:(cg + 8) * OSL].rearrange(
                            "p (c o) -> p c o", c=8),
                        in_=wvT_bnc[cg * 128:(cg + 8) * 128, :].rearrange(
                            "(c p) o -> p c o", p=128)), "wqkx", 16)

            emit_wqkT_load(0, None)
            for s in range(NQS):
                for t4 in range(4):
                    emit_ln_tile(4 * s + t4, 4 * s + t4, x_full, xn_tm)
                emit_xnfm_strip(s)
                for gg in range(2):
                    g = 2 * s + gg
                    emit_a1_group(0, g, a1_gctr, psK[:, :], psQ[:, :],
                                  pVap=psV[:, :], wvT_sb=wvT_sb,
                                  head_first_wait=[("p0ev", sval("p0ev"))] if g == 0 else None)
                    a1_gctr += 1

        vbuf = EA(nc.sbuf_tensor("vbuf", [128, T], BF16))
        pt = [EA(nc.sbuf_tensor(f"pt{i}", [128, 512], BF16)) for i in range(3)]
        accD = EA(nc.sbuf_tensor("accD", [128, 512], F32))
        accP = EA(nc.sbuf_tensor("accP", [128, 512], F32))
        linv_r = [EA(nc.sbuf_tensor(f"linv{i}", [1, 512], F32)) for i in range(2)]
        lnl_r = [EA(nc.sbuf_tensor(f"lnl{i}", [1, 512], F32)) for i in range(2)]
        linv_bc = EA(nc.sbuf_tensor("linv_bc", [128, 512], F32))
        o_sb = [EA(nc.sbuf_tensor(f"o_sb{i}", [128, 512], BF16)) for i in range(2)]

        with nc.psum_tensor("psS0", [128, 512], F32) as psS0, \
             nc.psum_tensor("psS1", [128, 512], F32) as psS1, \
             nc.psum_tensor("psO0", [128, 512], F32) as psO0, \
             nc.psum_tensor("psO1", [128, 512], F32) as psO1, \
             nc.psum_tensor("psL0", [1, 512], F32) as psL0, \
             nc.psum_tensor("psL1", [1, 512], F32) as psL1, \
             nc.psum_tensor("psBc", [128, 512], F32) as psBc:
            psS = [psS0, psS1]
            psO = [psO0, psO1]
            psL = [psL0, psL1]
            emit_vh_load(0)
            for s in range(NQS):
                emit_a2_strip(0, s, False, psS, psO, psL, psBc)

            if dbg:
                kq_dbg = nc.declare_dram_parameter("kq_dbg", [2 * OSL, T], BF16,
                                                   isOutput=True)

            def emit_kq_dump(h):
                if not dbg:
                    return

                @block.scalar
                def _(sc):
                    swait(sc, "evA1", sval("evA1"))
                    sinc(sc.dma_start(out=kq_dbg[h * 128:(h + 1) * 128, :],
                                      in_=K_h[:, :]), "kqd", 16)
                    sinc(sc.dma_start(out=kq_dbg[OSL + h * 128:OSL + (h + 1) * 128, :],
                                      in_=Q_h[:, :]), "kqd", 16)

            emit_kq_dump(0)
            for h in range(1, HPC):
                emit_wqkT_load(h, strip_free[-1])
                for g in range(NAG1):
                    emit_a1_group(h, g, a1_gctr, psS0[:, 0:AG1], psS1[:, 0:AG1],
                                  head_first_wait=[("exp", sval("exp"))] if g == 0 else None)
                    a1_gctr += 1
                emit_kq_dump(h)
                emit_vh_load(h)
                for s in range(NQS):
                    emit_a2_strip(h, s, h >= 2, psS, psO, psL, psBc)

        @block.gpsimd
        def _(g):
            swait(g, "cw", cw_qkv)
            sinc(g.collective_compute(
                "AllGather", ALU.bypass, replica_groups=[list(range(NC))],
                ins=[woT_loc.ap().opt()], outs=[woT_all.ap().opt()]), "cc")
            for cwv, loc_t, all_t in ((cw_wg, wgT_loc, wgT_all),
                                      (cw_wu, wuT_loc, wuT_all),
                                      (cw_all, wdT_loc, wdT_all)):
                swait(g, "cw", cwv)
                sinc(g.collective_compute(
                    "AllGather", ALU.bypass, replica_groups=[list(range(NC))],
                    ins=[loc_t.ap().opt()], outs=[all_t.ap().opt()]), "cc")


        @block.gpsimd
        def _(g):
            swait(g, "ow0", 16 * ((HPC * NQS + 1) // 2))
            swait(g, "ow1", 16 * ((HPC * NQS) // 2))
            sinc(g.collective_compute(
                "AllGather", ALU.bypass, replica_groups=[list(range(NC))],
                ins=[o_loc.ap().opt()], outs=[o_all.ap().opt()]), "cc")

        if dbg:
            o_loc_dbg = nc.declare_dram_parameter("o_loc_dbg", [OSL, T], BF16,
                                                  isOutput=True)

            @block.sync
            def _(sync):
                swait(sync, "cc", 5)
                for hh in range(HPC):
                    sinc(sync.dma_start(
                        out=o_loc_dbg[hh * 128:(hh + 1) * 128, :],
                        in_=o_loc[hh * 128:(hh + 1) * 128, :]), "odbg", 16)
                swait(sync, "odbg", 16 * HPC)

        st_A.close()

        if upto == "A":
            @block.sync
            def _(sync):
                swait(sync, "cc", 5)
                for t in range(TT):
                    sinc(sync.dma_start(out=out[t * 128:(t + 1) * 128, :],
                                        in_=x_my[t * 128:(t + 1) * 128, :]),
                         "outw0", 16)
                swait(sync, "outw0", sval("outw0"))
            return nc, c

        st_O = E(contextlib.ExitStack())
        EO = st_O.enter_context
        bo_bc = EO(nc.sbuf_tensor("bo_bc", [128, D], F32))
        attn_tail0 = [("exp", sval("exp")), ("on", sval("on")),
                      ("lin", sval("lin")), ("bce", sval("bce"))]
        bo_base = {"row": sval("p0row"), "pe": sval("p0pe"), "ev": sval("p0ev")}
        with nc.sbuf_tensor("rowb2", [1, D], F32) as rowb2, \
             nc.psum_tensor("psBo", [128, 512], F32) as psBo:
            @block.scalar
            def _(sc):
                sinc(sc.dma_start(out=rowb2[0:1, :], in_=bo_in[:, :]), "p0row", 16)

            @block.tensor
            def _(t):
                swait(t, "p0row", bo_base["row"] + 16)
                for m in attn_tail0:
                    swaitm(t, m)
                for k in range(DS):
                    swait(t, "p0ev", bo_base["ev"] + k)
                    sinc(t.matmul(psBo[:, :], ones_r[:],
                                  rowb2[0:1, k * 512:(k + 1) * 512],
                                  start=True, stop=True), "p0pe")

            @block.vector
            def _(v):
                for k in range(DS):
                    swait(v, "p0pe", bo_base["pe"] + k + 1)
                    sinc(v.tensor_copy(out=bo_bc[:, k * 512:(k + 1) * 512],
                                       in_=psBo[:, :]), "p0ev")
        bo_ev_done = sval("p0ev")
        att_sb = EO(nc.sbuf_tensor("att_sb", [128, DCH * ROWS], BF16))
        wo_t = [EO(nc.sbuf_tensor(f"wo_t{i}", [128, 512], BF16)) for i in range(2)]
        tmp_o = [EO(nc.sbuf_tensor(f"tmp_o{i}", [128, 512], F32)) for i in range(2)]
        x1sl = [EO(nc.sbuf_tensor(f"x1sl{i}", [128, 512], F32)) for i in range(2)]
        xsl = [EO(nc.sbuf_tensor(f"xsl{i}", [128, 512], F32)) for i in range(2)]

        @block.sync
        def _(sync):
            swait(sync, "cc", 5)
            swait(sync, "cid", 16)
            reg = sync.value_load(cid_sb[0:1, 0:1])
            reg = sync.snap(reg, min_val=0, max_val=NC - 1)
            for b in sync.Switch(reg, NC):
                for rb in range(8):
                    r0 = rb * (D // 8)
                    sync.dma_start(
                        out=o_my[r0:r0 + D // 8, :],
                        in_=o_all[r0:r0 + D // 8, b * ROWS:(b + 1) * ROWS]
                    ).then_inc(sget("omy"), 16)
            sem_n["omy"] = 16 * 8
            swait(sync, "omy", 16 * 8)
            swait(sync, "p0row", sval("p0row"))
            swait(sync, "p0pe", sval("p0pe"))
            for ac in range(DCH):
                sinc(sync.dma_start(out=att_sb[:, ac * ROWS:(ac + 1) * ROWS],
                                    in_=o_my[ac * 128:(ac + 1) * 128, :]), "attl", 16)

        with contextlib.ExitStack() as stO_ps:
            psOt_l = [stO_ps.enter_context(nc.psum_tensor(f"psOt{i}", [128, 512], F32))
                      for i in range(TT)]
            @block.scalar
            def _(sc):
                swait(sc, "cc", 1)
                swait(sc, "p0row", sval("p0row"))
                swait(sc, "p0pe", sval("p0pe"))
                for k in range(DS * DCH):
                    sl = k % 2
                    swait(sc, "opk", max(0, k - 1))
                    ds, ac = divmod(k, DCH)
                    sinc(sc.dma_start(out=wo_t[sl][:, :],
                                      in_=woT_all[ac * 128:(ac + 1) * 128,
                                                  ds * 512:(ds + 1) * 512]),
                         f"wox{sl}", 16)

            for ds in range(DS):
                @block.tensor
                def _(t):
                    for ac in range(DCH):
                        k = ds * DCH + ac
                        sl = k % 2
                        swait(t, f"wox{sl}", 16 * (k // 2 + 1))
                        if ac == 0:
                            swait(t, "attl", 16 * DCH)
                            if ds == 0:
                                swait(t, "p0ev", bo_ev_done)
                            else:
                                swait(t, "oev", 2 * TT * ds)
                        for tt in range(TT):
                            mm = t.matmul(psOt_l[tt][:, :],
                                          att_sb[:, ac * ROWS + tt * 128:
                                                 ac * ROWS + (tt + 1) * 128],
                                          wo_t[sl][:, :],
                                          start=(ac == 0), stop=(ac == DCH - 1))
                        sinc(mm, "opk")

                for tt in range(TT):
                    k2 = ds * TT + tt
                    sl = k2 % 2

                    @block.sync
                    def _(sync):
                        swait(sync, "oev", max(0, 2 * (k2 - 1)))
                        sinc(sync.dma_start(out=xsl[sl][:, :],
                                            in_=x_my[tt * 128:(tt + 1) * 128,
                                                     ds * 512:(ds + 1) * 512]),
                             f"oxl{sl}", 16)

                    @block.vector
                    def _(v):
                        swait(v, "opk", DCH * (ds + 1))
                        swait(v, "oev", max(0, 2 * (k2 - 1)))
                        sinc(v.tensor_add(tmp_o[sl][:, :],
                                          psOt_l[tt][:, :],
                                          bo_bc[:, ds * 512:(ds + 1) * 512]), "oev")
                        swait(v, "oev", sval("oev"))
                        swait(v, f"oxl{sl}", 16 * (k2 // 2 + 1))
                        swait(v, f"x1w{sl}", 16 * (k2 // 2))
                        sinc(v.tensor_add(x1sl[sl][:, :], tmp_o[sl][:, :],
                                          xsl[sl][:, :]), "oev")

                    @block.sync
                    def _(sync):
                        swait(sync, "oev", 2 * (k2 + 1))
                        sinc(sync.dma_start(out=x1_d[tt * 128:(tt + 1) * 128,
                                                     ds * 512:(ds + 1) * 512],
                                            in_=x1sl[sl][:, :]), f"x1w{sl}", 16)

        NT = c["NT"]
        x1_gate = [("x1w0", sval("x1w0")), ("x1w1", sval("x1w1"))]
        for t in range(TT):
            emit_ln_tile(t, NT + t, x1_d, xn1_tm,
                         gate_marks=x1_gate if t == 0 else ())
        ln2_done = (sval("Lxnw0"), sval("Lxnw1"))

        st_O.close()

        if upto == "O":
            @block.sync
            def _(sync):
                swait(sync, "Lxnw0", ln2_done[0])
                swait(sync, "Lxnw1", ln2_done[1])
                for t in range(TT):
                    sinc(sync.dma_start(out=out[t * 128:(t + 1) * 128, :],
                                        in_=x1_d[t * 128:(t + 1) * 128, :]),
                         "outw0", 16)
                swait(sync, "outw0", sval("outw0"))
            return nc, c

        oev_total = sval("oev")
        opk_total = sval("opk")
        st_LN.close()
        st_F = E(contextlib.ExitStack())
        EF = st_F.enter_context
        sd_bc = EF(nc.sbuf_tensor("sd_bc", [128, D], F32))
        sd_base = {"row": sval("p0row"), "pe": sval("p0pe"), "ev": sval("p0ev")}
        with nc.sbuf_tensor("rowb3", [1, D], F32) as rowb3, \
             nc.psum_tensor("psSd", [128, 512], F32) as psSd:
            @block.scalar
            def _(sc):
                sinc(sc.dma_start(out=rowb3[0:1, :], in_=sd_in[:, :]), "p0row", 16)

            @block.tensor
            def _(t):
                swait(t, "p0row", sd_base["row"] + 16)
                swait(t, "oev", oev_total)
                for k in range(DS):
                    swait(t, "p0ev", sd_base["ev"] + k)
                    sinc(t.matmul(psSd[:, :], ones_r[:],
                                  rowb3[0:1, k * 512:(k + 1) * 512],
                                  start=True, stop=True), "p0pe")

            @block.vector
            def _(v):
                for k in range(DS):
                    swait(v, "p0pe", sd_base["pe"] + k + 1)
                    sinc(v.tensor_copy(out=sd_bc[:, k * 512:(k + 1) * 512],
                                       in_=psSd[:, :]), "p0ev")
        sd_ev_done = sval("p0ev")
        xn1_sb = EF(nc.sbuf_tensor("xn1_sb", [128, DCH * TG], BF16,
                                   is_transpose_dest=True))
        hh_sb = EF(nc.sbuf_tensor("hh_sb", [128, ICH * TG], BF16))
        wgs = [EF(nc.sbuf_tensor(f"wgs{i}", [128, DCH * 128], BF16)) for i in range(2)]
        wus = [EF(nc.sbuf_tensor(f"wus{i}", [128, DCH * 128], BF16)) for i in range(2)]
        wds = [EF(nc.sbuf_tensor(f"wds{i}", [128, 512], BF16)) for i in range(2)]
        sgt = [EF(nc.sbuf_tensor(f"sgt{i}", [128, TG], BF16)) for i in range(2)]
        tmp2 = [EF(nc.sbuf_tensor(f"tmp2{i}", [128, 512], F32)) for i in range(2)]
        x1s = [EF(nc.sbuf_tensor(f"x1s{i}", [128, 512], F32)) for i in range(2)]
        x2s = [EF(nc.sbuf_tensor(f"x2s{i}", [128, 512], F32)) for i in range(2)]

        fst = {"ic": 0, "wd": 0, "ev": 0}
        with nc.psum_tensor("psG0", [128, TG], F32) as psG0, \
             nc.psum_tensor("psG1", [128, TG], F32) as psG1, \
             nc.psum_tensor("psU0", [128, TG], F32) as psU0, \
             nc.psum_tensor("psU1", [128, TG], F32) as psU1, \
             nc.psum_tensor("psD", [128, TGT * 512], F32) as psD:
            psG = [psG0, psG1]
            psU = [psU0, psU1]

            for tg in range(NTG):
                @block.scalar
                def _(sc):
                    swait(sc, "Lxnw0", ln2_done[0])
                    swait(sc, "Lxnw1", ln2_done[1])
                    if tg > 0:
                        swait(sc, "pU", sval("pU"))
                    else:
                        swait(sc, "opk", opk_total)
                        swait(sc, "p0row", sval("p0row"))
                        swait(sc, "p0pe", sval("p0pe"))
                    for hc in range(DCH):
                        sinc(sc.dma_start(out=xn1_sb[:, hc * TG:(hc + 1) * TG],
                                          in_=xn1_tm[tg * TG:(tg + 1) * TG,
                                                     hc * 128:(hc + 1) * 128],
                                          transpose=True), "xg1", 16)

                for ic in range(ICH):
                    kk = fst["ic"]
                    fst["ic"] += 1
                    sl = kk % 2

                    @block.sync
                    def _(sync):
                        swait(sync, "pG", max(0, sval("pG") - 1))
                        sinc(sync.dma_start(
                            out=wgs[sl].ap().rearrange("p (c i) -> p c i", c=DCH),
                            in_=wgT_all.ap().rearrange("(c q p) i -> p c q i",
                                                       q=ICHP, p=128)[
                                :, :, ic, :]), f"wgl{sl}", 16)
                        swait(sync, "pU", max(0, sval("pU") - 1))
                        sinc(sync.dma_start(
                            out=wus[sl].ap().rearrange("p (c i) -> p c i", c=DCH),
                            in_=wuT_all.ap().rearrange("(c q p) i -> p c q i",
                                                       q=ICHP, p=128)[
                                :, :, ic, :]), f"wul{sl}", 16)

                    @block.tensor
                    def _(t):
                        swait(t, f"wgl{sl}", 16 * (kk // 2 + 1))
                        swait(t, "xg1", 16 * DCH * (tg + 1))
                        swait(t, "sil", max(0, sval("sil") - 1))
                        if tg == 0 and ic == 0:
                            swait(t, "cc", 4)
                            swait(t, "p0ev", sd_ev_done)
                        for hc in range(DCH):
                            mg = t.matmul(psG[sl][:, :],
                                          wgs[sl][:, hc * 128:(hc + 1) * 128],
                                          xn1_sb[:, hc * TG:(hc + 1) * TG],
                                          start=(hc == 0), stop=(hc == DCH - 1))
                        sinc(mg, "pG")
                        swait(t, f"wul{sl}", 16 * (kk // 2 + 1))
                        swait(t, "hhe", max(0, sval("hhe") - 1))
                        for hc in range(DCH):
                            mu = t.matmul(psU[sl][:, :],
                                          wus[sl][:, hc * 128:(hc + 1) * 128],
                                          xn1_sb[:, hc * TG:(hc + 1) * TG],
                                          start=(hc == 0), stop=(hc == DCH - 1))
                        sinc(mu, "pU")

                    @block.scalar
                    def _(sc):
                        swait(sc, "pG", sval("pG"))
                        swait(sc, "hhe", max(0, sval("hhe") - 1))
                        sinc(sc.activation(out=sgt[sl][:, :], in_=psG[sl][:, :],
                                           func=AF.Silu,
                                           scale=sgc[:, ic:ic + 1]), "sil")

                    @block.vector
                    def _(v):
                        swait(v, "pU", sval("pU"))
                        swait(v, "sil", sval("sil"))
                        if tg > 0 and ic == 0:
                            swait(v, "pD", sval("pD"))
                        sinc(v.scalar_tensor_tensor(
                            out=hh_sb[:, ic * TG:(ic + 1) * TG],
                            in0=psU[sl][:, :], scalar=suc[:, ic:ic + 1],
                            in1=sgt[sl][:, :], op0=ALU.mult, op1=ALU.mult), "hhe")

                for hs in range(DS):
                    for ic in range(ICH):
                        kd = fst["wd"]
                        fst["wd"] += 1
                        sl = kd % 2

                        @block.sync
                        def _(sync):
                            swait(sync, "pD", max(0, sval("pD") - 1))
                            sinc(sync.dma_start(out=wds[sl][:, :],
                                                in_=wdT_all[ic * 128:(ic + 1) * 128,
                                                            hs * 512:(hs + 1) * 512]),
                                 f"wdl{sl}", 16)

                        @block.tensor
                        def _(t):
                            swait(t, f"wdl{sl}", 16 * (kd // 2 + 1))
                            if ic == 0:
                                swait(t, "hhe", ICH * (tg + 1))
                                swait(t, "x2e", sval("x2e"))
                            for tt in range(TGT):
                                md = t.matmul(psD[:, tt * 512:tt * 512 + 512],
                                              hh_sb[:, ic * TG + tt * 128:
                                                    ic * TG + (tt + 1) * 128],
                                              wds[sl][:, :],
                                              start=(ic == 0), stop=(ic == ICH - 1))
                            sinc(md, "pD")

                    for tt in range(TGT):
                        ke = fst["ev"]
                        fst["ev"] += 1
                        sl = ke % 2
                        row = tg * TG + tt * 128

                        @block.sync
                        def _(sync):
                            swait(sync, "x2e", max(0, 2 * (ke - 1)))
                            swait(sync, "x1w0", sval("x1w0"))
                            swait(sync, "x1w1", sval("x1w1"))
                            sinc(sync.dma_start(out=x1s[sl][:, :],
                                                in_=x1_d[row:row + 128,
                                                         hs * 512:(hs + 1) * 512]),
                                 f"fx1{sl}", 16)

                        @block.vector
                        def _(v):
                            swait(v, "pD", sval("pD"))
                            swait(v, "x2e", max(0, 2 * (ke - 1)))
                            sinc(v.tensor_mul(tmp2[sl][:, :],
                                              psD[:, tt * 512:tt * 512 + 512],
                                              sd_bc[:, hs * 512:(hs + 1) * 512]),
                                 "x2e")
                            swait(v, "x2e", sval("x2e"))
                            swait(v, f"fx1{sl}", 16 * (ke // 2 + 1))
                            swait(v, f"outw{sl}", 16 * (ke // 2))
                            sinc(v.tensor_add(x2s[sl][:, :], tmp2[sl][:, :],
                                              x1s[sl][:, :]), "x2e")

                        @block.sync
                        def _(sync):
                            swait(sync, "x2e", 2 * (ke + 1))
                            sinc(sync.dma_start(out=out[row:row + 128,
                                                        hs * 512:(hs + 1) * 512],
                                                in_=x2s[sl][:, :]), f"outw{sl}", 16)

        @block.sync
        def _(sync):
            swait(sync, "outw0", sval("outw0"))
            swait(sync, "outw1", sval("outw1"))

        st_F.close()

    return nc, c


def _shard_inputs(c, inputs):
    T, D, I = c["T"], c["D"], c["I"]
    ROWS, OSL, DPC = c["ROWS"], c["OSL"], c["DPC"]
    IPC, IPAD, HCC, ICHP = c["IPC"], c["IPAD"], c["HCC"], c["ICHP"]
    x = np.ascontiguousarray(np.asarray(inputs["L0_attn_x"], np.float32).reshape(T, D))
    wq = np.asarray(inputs["L0_attn_Wq"], np.float32)
    wk = np.asarray(inputs["L0_attn_Wk"], np.float32)
    wv = np.asarray(inputs["L0_attn_Wv"], np.float32)
    bq = np.asarray(inputs["L0_attn_bq"], np.float32).reshape(1, D)
    bk = np.asarray(inputs["L0_attn_bk"], np.float32).reshape(1, D)
    bv = np.asarray(inputs["L0_attn_bv"], np.float32).reshape(1, D)
    wo = np.asarray(inputs["L0_attn_Wo"], np.float32)
    bo = np.asarray(inputs["L0_attn_bo"], np.float32).reshape(1, D)
    wg = np.asarray(inputs["L0_ffn_w_gate"], np.int32)
    wu = np.asarray(inputs["L0_ffn_w_up"], np.int32)
    wd = np.asarray(inputs["L0_ffn_w_down"], np.int32)
    sg = np.zeros((1, IPAD), np.float32)
    sg[0, :I] = np.asarray(inputs["L0_ffn_s_gate"], np.float32)
    su = np.zeros((1, IPAD), np.float32)
    su[0, :I] = np.asarray(inputs["L0_ffn_s_up"], np.float32)
    sd = np.asarray(inputs["L0_ffn_s_down"], np.float32).reshape(1, D)

    def blk_gate(w, q):
        a = np.zeros((IPAD, DPC), np.int32)
        a[:I, :] = w[:, q * DPC:(q + 1) * DPC]
        a = a.reshape(ICHP, 128, HCC, 128)
        a = a.transpose(2, 0, 3, 1)
        return np.ascontiguousarray(a.reshape(HCC * ICHP * 128, 128))

    def wdT_slice(q):
        a = np.zeros((IPC * 128, D), np.int32)
        lo = q * IPC * 128
        hi = min(I, (q + 1) * IPC * 128)
        if hi > lo:
            a[0:hi - lo, :] = wd[:, lo:hi].T
        return np.ascontiguousarray(a)

    import ml_dtypes
    p_ = np.arange(128)[:, None]
    f_ = np.arange(512)[None, :]
    mask = np.concatenate(
        [(f_ - p_ - 128 * j >= 0) for j in range(4)],
        axis=1).astype(ml_dtypes.bfloat16)
    in_maps = []
    for q in range(NC):
        in_maps.append({
            "x_full": x,
            "x_my": np.ascontiguousarray(x[q * ROWS:(q + 1) * ROWS, :]),
            "wqT_sl": np.ascontiguousarray(wq[q * OSL:(q + 1) * OSL, :].T),
            "wkT_sl": np.ascontiguousarray(wk[q * OSL:(q + 1) * OSL, :].T),
            "wvT_sl": np.ascontiguousarray(wv[q * OSL:(q + 1) * OSL, :].T),
            "bq_sl": np.ascontiguousarray(bq[:, q * OSL:(q + 1) * OSL]),
            "bk_sl": np.ascontiguousarray(bk[:, q * OSL:(q + 1) * OSL]),
            "bv_sl": np.ascontiguousarray(bv[:, q * OSL:(q + 1) * OSL]),
            "woT_sl": np.ascontiguousarray(wo[:, q * DPC:(q + 1) * DPC].T),
            "bo_in": bo,
            "wgT_blk": blk_gate(wg, q),
            "wuT_blk": blk_gate(wu, q),
            "wdT_sl": wdT_slice(q),
            "sg_in": sg,
            "su_in": su,
            "sd_in": sd,
            "cid_in": np.array([[q]], np.int32),
            "mask_in": mask,
        })
    return in_maps


def kernel(**inputs):
    T, D, HPC, I = 8192, 4096, 4, 11008
    nc, c = build(T, D, HPC, I)
    in_maps = _shard_inputs(c, inputs)
    res = run_bass_kernel_spmd(nc, in_maps, core_ids=list(range(NC)))
    shards = [np.asarray(res.results[q]["out"]) for q in range(NC)]
    return np.concatenate(shards, axis=0).reshape(1, T, D).astype(np.float32)


# version 05
# speedup vs baseline: 1.0513x; 1.0513x over previous
import numpy as np
import contextlib
import concourse.bass as bass
import concourse.mybir as mybir
from concourse.bass_utils import run_bass_kernel_spmd

F32 = mybir.dt.float32
FP8 = mybir.dt.float8e4
BF16 = mybir.dt.bfloat16
I32 = mybir.dt.int32
AF = mybir.ActivationFunctionType
ALU = mybir.AluOpType

NC = 8


def derived(T, D, HPC, I):
    c = dict(T=T, D=D, HPC=HPC, I=I)
    c["ROWS"] = T // NC
    c["NT"] = T // 128
    c["DCH"] = D // 128
    c["DS"] = D // 512
    c["DPC"] = D // NC
    c["HCC"] = c["DPC"] // 128
    c["ICH"] = I // 128
    c["IPC"] = -(-c["ICH"] // NC)
    c["IPAD"] = c["IPC"] * 128 * NC
    c["NIB"] = -(-I // 512)
    c["ICHP"] = c["IPC"] * NC
    c["QS"] = 512
    c["NQS"] = T // 512
    c["QSC"] = 4
    c["AG1"] = 256
    c["NAG1"] = T // 256
    c["AGD"] = 2
    c["TT"] = c["ROWS"] // 128
    c["TG"] = min(512, c["ROWS"])
    c["NTG"] = c["ROWS"] // c["TG"]
    c["TGT"] = c["TG"] // 128
    c["OSL"] = HPC * 128
    assert T % 512 == 0 and T % (NC * 128) == 0 and D % 512 == 0 and I % 128 == 0
    assert c["TT"] <= 8
    return c


def build(T, D, HPC, I, dbg=False, upto=None):
    c = derived(T, D, HPC, I)
    ROWS, DCH, DS, DPC, HCC = c["ROWS"], c["DCH"], c["DS"], c["DPC"], c["HCC"]
    ICH, IPC, IPAD, NIB = c["ICH"], c["IPC"], c["IPAD"], c["NIB"]
    ICHP = c["ICHP"]
    NQS, QSC, AG1, NAG1, AGD = c["NQS"], c["QSC"], c["AG1"], c["NAG1"], c["AGD"]
    TT, TG, NTG, TGT, OSL = c["TT"], c["TG"], c["NTG"], c["TGT"], c["OSL"]
    INV_D = 1.0 / D
    QSCALE = float(1.0 / np.sqrt(128.0))
    EPS = 1e-5

    nc = bass.Bass("TRN2", target_bir_lowering=False, debug=False)

    x_full = nc.declare_dram_parameter("x_full", [T, D], F32, isOutput=False)
    x_my = nc.declare_dram_parameter("x_my", [ROWS, D], F32, isOutput=False)
    wqT_sl = nc.declare_dram_parameter("wqT_sl", [D, OSL], F32, isOutput=False)
    wkT_sl = nc.declare_dram_parameter("wkT_sl", [D, OSL], F32, isOutput=False)
    wvT_sl = nc.declare_dram_parameter("wvT_sl", [D, OSL], F32, isOutput=False)
    bq_sl = nc.declare_dram_parameter("bq_sl", [1, OSL], F32, isOutput=False)
    bk_sl = nc.declare_dram_parameter("bk_sl", [1, OSL], F32, isOutput=False)
    bv_sl = nc.declare_dram_parameter("bv_sl", [1, OSL], F32, isOutput=False)
    woT_sl = nc.declare_dram_parameter("woT_sl", [DPC, D], F32, isOutput=False)
    bo_in = nc.declare_dram_parameter("bo_in", [1, D], F32, isOutput=False)
    wgT_blk = nc.declare_dram_parameter("wgT_blk", [HCC * ICHP * 128, 128], I32,
                                        isOutput=False)
    wuT_blk = nc.declare_dram_parameter("wuT_blk", [HCC * ICHP * 128, 128], I32,
                                        isOutput=False)
    wdT_sl = nc.declare_dram_parameter("wdT_sl", [IPC * 128, D], I32, isOutput=False)
    sg_in = nc.declare_dram_parameter("sg_in", [1, IPAD], F32, isOutput=False)
    su_in = nc.declare_dram_parameter("su_in", [1, IPAD], F32, isOutput=False)
    sd_in = nc.declare_dram_parameter("sd_in", [1, D], F32, isOutput=False)
    cid_in = nc.declare_dram_parameter("cid_in", [1, 1], I32, isOutput=False)
    mask_in = nc.declare_dram_parameter("mask_in", [128, 4 * 512], BF16, isOutput=False)
    out = nc.declare_dram_parameter("out", [ROWS, D], F32, isOutput=True)

    def idram(name, shape, dt_):
        if dbg:
            return nc.declare_dram_parameter(name, shape, dt_, isOutput=True)
        return nc.dram_tensor(name, shape, dt_)

    xn_tm = idram("xn_tm", [T, D], BF16)
    xn_fm = idram("xn_fm", [D, T], BF16)
    wqT_bnc = nc.dram_tensor("wqT_bnc", [D, OSL], BF16)
    wkT_bnc = nc.dram_tensor("wkT_bnc", [D, OSL], BF16)
    wvT_bnc = nc.dram_tensor("wvT_bnc", [D, OSL], BF16)
    woT_loc = nc.dram_tensor("woT_loc", [DPC, D], BF16)
    woT_all = nc.dram_tensor("woT_all", [D, D], BF16, addr_space="Shared")
    wgT_loc = nc.dram_tensor("wgT_loc", [HCC * ICHP * 128, 128], BF16)
    wuT_loc = nc.dram_tensor("wuT_loc", [HCC * ICHP * 128, 128], BF16)
    wdT_loc = nc.dram_tensor("wdT_loc", [IPC * 128, D], BF16)
    wgT_all = nc.dram_tensor("wgT_all", [DCH * ICHP * 128, 128], BF16, addr_space="Shared")
    wuT_all = nc.dram_tensor("wuT_all", [DCH * ICHP * 128, 128], BF16, addr_space="Shared")
    wdT_all = nc.dram_tensor("wdT_all", [IPAD, D], BF16, addr_space="Shared")
    v_dram = idram("v_dram", [HPC * T, 128], BF16)
    o_loc = nc.dram_tensor("o_loc", [OSL, T], BF16)
    o_all = nc.dram_tensor("o_all", [D, T], BF16, addr_space="Shared")
    o_my = idram("o_my", [D, ROWS], BF16)
    x1_d = idram("x1_d", [ROWS, D], F32)
    xn1_tm = idram("xn1_tm", [ROWS, D], BF16)

    sem_n = {}
    sem_h = {}

    with contextlib.ExitStack() as st0:
        E = st0.enter_context
        block = E(nc.Block())

        def sget(name):
            if name not in sem_h:
                sem_h[name] = E(nc.semaphore(name))
                sem_n[name] = 0
            return sem_h[name]

        def sinc(instr, name, k=1):
            instr.then_inc(sget(name), k)
            sem_n[name] += k
            return sem_n[name]

        def sval(name):
            sget(name)
            return sem_n[name]

        def swait(eng, name, v):
            if v > 0:
                eng.wait_ge(sget(name), v)

        def swaitm(eng, mark):
            if mark is not None:
                swait(eng, mark[0], mark[1])

        ones_r = E(nc.sbuf_tensor("ones_r", [1, 128], F32))
        ones_c = E(nc.sbuf_tensor("ones_c", [128, 1], F32))
        ones_s = E(nc.sbuf_tensor("ones_s", [1, 1], F32))
        bqc = E(nc.sbuf_tensor("bqc", [128, HPC], F32))
        bkc = E(nc.sbuf_tensor("bkc", [128, HPC], F32))
        bv_bc = E(nc.sbuf_tensor("bv_bc", [128, OSL], F32))
        sgc = E(nc.sbuf_tensor("sgc", [128, ICH], F32))
        suc = E(nc.sbuf_tensor("suc", [128, ICH], F32))
        masks = E(nc.sbuf_tensor("masks", [128, QSC * 512], BF16))
        cid_sb = E(nc.sbuf_tensor("cid_sb", [1, 1], I32))

        @block.sync
        def _(sync):
            sinc(sync.dma_start(out=cid_sb[:], in_=cid_in[:, :]), "cid", 16)

        @block.sync
        def _(sync):
            sinc(sync.dma_start(out=masks[:], in_=mask_in[:, :]), "mskl", 16)

        @block.gpsimd
        def _(g):
            sinc(g.memset(ones_r[:], 1.0), "p0g")
            sinc(g.memset(ones_c[:], 1.0), "p0g")
            sinc(g.memset(ones_s[:], 1.0), "p0g")
        p0g_all = sval("p0g")

        jobs = [
            (bq_sl[:, :], OSL, "col", bqc),
            (bk_sl[:, :], OSL, "col", bkc),
            (bv_sl[:, :], OSL, "bc", bv_bc),
            (sg_in[0:1, 0:I], I, "col", sgc),
            (su_in[0:1, 0:I], I, "col", suc),
        ]
        maxw = max(j[1] for j in jobs)
        chunks_before = [0]
        for (_src, w, kind, _d) in jobs:
            chunks_before.append(chunks_before[-1] + (w // 128 if kind == "col" else -(-w // 512)))

        with nc.sbuf_tensor("rowbuf", [1, maxw], F32) as rowbuf, \
             nc.psum_tensor("pbc", [128, 512], F32) as pbc, \
             nc.psum_tensor("pbc2", [128, 512], F32) as pbc2:
            pb = [pbc, pbc2]

            @block.scalar
            def _(sc):
                for ji, (src, w, kind, dest) in enumerate(jobs):
                    swait(sc, "p0pe", chunks_before[ji])
                    sinc(sc.dma_start(out=rowbuf[0:1, 0:w], in_=src), "p0row", 16)

            @block.tensor
            def _(t):
                kg = 0
                swait(t, "p0g", 3)
                for ji, (src, w, kind, dest) in enumerate(jobs):
                    nch = w // 128 if kind == "col" else -(-w // 512)
                    swait(t, "p0row", 16 * (ji + 1))
                    for k in range(nch):
                        swait(t, "p0ev", max(0, kg - 1))
                        if kind == "col":
                            mm = t.matmul(pb[kg % 2][:, 0:1],
                                          rowbuf[0:1, k * 128:(k + 1) * 128],
                                          ones_s[:], start=True, stop=True)
                        else:
                            lo, hi = k * 512, min(w, k * 512 + 512)
                            mm = t.matmul(pb[kg % 2][:, 0:hi - lo], ones_r[:],
                                          rowbuf[0:1, lo:hi], start=True, stop=True)
                        sinc(mm, "p0pe")
                        kg += 1

            @block.vector
            def _(v):
                kg = 0
                for ji, (src, w, kind, dest) in enumerate(jobs):
                    nch = w // 128 if kind == "col" else -(-w // 512)
                    for k in range(nch):
                        swait(v, "p0pe", kg + 1)
                        if kind == "col":
                            cp = v.tensor_copy(out=dest[:, k:k + 1],
                                               in_=pb[kg % 2][:, 0:1])
                        else:
                            lo, hi = k * 512, min(w, k * 512 + 512)
                            cp = v.tensor_copy(out=dest[:, lo:hi],
                                               in_=pb[kg % 2][:, 0:hi - lo])
                        sinc(cp, "p0ev")
                        kg += 1

        def flat_aps(t, total_elems, nchunks):
            per = total_elems // 128 // nchunks
            return [t.ap().rearrange("a i -> (a i)").rearrange("(p x) -> p x", p=128)[
                :, k * per:(k + 1) * per] for k in range(nchunks)]

        cast_jobs = []
        for t_in, t_out_, nch in ((woT_sl, woT_loc, 8),
                                  (wqT_sl, wqT_bnc, 8), (wkT_sl, wkT_bnc, 8),
                                  (wvT_sl, wvT_bnc, 8)):
            n = 1
            for d_ in t_in.shape:
                n *= d_
            ins_l = flat_aps(t_in, n, nch)
            outs_l = flat_aps(t_out_, n, nch)
            cast_jobs += list(zip(ins_l, outs_l, [n // 128 // nch] * nch))
        cw_qkv_jobs = len(cast_jobs)
        for t_in, t_out_, nch in ((wgT_blk, wgT_loc, 16), (wuT_blk, wuT_loc, 16),
                                  (wdT_sl, wdT_loc, 16)):
            n = 1
            for d_ in t_in.shape:
                n *= d_
            ins_l = flat_aps(t_in, n, nch)
            outs_l = flat_aps(t_out_, n, nch)
            cast_jobs += list(zip(ins_l, outs_l, [n // 128 // nch] * nch))
        cw_wg_jobs = cw_qkv_jobs + 16
        cw_wu_jobs = cw_qkv_jobs + 32
        maxper = max(w for (_i, _o, w) in cast_jobs)

        cnvb = [E(nc.sbuf_tensor(f"cnv{i}", [128, maxper], BF16, side="right"))
                for i in range(2)]

        def cnt_ab(k):
            return (k + 1) // 2, k // 2

        ag_at = {cw_qkv_jobs: ("woT", None),
                 cw_wg_jobs: ("wg", None), cw_wu_jobs: ("wu", None),
                 len(cast_jobs): ("wd", None)}
        ag_tensors = {"woT": (woT_loc, woT_all), "wg": (wgT_loc, wgT_all),
                      "wu": (wuT_loc, wuT_all), "wd": (wdT_loc, wdT_all)}

        @block.gpsimd
        def _(g):
            for kk, (iap, oap, w) in enumerate(cast_jobs):
                sA = "cwA" if kk % 2 == 0 else "cwB"
                sL = "clA" if kk % 2 == 0 else "clB"
                nprev = (kk - 2) // 2 + 1 if kk >= 2 else 0
                swait(g, sA, 16 * nprev)
                sinc(g.dma_start(out=cnvb[kk % 2][:, 0:w], in_=iap), sL, 16)
                swait(g, sL, sval(sL))
                sinc(g.dma_start(out=oap, in_=cnvb[kk % 2][:, 0:w]), sA, 16)
                if kk + 1 in ag_at:
                    a_n, b_n = cnt_ab(kk + 1)
                    swait(g, "cwA", 16 * a_n)
                    swait(g, "cwB", 16 * b_n)
                    loc_t, all_t = ag_tensors[ag_at[kk + 1][0]]
                    sinc(g.collective_compute(
                        "AllGather", ALU.bypass,
                        replica_groups=[list(range(NC))],
                        ins=[loc_t.ap().opt()], outs=[all_t.ap().opt()]), "cc")

        cw_qkv = cnt_ab(cw_qkv_jobs)

        def swait_cw(eng, njobs):
            a_n, b_n = cnt_ab(njobs)
            swait(eng, "cwA", 16 * a_n)
            swait(eng, "cwB", 16 * b_n)

        st_LN = E(contextlib.ExitStack())
        EL = st_LN.enter_context
        lb = {
            "xb": [EL(nc.sbuf_tensor(f"xb{i}", [128, D], F32, side="right")) for i in range(2)],
            "junk_d": [EL(nc.sbuf_tensor(f"junkd{i}", [128, D], FP8, side="right")) for i in range(2)],
            "junk_a": [EL(nc.sbuf_tensor(f"junka{i}", [128, D], FP8, side="right")) for i in range(2)],
            "sum": [EL(nc.sbuf_tensor(f"lsum{i}", [128, 1], F32, side="right")) for i in range(2)],
            "sq": [EL(nc.sbuf_tensor(f"lsq{i}", [128, 1], F32, side="right")) for i in range(2)],
            "m2": [EL(nc.sbuf_tensor(f"lm2{i}", [128, 1], F32, side="right")) for i in range(2)],
            "var": [EL(nc.sbuf_tensor(f"lvar{i}", [128, 1], F32, side="right")) for i in range(2)],
            "lnv": [EL(nc.sbuf_tensor(f"llnv{i}", [128, 1], F32, side="right")) for i in range(2)],
            "var2": [EL(nc.sbuf_tensor(f"lvar2{i}", [128, 1], F32, side="right")) for i in range(2)],
            "rstd": [EL(nc.sbuf_tensor(f"lrstd{i}", [128, 1], F32, side="right")) for i in range(2)],
            "negm": [EL(nc.sbuf_tensor(f"lnegm{i}", [128, 1], F32, side="right")) for i in range(2)],
            "xnb": [EL(nc.sbuf_tensor(f"lxnb{i}", [128, D], BF16, side="right")) for i in range(2)],
        }
        xfg = [EL(nc.sbuf_tensor(f"xfg{i}", [128, 4 * 512], BF16,
                                 is_transpose_dest=True, side="right"))
               for i in range(2)]
        xfw8 = [xfg[i // 4][:, (i % 4) * 512:(i % 4 + 1) * 512] for i in range(8)]

        def emit_ln_tile(trow, t, src, dst, gate_marks=()):
            pfx = "L"
            q = t % 2

            @block.sync
            def _(sync):
                swait(sync, pfx + "an", t - 1)
                for m in gate_marks:
                    swaitm(sync, m)
                sinc(sync.dma_start(out=lb["xb"][q][:],
                                    in_=src[trow * 128:(trow + 1) * 128, :]),
                     pfx + f"lx{q}", 16)

            @block.vector
            def _(v):
                swait(v, pfx + f"lx{q}", 16 * (t // 2 + 1))
                swait(v, pfx + "d1", t - 1)
                sinc(v.tensor_scalar(out=lb["junk_d"][q][:], in0=lb["xb"][q][:],
                                     scalar1=1.0, scalar2=0.0, op0=ALU.mult,
                                     op1=ALU.add,
                                     accum_out=lb["sum"][q][:]), pfx + "d1")

            @block.scalar
            def _(sc):
                swait(sc, pfx + f"lx{q}", 16 * (t // 2 + 1))
                swait(sc, pfx + "a2", t - 1)
                sinc(sc.activation(out=lb["junk_a"][q][:], in_=lb["xb"][q][:],
                                   func=AF.Square, accum_out=lb["sq"][q][:]),
                     pfx + "a2")

            @block.vector
            def _(v):
                swait(v, pfx + "d1", t + 1)
                swait(v, pfx + "dv1", t - 1)
                sinc(v.tensor_mul(lb["m2"][q][:], lb["sum"][q][:], lb["sum"][q][:]),
                     pfx + "dv1")
                swait(v, pfx + "dv1", t + 1)
                swait(v, pfx + "a2", t + 1)
                swait(v, pfx + "dv2", t - 1)
                sinc(v.scalar_tensor_tensor(out=lb["var"][q][:], in0=lb["m2"][q][:],
                                            scalar=-INV_D, in1=lb["sq"][q][:],
                                            op0=ALU.mult, op1=ALU.add), pfx + "dv2")
                swait(v, pfx + "dv2", t + 1)
                swait(v, pfx + "dv2b", t - 1)
                sinc(v.tensor_scalar(out=lb["var2"][q][:], in0=lb["var"][q][:],
                                     scalar1=INV_D, scalar2=EPS,
                                     op0=ALU.mult, op1=ALU.add), pfx + "dv2b")

            @block.scalar
            def _(sc):
                swait(sc, pfx + "dv2b", t + 1)
                swait(sc, pfx + "aln", t - 1)
                sinc(sc.activation(out=lb["lnv"][q][:], in_=lb["var2"][q][:],
                                   func=AF.Ln), pfx + "aln")
                swait(sc, pfx + "aln", t + 1)
                swait(sc, pfx + "aex", t - 1)
                sinc(sc.activation(out=lb["rstd"][q][:], in_=lb["lnv"][q][:],
                                   func=AF.Exp, scale=-0.5), pfx + "aex")

            @block.vector
            def _(v):
                swait(v, pfx + "aex", t + 1)
                swait(v, pfx + "dv3", t - 1)
                sinc(v.tensor_scalar(out=lb["negm"][q][:], in0=lb["sum"][q][:],
                                     scalar1=lb["rstd"][q][:], scalar2=-INV_D,
                                     op0=ALU.mult, op1=ALU.mult), pfx + "dv3")

            @block.scalar
            def _(sc):
                swait(sc, pfx + "dv3", t + 1)
                if t >= 2:
                    swait(sc, pfx + f"xnw{q}", 16 * ((t - 2) // 2 + 1))
                sinc(sc.activation(out=lb["xnb"][q][:], in_=lb["xb"][q][:],
                                   func=AF.Identity, scale=lb["rstd"][q][:],
                                   bias=lb["negm"][q][:]), pfx + "an")

            @block.sync
            def _(sync):
                swait(sync, pfx + "an", t + 1)
                sinc(sync.dma_start(out=dst[trow * 128:(trow + 1) * 128, :],
                                    in_=lb["xnb"][q][:]), pfx + f"xnw{q}", 16)

        xf_last = {0: 0, 1: 0}
        xf_k = [0]

        def emit_xnfm_strip(s):
            @block.sync
            def _(sync):
                swait(sync, "Lxnw0", 16 * (2 * s + 2))
                swait(sync, "Lxnw1", 16 * (2 * s + 2))
                for dcg in range(0, DCH, 4):
                    grp = (xf_k[0] // 4) % 2
                    for j in range(4):
                        sl4 = grp * 4 + j
                        xf_k[0] += 1
                        swait(sync, f"xfw{grp}", xf_last[grp])
                        sinc(sync.dma_start(
                            out=xfw8[sl4][:, :],
                            in_=xn_tm[s * 512:(s + 1) * 512,
                                      (dcg + j) * 128:(dcg + j + 1) * 128],
                            transpose=True), f"xfl{grp}", 16)
                    swait(sync, f"xfl{grp}", sval(f"xfl{grp}"))
                    xf_last[grp] = sinc(sync.dma_start(
                        out=xn_fm[dcg * 128:(dcg + 4) * 128,
                                  s * 512:(s + 1) * 512].rearrange(
                            "(c p) t -> p c t", p=128),
                        in_=xfg[grp].ap().rearrange("p (c t) -> p c t", c=4)),
                        f"xfw{grp}", 16)

        st_A = E(contextlib.ExitStack())
        EA = st_A.enter_context
        stp = [EA(nc.sbuf_tensor(f"stp{i}", [128, DCH * AG1], BF16)) for i in range(2)]
        K_h = EA(nc.sbuf_tensor("K_h", [128, T], BF16))
        Q_h = EA(nc.sbuf_tensor("Q_h", [128, T], BF16))
        wqT_sb = EA(nc.sbuf_tensor("wqT_sb", [128, DCH * 128], BF16,
                                   is_transpose_dest=True))
        wkT_sb = EA(nc.sbuf_tensor("wkT_sb", [128, DCH * 128], BF16,
                                   is_transpose_dest=True))
        v_sb = [EA(nc.sbuf_tensor(f"v_sb{i}", [128, OSL], BF16)) for i in range(2)]

        strip_free = []
        a1v = {"v": 0}
        a2st = {"s": 0, "chunk": 0, "ptmark": [], "lm_mark": [], "on_mark": [],
                "av_mark": [], "lacp_prev": None, "psl_free": []}

        if dbg:
            bnc_dbg = nc.declare_dram_parameter("bnc_dbg", [128, D], BF16,
                                                isOutput=True)

        def emit_wqkT_load(h, prev_head_mark):
            @block.scalar
            def _(sc):
                swait_cw(sc, cw_qkv_jobs)
                if prev_head_mark is not None:
                    swaitm(sc, prev_head_mark)
                for wsrc, dst in ((wqT_bnc, wqT_sb), (wkT_bnc, wkT_sb)):
                    for cg in range(0, DCH, 8):
                        sinc(sc.dma_start(
                            out=dst[:, cg * 128:(cg + 8) * 128].rearrange(
                                "p (c o) -> p c o", c=8),
                            in_=wsrc[cg * 128:(cg + 8) * 128,
                                     h * 128:(h + 1) * 128].rearrange(
                                "(c p) o -> p c o", p=128)), "wqkx", 16)

        a1ev = {"K": 0, "Q": 0, "V": 0}

        def emit_a1_group(h, g, gctr, pKap, pQap, pVap=None, wvT_sb=None,
                          head_first_wait=None):
            sl = gctr % 2

            @block.scalar
            def _(sc):
                s_idx = g // 2
                ngrp = (s_idx + 1) * (DCH // 4)
                swait(sc, "xfw0", 16 * ((ngrp + 1) // 2))
                swait(sc, "xfw1", 16 * (ngrp // 2))
                if gctr >= 2:
                    swaitm(sc, strip_free[gctr - 2])
                for cg in range(0, DCH, 8):
                    sinc(sc.dma_start(
                        out=stp[sl][:, cg * AG1:(cg + 8) * AG1].rearrange(
                            "p (c t) -> p c t", c=8),
                        in_=xn_fm[cg * 128:(cg + 8) * 128, :].rearrange(
                            "(c p) t -> p c t", p=128)[
                            :, :, g * AG1:(g + 1) * AG1]), f"stl{sl}", 16)

            @block.tensor
            def _(t):
                swait(t, f"stl{sl}", 16 * (DCH // 8) * (gctr // 2 + 1))
                if g == 0:
                    swait(t, "wqkx", sval("wqkx"))
                    if head_first_wait is not None:
                        for m in head_first_wait:
                            swaitm(t, m)
                swait(t, "evA1", a1ev["K"])
                for dc in range(DCH):
                    mmk = t.matmul(pKap, wkT_sb[:, dc * 128:(dc + 1) * 128],
                                   stp[sl][:, dc * AG1:(dc + 1) * AG1],
                                   start=(dc == 0), stop=(dc == DCH - 1))
                sinc(mmk, "pA1")

            @block.vector
            def _(v):
                swait(v, "pA1", sval("pA1"))
                a1ev["K"] = sinc(
                    v.tensor_scalar(out=K_h[:, g * AG1:(g + 1) * AG1], in0=pKap,
                                    scalar1=bkc[:, h:h + 1], scalar2=None,
                                    op0=ALU.add), "evA1")

            @block.tensor
            def _(t):
                swait(t, "evA1", a1ev["Q"])
                for dc in range(DCH):
                    mmq = t.matmul(pQap, wqT_sb[:, dc * 128:(dc + 1) * 128],
                                   stp[sl][:, dc * AG1:(dc + 1) * AG1],
                                   start=(dc == 0), stop=(dc == DCH - 1))
                sinc(mmq, "pA1")

            @block.vector
            def _(v):
                swait(v, "pA1", sval("pA1"))
                a1ev["Q"] = sinc(
                    v.tensor_scalar(out=Q_h[:, g * AG1:(g + 1) * AG1], in0=pQap,
                                    scalar1=bqc[:, h:h + 1], scalar2=QSCALE,
                                    op0=ALU.add, op1=ALU.mult), "evA1")

            if h == 0:
                for tt in range(AGD):
                    vctr = a1v["v"]
                    a1v["v"] += 1

                    @block.tensor
                    def _(t):
                        swait(t, "evA1", a1ev["V"])
                        if g == 0 and tt == 0:
                            swait(t, "wqkx", sval("wqkx"))
                        for dc in range(DCH):
                            mmv = t.matmul(pVap,
                                           stp[sl][:, dc * AG1 + tt * 128:
                                                   dc * AG1 + (tt + 1) * 128],
                                           wvT_sb[:, dc * OSL:(dc + 1) * OSL],
                                           start=(dc == 0), stop=(dc == DCH - 1))
                        sinc(mmv, "pA1")

                    @block.vector
                    def _(v):
                        swait(v, "pA1", sval("pA1"))
                        swait(v, f"vw{vctr % 2}", 16 * HPC * (vctr // 2))
                        a1ev["V"] = sinc(
                            v.tensor_add(v_sb[vctr % 2][:, :], pVap, bv_bc[:, :]),
                            "evA1")

                    @block.scalar
                    def _(sc):
                        swait(sc, "evA1", a1ev["V"])
                        row = g * AG1 + tt * 128
                        for hh in range(HPC):
                            sinc(sc.dma_start(
                                out=v_dram[hh * T + row:hh * T + row + 128, :],
                                in_=v_sb[vctr % 2][:, hh * 128:(hh + 1) * 128]),
                                f"vw{vctr % 2}", 16)

            strip_free.append(("pA1", sval("pA1")))

        def emit_vh_load(h):
            @block.scalar
            def _(sc):
                swait(sc, "vw0", 16 * HPC * (T // 256))
                swait(sc, "vw1", 16 * HPC * (T // 256))
                if a2st["av_mark"]:
                    swaitm(sc, a2st["av_mark"][-1])
                KT = T // 128
                for kg in range(0, KT, 8):
                    sinc(sc.dma_start(
                        out=vbuf[:, kg * 128:(kg + 8) * 128].rearrange(
                            "p (k f) -> p k f", k=8),
                        in_=v_dram[h * T + kg * 128:h * T + (kg + 8) * 128, :]
                        .rearrange("(k p) f -> p k f", p=128)), "vhl", 16)

        def emit_a2_strip(h, s, use_pool, psS, psO, psL, psBc):
            kmax = QSC * (s + 1)
            sctr = a2st["s"]
            a2st["s"] += 1

            for kc in range(kmax):
                diag = kc >= QSC * s
                j = kc - QSC * s
                eng_pool = use_pool and (kc % 2 == 1)
                cglob = a2st["chunk"]
                a2st["chunk"] += 1

                @block.tensor
                def _(t):
                    if s == 0 and kc == 0:
                        swait(t, "p0g", p0g_all)
                        swait(t, "mskl", 16)
                        swait(t, "evA1", sval("evA1"))
                    swait(t, "exp", max(0, sval("st") - 1))
                    sinc(t.matmul(psS[kc % 2][:, :], K_h[:, kc * 128:(kc + 1) * 128],
                                  Q_h[:, s * 512:(s + 1) * 512],
                                  start=True, stop=True), "st")

                @block.scalar
                def _(sc):
                    swait(sc, "st", sval("st"))
                    if cglob >= 3:
                        for m in a2st["ptmark"][cglob - 3]:
                            swaitm(sc, m)
                    sinc(sc.activation(out=pt[cglob % 3][:, :],
                                       in_=psS[kc % 2][:, :], func=AF.Exp), "exp")

                if diag:
                    @block.vector
                    def _(v):
                        if sval("msk") == 0:
                            swait(v, "mskl", 16)
                        swait(v, "exp", sval("exp"))
                        swait(v, "msk", max(0, sval("msk") - 1))
                        sinc(v.tensor_mul(pt[cglob % 3][:, :], pt[cglob % 3][:, :],
                                          masks[:, j * 512:(j + 1) * 512]), "msk")
                ready = ("msk", sval("msk")) if diag else ("exp", sval("exp"))

                if eng_pool:
                    @block.gpsimd
                    def _(g):
                        swaitm(g, ready)
                        if kc == 1:
                            swaitm(g, a2st["lacp_prev"])
                            op = g.tensor_copy(out=accP[:, :], in_=pt[cglob % 3][:, :])
                        else:
                            swait(g, "lacP", sval("lacP"))
                            op = g.tensor_add(accP[:, :], accP[:, :],
                                              pt[cglob % 3][:, :])
                        sinc(op, "lacP")
                    lmark = ("lacP", sval("lacP"))
                else:
                    @block.vector
                    def _(v):
                        swaitm(v, ready)
                        if kc == 0:
                            if a2st["lm_mark"]:
                                swaitm(v, a2st["lm_mark"][-1])
                            op = v.tensor_copy(out=accD[:, :], in_=pt[cglob % 3][:, :])
                        else:
                            swait(v, "lacD", sval("lacD"))
                            op = v.tensor_add(accD[:, :], accD[:, :],
                                              pt[cglob % 3][:, :])
                        sinc(op, "lacD")
                    lmark = ("lacD", sval("lacD"))

                @block.tensor
                def _(t):
                    swaitm(t, ready)
                    swait(t, "vhl", 16 * (T // 1024) * (h + 1))
                    if kc == 0 and sctr >= 2:
                        swaitm(t, a2st["on_mark"][sctr - 2])
                    sinc(t.matmul(psO[sctr % 2][:, :],
                                  vbuf[:, kc * 128:(kc + 1) * 128],
                                  pt[cglob % 3][:, :],
                                  start=(kc == 0), stop=(kc == kmax - 1)), "av")
                a2st["ptmark"].append([("av", sval("av")), lmark])

            a2st["av_mark"].append(("av", sval("av")))

            if use_pool:
                @block.vector
                def _(v):
                    swait(v, "lacP", sval("lacP"))
                    swait(v, "lacD", sval("lacD"))
                    sinc(v.tensor_add(accD[:, :], accD[:, :], accP[:, :]), "lacD")
                a2st["lacp_prev"] = ("lacD", sval("lacD"))

            @block.tensor
            def _(t):
                swait(t, "lacD", sval("lacD"))
                if sctr >= 2:
                    swaitm(t, a2st["psl_free"][sctr - 2])
                sinc(t.matmul(psL[sctr % 2][0:1, :], ones_c[:, :], accD[:, :],
                              start=True, stop=True), "lm")
            a2st["lm_mark"].append(("lm", sval("lm")))

            @block.scalar
            def _(sc):
                swait(sc, "lm", sval("lm"))
                swait(sc, "lin", max(0, sval("lin") - 2))
                sinc(sc.activation(out=lnl_r[sctr % 2][:, :],
                                   in_=psL[sctr % 2][0:1, :], func=AF.Ln), "lin")
                a2st["psl_free"].append(("lin", sval("lin")))
                swait(sc, "lin", sval("lin"))
                swait(sc, "bcm", sval("bcm"))
                sinc(sc.activation(out=linv_r[sctr % 2][:, :],
                                   in_=lnl_r[sctr % 2][:, :],
                                   func=AF.Exp, scale=-1.0), "lin")

            @block.tensor
            def _(t):
                swait(t, "lin", sval("lin"))
                swait(t, "bce", max(0, sval("bcm")))
                sinc(t.matmul(psBc[:, :], ones_r[:, :], linv_r[sctr % 2][:, :],
                              start=True, stop=True), "bcm")

            @block.vector
            def _(v):
                swait(v, "bcm", sval("bcm"))
                swait(v, "on", sval("on"))
                sinc(v.tensor_copy(out=linv_bc[:, :], in_=psBc[:, :]), "bce")
                swait(v, "bce", sval("bce"))
                swait(v, "av", sval("av"))
                swait(v, f"ow{sctr % 2}", 16 * (sctr // 2))
                sinc(v.tensor_mul(o_sb[sctr % 2][:, :], psO[sctr % 2][:, :],
                                  linv_bc[:, :]), "on")
            a2st["on_mark"].append(("on", sval("on")))

            @block.scalar
            def _(sc):
                swait(sc, "on", sval("on"))
                sinc(sc.dma_start(out=o_loc[h * 128:(h + 1) * 128,
                                            s * 512:(s + 1) * 512],
                                  in_=o_sb[sctr % 2][:, :]), f"ow{sctr % 2}", 16)

        a1_gctr = 0
        with nc.psum_tensor("psK", [128, AG1], F32) as psK, \
             nc.psum_tensor("psQ", [128, AG1], F32) as psQ, \
             nc.psum_tensor("psV", [128, OSL], F32) as psV, \
             nc.sbuf_tensor("wvT_sb", [128, DCH * OSL], BF16,
                            is_transpose_dest=True) as wvT_sb:

            @block.scalar
            def _(sc):
                swait_cw(sc, cw_qkv_jobs)
                for cg in range(0, DCH, 8):
                    sinc(sc.dma_start(
                        out=wvT_sb[:, cg * OSL:(cg + 8) * OSL].rearrange(
                            "p (c o) -> p c o", c=8),
                        in_=wvT_bnc[cg * 128:(cg + 8) * 128, :].rearrange(
                            "(c p) o -> p c o", p=128)), "wqkx", 16)

            emit_wqkT_load(0, None)
            for s in range(NQS):
                for t4 in range(4):
                    emit_ln_tile(4 * s + t4, 4 * s + t4, x_full, xn_tm)
                emit_xnfm_strip(s)
                for gg in range(2):
                    g = 2 * s + gg
                    emit_a1_group(0, g, a1_gctr, psK[:, :], psQ[:, :],
                                  pVap=psV[:, :], wvT_sb=wvT_sb,
                                  head_first_wait=[("p0ev", sval("p0ev"))] if g == 0 else None)
                    a1_gctr += 1

        vbuf = EA(nc.sbuf_tensor("vbuf", [128, T], BF16))
        pt = [EA(nc.sbuf_tensor(f"pt{i}", [128, 512], BF16)) for i in range(3)]
        accD = EA(nc.sbuf_tensor("accD", [128, 512], F32))
        accP = EA(nc.sbuf_tensor("accP", [128, 512], F32))
        linv_r = [EA(nc.sbuf_tensor(f"linv{i}", [1, 512], F32)) for i in range(2)]
        lnl_r = [EA(nc.sbuf_tensor(f"lnl{i}", [1, 512], F32)) for i in range(2)]
        linv_bc = EA(nc.sbuf_tensor("linv_bc", [128, 512], F32))
        o_sb = [EA(nc.sbuf_tensor(f"o_sb{i}", [128, 512], BF16)) for i in range(2)]

        with nc.psum_tensor("psS0", [128, 512], F32) as psS0, \
             nc.psum_tensor("psS1", [128, 512], F32) as psS1, \
             nc.psum_tensor("psO0", [128, 512], F32) as psO0, \
             nc.psum_tensor("psO1", [128, 512], F32) as psO1, \
             nc.psum_tensor("psL0", [1, 512], F32) as psL0, \
             nc.psum_tensor("psL1", [1, 512], F32) as psL1, \
             nc.psum_tensor("psBc", [128, 512], F32) as psBc:
            psS = [psS0, psS1]
            psO = [psO0, psO1]
            psL = [psL0, psL1]
            emit_vh_load(0)
            for s in range(NQS):
                emit_a2_strip(0, s, False, psS, psO, psL, psBc)

            if dbg:
                kq_dbg = nc.declare_dram_parameter("kq_dbg", [2 * OSL, T], BF16,
                                                   isOutput=True)

            def emit_kq_dump(h):
                if not dbg:
                    return

                @block.scalar
                def _(sc):
                    swait(sc, "evA1", sval("evA1"))
                    sinc(sc.dma_start(out=kq_dbg[h * 128:(h + 1) * 128, :],
                                      in_=K_h[:, :]), "kqd", 16)
                    sinc(sc.dma_start(out=kq_dbg[OSL + h * 128:OSL + (h + 1) * 128, :],
                                      in_=Q_h[:, :]), "kqd", 16)

            emit_kq_dump(0)
            for h in range(1, HPC):
                emit_wqkT_load(h, strip_free[-1])
                for g in range(NAG1):
                    emit_a1_group(h, g, a1_gctr, psS0[:, 0:AG1], psS1[:, 0:AG1],
                                  head_first_wait=[("exp", sval("exp"))] if g == 0 else None)
                    a1_gctr += 1
                emit_kq_dump(h)
                emit_vh_load(h)
                for s in range(NQS):
                    emit_a2_strip(h, s, h >= 2, psS, psO, psL, psBc)

        @block.gpsimd
        def _(g):
            swait(g, "ow0", 16 * ((HPC * NQS + 1) // 2))
            swait(g, "ow1", 16 * ((HPC * NQS) // 2))
            sinc(g.collective_compute(
                "AllGather", ALU.bypass, replica_groups=[list(range(NC))],
                ins=[o_loc.ap().opt()], outs=[o_all.ap().opt()]), "cc")

        if dbg:
            o_loc_dbg = nc.declare_dram_parameter("o_loc_dbg", [OSL, T], BF16,
                                                  isOutput=True)

            @block.sync
            def _(sync):
                swait(sync, "cc", 5)
                for hh in range(HPC):
                    sinc(sync.dma_start(
                        out=o_loc_dbg[hh * 128:(hh + 1) * 128, :],
                        in_=o_loc[hh * 128:(hh + 1) * 128, :]), "odbg", 16)
                swait(sync, "odbg", 16 * HPC)

        st_A.close()

        if upto == "A":
            @block.sync
            def _(sync):
                swait(sync, "cc", 5)
                for t in range(TT):
                    sinc(sync.dma_start(out=out[t * 128:(t + 1) * 128, :],
                                        in_=x_my[t * 128:(t + 1) * 128, :]),
                         "outw0", 16)
                swait(sync, "outw0", sval("outw0"))
            return nc, c

        st_O = E(contextlib.ExitStack())
        EO = st_O.enter_context
        bo_bc = EO(nc.sbuf_tensor("bo_bc", [128, D], F32))
        attn_tail0 = [("exp", sval("exp")), ("on", sval("on")),
                      ("lin", sval("lin")), ("bce", sval("bce"))]
        bo_base = {"row": sval("p0row"), "pe": sval("p0pe"), "ev": sval("p0ev")}
        with nc.sbuf_tensor("rowb2", [1, D], F32) as rowb2, \
             nc.psum_tensor("psBo", [128, 512], F32) as psBo:
            @block.scalar
            def _(sc):
                sinc(sc.dma_start(out=rowb2[0:1, :], in_=bo_in[:, :]), "p0row", 16)

            @block.tensor
            def _(t):
                swait(t, "p0row", bo_base["row"] + 16)
                for m in attn_tail0:
                    swaitm(t, m)
                for k in range(DS):
                    swait(t, "p0ev", bo_base["ev"] + k)
                    sinc(t.matmul(psBo[:, :], ones_r[:],
                                  rowb2[0:1, k * 512:(k + 1) * 512],
                                  start=True, stop=True), "p0pe")

            @block.vector
            def _(v):
                for k in range(DS):
                    swait(v, "p0pe", bo_base["pe"] + k + 1)
                    sinc(v.tensor_copy(out=bo_bc[:, k * 512:(k + 1) * 512],
                                       in_=psBo[:, :]), "p0ev")
        bo_ev_done = sval("p0ev")
        att_sb = EO(nc.sbuf_tensor("att_sb", [128, DCH * ROWS], BF16))
        wo_t = [EO(nc.sbuf_tensor(f"wo_t{i}", [128, 512], BF16)) for i in range(2)]
        tmp_o = [EO(nc.sbuf_tensor(f"tmp_o{i}", [128, 512], F32)) for i in range(2)]
        x1sl = [EO(nc.sbuf_tensor(f"x1sl{i}", [128, 512], F32)) for i in range(2)]
        xsl = [EO(nc.sbuf_tensor(f"xsl{i}", [128, 512], F32)) for i in range(2)]

        @block.sync
        def _(sync):
            swait(sync, "cc", 5)
            swait(sync, "cid", 16)
            reg = sync.value_load(cid_sb[0:1, 0:1])
            reg = sync.snap(reg, min_val=0, max_val=NC - 1)
            for b in sync.Switch(reg, NC):
                for rb in range(8):
                    r0 = rb * (D // 8)
                    sync.dma_start(
                        out=o_my[r0:r0 + D // 8, :],
                        in_=o_all[r0:r0 + D // 8, b * ROWS:(b + 1) * ROWS]
                    ).then_inc(sget("omy"), 16)
            sem_n["omy"] = 16 * 8
            swait(sync, "omy", 16 * 8)
            swait(sync, "p0row", sval("p0row"))
            swait(sync, "p0pe", sval("p0pe"))
            for ac in range(DCH):
                sinc(sync.dma_start(out=att_sb[:, ac * ROWS:(ac + 1) * ROWS],
                                    in_=o_my[ac * 128:(ac + 1) * 128, :]), "attl", 16)

        with contextlib.ExitStack() as stO_ps:
            psOt_l = [stO_ps.enter_context(nc.psum_tensor(f"psOt{i}", [128, 512], F32))
                      for i in range(TT)]
            @block.scalar
            def _(sc):
                swait(sc, "cc", 1)
                swait(sc, "p0row", sval("p0row"))
                swait(sc, "p0pe", sval("p0pe"))
                for k in range(DS * DCH):
                    sl = k % 2
                    swait(sc, "opk", max(0, k - 1))
                    ds, ac = divmod(k, DCH)
                    sinc(sc.dma_start(out=wo_t[sl][:, :],
                                      in_=woT_all[ac * 128:(ac + 1) * 128,
                                                  ds * 512:(ds + 1) * 512]),
                         f"wox{sl}", 16)

            for ds in range(DS):
                @block.tensor
                def _(t):
                    for ac in range(DCH):
                        k = ds * DCH + ac
                        sl = k % 2
                        swait(t, f"wox{sl}", 16 * (k // 2 + 1))
                        if ac == 0:
                            swait(t, "attl", 16 * DCH)
                            if ds == 0:
                                swait(t, "p0ev", bo_ev_done)
                            else:
                                swait(t, "oev", 2 * TT * ds)
                        for tt in range(TT):
                            mm = t.matmul(psOt_l[tt][:, :],
                                          att_sb[:, ac * ROWS + tt * 128:
                                                 ac * ROWS + (tt + 1) * 128],
                                          wo_t[sl][:, :],
                                          start=(ac == 0), stop=(ac == DCH - 1))
                        sinc(mm, "opk")

                for tt in range(TT):
                    k2 = ds * TT + tt
                    sl = k2 % 2

                    @block.sync
                    def _(sync):
                        swait(sync, "oev", max(0, 2 * (k2 - 1)))
                        sinc(sync.dma_start(out=xsl[sl][:, :],
                                            in_=x_my[tt * 128:(tt + 1) * 128,
                                                     ds * 512:(ds + 1) * 512]),
                             f"oxl{sl}", 16)

                    @block.vector
                    def _(v):
                        swait(v, "opk", DCH * (ds + 1))
                        swait(v, "oev", max(0, 2 * (k2 - 1)))
                        sinc(v.tensor_add(tmp_o[sl][:, :],
                                          psOt_l[tt][:, :],
                                          bo_bc[:, ds * 512:(ds + 1) * 512]), "oev")
                        swait(v, "oev", sval("oev"))
                        swait(v, f"oxl{sl}", 16 * (k2 // 2 + 1))
                        swait(v, f"x1w{sl}", 16 * (k2 // 2))
                        sinc(v.tensor_add(x1sl[sl][:, :], tmp_o[sl][:, :],
                                          xsl[sl][:, :]), "oev")

                    @block.sync
                    def _(sync):
                        swait(sync, "oev", 2 * (k2 + 1))
                        sinc(sync.dma_start(out=x1_d[tt * 128:(tt + 1) * 128,
                                                     ds * 512:(ds + 1) * 512],
                                            in_=x1sl[sl][:, :]), f"x1w{sl}", 16)

        NT = c["NT"]
        x1_gate = [("x1w0", sval("x1w0")), ("x1w1", sval("x1w1"))]
        for t in range(TT):
            emit_ln_tile(t, NT + t, x1_d, xn1_tm,
                         gate_marks=x1_gate if t == 0 else ())
        ln2_done = (sval("Lxnw0"), sval("Lxnw1"))

        st_O.close()

        if upto == "O":
            @block.sync
            def _(sync):
                swait(sync, "Lxnw0", ln2_done[0])
                swait(sync, "Lxnw1", ln2_done[1])
                for t in range(TT):
                    sinc(sync.dma_start(out=out[t * 128:(t + 1) * 128, :],
                                        in_=x1_d[t * 128:(t + 1) * 128, :]),
                         "outw0", 16)
                swait(sync, "outw0", sval("outw0"))
            return nc, c

        oev_total = sval("oev")
        opk_total = sval("opk")
        st_LN.close()
        st_F = E(contextlib.ExitStack())
        EF = st_F.enter_context
        sd_bc = EF(nc.sbuf_tensor("sd_bc", [128, D], F32))
        sd_base = {"row": sval("p0row"), "pe": sval("p0pe"), "ev": sval("p0ev")}
        with nc.sbuf_tensor("rowb3", [1, D], F32) as rowb3, \
             nc.psum_tensor("psSd", [128, 512], F32) as psSd:
            @block.scalar
            def _(sc):
                sinc(sc.dma_start(out=rowb3[0:1, :], in_=sd_in[:, :]), "p0row", 16)

            @block.tensor
            def _(t):
                swait(t, "p0row", sd_base["row"] + 16)
                swait(t, "oev", oev_total)
                for k in range(DS):
                    swait(t, "p0ev", sd_base["ev"] + k)
                    sinc(t.matmul(psSd[:, :], ones_r[:],
                                  rowb3[0:1, k * 512:(k + 1) * 512],
                                  start=True, stop=True), "p0pe")

            @block.vector
            def _(v):
                for k in range(DS):
                    swait(v, "p0pe", sd_base["pe"] + k + 1)
                    sinc(v.tensor_copy(out=sd_bc[:, k * 512:(k + 1) * 512],
                                       in_=psSd[:, :]), "p0ev")
        sd_ev_done = sval("p0ev")
        xn1_sb = EF(nc.sbuf_tensor("xn1_sb", [128, DCH * TG], BF16,
                                   is_transpose_dest=True))
        hh_sb = EF(nc.sbuf_tensor("hh_sb", [128, ICH * TG], BF16))
        wgs = [EF(nc.sbuf_tensor(f"wgs{i}", [128, DCH * 128], BF16)) for i in range(2)]
        wus = [EF(nc.sbuf_tensor(f"wus{i}", [128, DCH * 128], BF16)) for i in range(2)]
        wds = [EF(nc.sbuf_tensor(f"wds{i}", [128, 512], BF16)) for i in range(2)]
        sgt = [EF(nc.sbuf_tensor(f"sgt{i}", [128, TG], BF16)) for i in range(2)]
        tmp2 = [EF(nc.sbuf_tensor(f"tmp2{i}", [128, 512], F32)) for i in range(2)]
        x1s = [EF(nc.sbuf_tensor(f"x1s{i}", [128, 512], F32)) for i in range(2)]
        x2s = [EF(nc.sbuf_tensor(f"x2s{i}", [128, 512], F32)) for i in range(2)]

        fst = {"ic": 0, "wd": 0, "ev": 0}
        with nc.psum_tensor("psG0", [128, TG], F32) as psG0, \
             nc.psum_tensor("psG1", [128, TG], F32) as psG1, \
             nc.psum_tensor("psU0", [128, TG], F32) as psU0, \
             nc.psum_tensor("psU1", [128, TG], F32) as psU1, \
             nc.psum_tensor("psD", [128, TGT * 512], F32) as psD:
            psG = [psG0, psG1]
            psU = [psU0, psU1]

            for tg in range(NTG):
                @block.scalar
                def _(sc):
                    swait(sc, "Lxnw0", ln2_done[0])
                    swait(sc, "Lxnw1", ln2_done[1])
                    if tg > 0:
                        swait(sc, "pU", sval("pU"))
                    else:
                        swait(sc, "opk", opk_total)
                        swait(sc, "p0row", sval("p0row"))
                        swait(sc, "p0pe", sval("p0pe"))
                    for hc in range(DCH):
                        sinc(sc.dma_start(out=xn1_sb[:, hc * TG:(hc + 1) * TG],
                                          in_=xn1_tm[tg * TG:(tg + 1) * TG,
                                                     hc * 128:(hc + 1) * 128],
                                          transpose=True), "xg1", 16)

                for ic in range(ICH):
                    kk = fst["ic"]
                    fst["ic"] += 1
                    sl = kk % 2

                    @block.sync
                    def _(sync):
                        swait(sync, "pG", max(0, sval("pG") - 1))
                        sinc(sync.dma_start(
                            out=wgs[sl].ap().rearrange("p (c i) -> p c i", c=DCH),
                            in_=wgT_all.ap().rearrange("(c q p) i -> p c q i",
                                                       q=ICHP, p=128)[
                                :, :, ic, :]), f"wgl{sl}", 16)
                        swait(sync, "pU", max(0, sval("pU") - 1))
                        sinc(sync.dma_start(
                            out=wus[sl].ap().rearrange("p (c i) -> p c i", c=DCH),
                            in_=wuT_all.ap().rearrange("(c q p) i -> p c q i",
                                                       q=ICHP, p=128)[
                                :, :, ic, :]), f"wul{sl}", 16)

                    @block.tensor
                    def _(t):
                        swait(t, f"wgl{sl}", 16 * (kk // 2 + 1))
                        swait(t, "xg1", 16 * DCH * (tg + 1))
                        swait(t, "sil", max(0, sval("sil") - 1))
                        if tg == 0 and ic == 0:
                            swait(t, "cc", 4)
                            swait(t, "p0ev", sd_ev_done)
                        for hc in range(DCH):
                            mg = t.matmul(psG[sl][:, :],
                                          wgs[sl][:, hc * 128:(hc + 1) * 128],
                                          xn1_sb[:, hc * TG:(hc + 1) * TG],
                                          start=(hc == 0), stop=(hc == DCH - 1))
                        sinc(mg, "pG")
                        swait(t, f"wul{sl}", 16 * (kk // 2 + 1))
                        swait(t, "hhe", max(0, sval("hhe") - 1))
                        for hc in range(DCH):
                            mu = t.matmul(psU[sl][:, :],
                                          wus[sl][:, hc * 128:(hc + 1) * 128],
                                          xn1_sb[:, hc * TG:(hc + 1) * TG],
                                          start=(hc == 0), stop=(hc == DCH - 1))
                        sinc(mu, "pU")

                    @block.scalar
                    def _(sc):
                        swait(sc, "pG", sval("pG"))
                        swait(sc, "hhe", max(0, sval("hhe") - 1))
                        sinc(sc.activation(out=sgt[sl][:, :], in_=psG[sl][:, :],
                                           func=AF.Silu,
                                           scale=sgc[:, ic:ic + 1]), "sil")

                    @block.vector
                    def _(v):
                        swait(v, "pU", sval("pU"))
                        swait(v, "sil", sval("sil"))
                        if tg > 0 and ic == 0:
                            swait(v, "pD", sval("pD"))
                        sinc(v.scalar_tensor_tensor(
                            out=hh_sb[:, ic * TG:(ic + 1) * TG],
                            in0=psU[sl][:, :], scalar=suc[:, ic:ic + 1],
                            in1=sgt[sl][:, :], op0=ALU.mult, op1=ALU.mult), "hhe")

                for hs in range(DS):
                    for ic in range(ICH):
                        kd = fst["wd"]
                        fst["wd"] += 1
                        sl = kd % 2

                        @block.sync
                        def _(sync):
                            swait(sync, "pD", max(0, sval("pD") - 1))
                            sinc(sync.dma_start(out=wds[sl][:, :],
                                                in_=wdT_all[ic * 128:(ic + 1) * 128,
                                                            hs * 512:(hs + 1) * 512]),
                                 f"wdl{sl}", 16)

                        @block.tensor
                        def _(t):
                            swait(t, f"wdl{sl}", 16 * (kd // 2 + 1))
                            if ic == 0:
                                swait(t, "hhe", ICH * (tg + 1))
                                swait(t, "x2e", sval("x2e"))
                            for tt in range(TGT):
                                md = t.matmul(psD[:, tt * 512:tt * 512 + 512],
                                              hh_sb[:, ic * TG + tt * 128:
                                                    ic * TG + (tt + 1) * 128],
                                              wds[sl][:, :],
                                              start=(ic == 0), stop=(ic == ICH - 1))
                            sinc(md, "pD")

                    for tt in range(TGT):
                        ke = fst["ev"]
                        fst["ev"] += 1
                        sl = ke % 2
                        row = tg * TG + tt * 128

                        @block.sync
                        def _(sync):
                            swait(sync, "x2e", max(0, 2 * (ke - 1)))
                            swait(sync, "x1w0", sval("x1w0"))
                            swait(sync, "x1w1", sval("x1w1"))
                            sinc(sync.dma_start(out=x1s[sl][:, :],
                                                in_=x1_d[row:row + 128,
                                                         hs * 512:(hs + 1) * 512]),
                                 f"fx1{sl}", 16)

                        @block.vector
                        def _(v):
                            swait(v, "pD", sval("pD"))
                            swait(v, "x2e", max(0, 2 * (ke - 1)))
                            sinc(v.tensor_mul(tmp2[sl][:, :],
                                              psD[:, tt * 512:tt * 512 + 512],
                                              sd_bc[:, hs * 512:(hs + 1) * 512]),
                                 "x2e")
                            swait(v, "x2e", sval("x2e"))
                            swait(v, f"fx1{sl}", 16 * (ke // 2 + 1))
                            swait(v, f"outw{sl}", 16 * (ke // 2))
                            sinc(v.tensor_add(x2s[sl][:, :], tmp2[sl][:, :],
                                              x1s[sl][:, :]), "x2e")

                        @block.sync
                        def _(sync):
                            swait(sync, "x2e", 2 * (ke + 1))
                            sinc(sync.dma_start(out=out[row:row + 128,
                                                        hs * 512:(hs + 1) * 512],
                                                in_=x2s[sl][:, :]), f"outw{sl}", 16)

        @block.sync
        def _(sync):
            swait(sync, "outw0", sval("outw0"))
            swait(sync, "outw1", sval("outw1"))

        st_F.close()

    return nc, c


def _shard_inputs(c, inputs):
    T, D, I = c["T"], c["D"], c["I"]
    ROWS, OSL, DPC = c["ROWS"], c["OSL"], c["DPC"]
    IPC, IPAD, HCC, ICHP = c["IPC"], c["IPAD"], c["HCC"], c["ICHP"]
    x = np.ascontiguousarray(np.asarray(inputs["L0_attn_x"], np.float32).reshape(T, D))
    wq = np.asarray(inputs["L0_attn_Wq"], np.float32)
    wk = np.asarray(inputs["L0_attn_Wk"], np.float32)
    wv = np.asarray(inputs["L0_attn_Wv"], np.float32)
    bq = np.asarray(inputs["L0_attn_bq"], np.float32).reshape(1, D)
    bk = np.asarray(inputs["L0_attn_bk"], np.float32).reshape(1, D)
    bv = np.asarray(inputs["L0_attn_bv"], np.float32).reshape(1, D)
    wo = np.asarray(inputs["L0_attn_Wo"], np.float32)
    bo = np.asarray(inputs["L0_attn_bo"], np.float32).reshape(1, D)
    wg = np.asarray(inputs["L0_ffn_w_gate"], np.int32)
    wu = np.asarray(inputs["L0_ffn_w_up"], np.int32)
    wd = np.asarray(inputs["L0_ffn_w_down"], np.int32)
    sg = np.zeros((1, IPAD), np.float32)
    sg[0, :I] = np.asarray(inputs["L0_ffn_s_gate"], np.float32)
    su = np.zeros((1, IPAD), np.float32)
    su[0, :I] = np.asarray(inputs["L0_ffn_s_up"], np.float32)
    sd = np.asarray(inputs["L0_ffn_s_down"], np.float32).reshape(1, D)

    def blk_gate(w, q):
        a = np.zeros((IPAD, DPC), np.int32)
        a[:I, :] = w[:, q * DPC:(q + 1) * DPC]
        a = a.reshape(ICHP, 128, HCC, 128)
        a = a.transpose(2, 0, 3, 1)
        return np.ascontiguousarray(a.reshape(HCC * ICHP * 128, 128))

    def wdT_slice(q):
        a = np.zeros((IPC * 128, D), np.int32)
        lo = q * IPC * 128
        hi = min(I, (q + 1) * IPC * 128)
        if hi > lo:
            a[0:hi - lo, :] = wd[:, lo:hi].T
        return np.ascontiguousarray(a)

    import ml_dtypes
    p_ = np.arange(128)[:, None]
    f_ = np.arange(512)[None, :]
    mask = np.concatenate(
        [(f_ - p_ - 128 * j >= 0) for j in range(4)],
        axis=1).astype(ml_dtypes.bfloat16)
    in_maps = []
    for q in range(NC):
        in_maps.append({
            "x_full": x,
            "x_my": np.ascontiguousarray(x[q * ROWS:(q + 1) * ROWS, :]),
            "wqT_sl": np.ascontiguousarray(wq[q * OSL:(q + 1) * OSL, :].T),
            "wkT_sl": np.ascontiguousarray(wk[q * OSL:(q + 1) * OSL, :].T),
            "wvT_sl": np.ascontiguousarray(wv[q * OSL:(q + 1) * OSL, :].T),
            "bq_sl": np.ascontiguousarray(bq[:, q * OSL:(q + 1) * OSL]),
            "bk_sl": np.ascontiguousarray(bk[:, q * OSL:(q + 1) * OSL]),
            "bv_sl": np.ascontiguousarray(bv[:, q * OSL:(q + 1) * OSL]),
            "woT_sl": np.ascontiguousarray(wo[:, q * DPC:(q + 1) * DPC].T),
            "bo_in": bo,
            "wgT_blk": blk_gate(wg, q),
            "wuT_blk": blk_gate(wu, q),
            "wdT_sl": wdT_slice(q),
            "sg_in": sg,
            "su_in": su,
            "sd_in": sd,
            "cid_in": np.array([[q]], np.int32),
            "mask_in": mask,
        })
    return in_maps


def kernel(**inputs):
    T, D, HPC, I = 8192, 4096, 4, 11008
    nc, c = build(T, D, HPC, I)
    in_maps = _shard_inputs(c, inputs)
    res = run_bass_kernel_spmd(nc, in_maps, core_ids=list(range(NC)))
    shards = [np.asarray(res.results[q]["out"]) for q in range(NC)]
    return np.concatenate(shards, axis=0).reshape(1, T, D).astype(np.float32)


# version 06
# speedup vs baseline: 1.0557x; 1.0042x over previous
import numpy as np
import contextlib
import concourse.bass as bass
import concourse.mybir as mybir
from concourse.bass_utils import run_bass_kernel_spmd

F32 = mybir.dt.float32
FP8 = mybir.dt.float8e4
BF16 = mybir.dt.bfloat16
I32 = mybir.dt.int32
AF = mybir.ActivationFunctionType
ALU = mybir.AluOpType

NC = 8


def derived(T, D, HPC, I):
    c = dict(T=T, D=D, HPC=HPC, I=I)
    c["ROWS"] = T // NC
    c["NT"] = T // 128
    c["DCH"] = D // 128
    c["DS"] = D // 512
    c["DPC"] = D // NC
    c["HCC"] = c["DPC"] // 128
    c["ICH"] = I // 128
    c["IPC"] = -(-c["ICH"] // NC)
    c["IPAD"] = c["IPC"] * 128 * NC
    c["NIB"] = -(-I // 512)
    c["ICHP"] = c["IPC"] * NC
    c["QS"] = 512
    c["NQS"] = T // 512
    c["QSC"] = 4
    c["AG1"] = 256
    c["NAG1"] = T // 256
    c["AGD"] = 2
    c["TT"] = c["ROWS"] // 128
    c["TG"] = min(512, c["ROWS"])
    c["NTG"] = c["ROWS"] // c["TG"]
    c["TGT"] = c["TG"] // 128
    c["OSL"] = HPC * 128
    assert T % 512 == 0 and T % (NC * 128) == 0 and D % 512 == 0 and I % 128 == 0
    assert c["TT"] <= 8
    return c


def build(T, D, HPC, I, dbg=False, upto=None):
    c = derived(T, D, HPC, I)
    ROWS, DCH, DS, DPC, HCC = c["ROWS"], c["DCH"], c["DS"], c["DPC"], c["HCC"]
    ICH, IPC, IPAD, NIB = c["ICH"], c["IPC"], c["IPAD"], c["NIB"]
    ICHP = c["ICHP"]
    NQS, QSC, AG1, NAG1, AGD = c["NQS"], c["QSC"], c["AG1"], c["NAG1"], c["AGD"]
    TT, TG, NTG, TGT, OSL = c["TT"], c["TG"], c["NTG"], c["TGT"], c["OSL"]
    INV_D = 1.0 / D
    QSCALE = float(1.0 / np.sqrt(128.0))
    EPS = 1e-5

    nc = bass.Bass("TRN2", target_bir_lowering=False, debug=False)

    x_full = nc.declare_dram_parameter("x_full", [T, D], F32, isOutput=False)
    x_my = nc.declare_dram_parameter("x_my", [ROWS, D], F32, isOutput=False)
    wqT_sl = nc.declare_dram_parameter("wqT_sl", [D, OSL], F32, isOutput=False)
    wkT_sl = nc.declare_dram_parameter("wkT_sl", [D, OSL], F32, isOutput=False)
    wvT_sl = nc.declare_dram_parameter("wvT_sl", [D, OSL], F32, isOutput=False)
    bq_sl = nc.declare_dram_parameter("bq_sl", [1, OSL], F32, isOutput=False)
    bk_sl = nc.declare_dram_parameter("bk_sl", [1, OSL], F32, isOutput=False)
    bv_sl = nc.declare_dram_parameter("bv_sl", [1, OSL], F32, isOutput=False)
    woT_sl = nc.declare_dram_parameter("woT_sl", [DPC, D], F32, isOutput=False)
    bo_in = nc.declare_dram_parameter("bo_in", [1, D], F32, isOutput=False)
    wgT_blk = nc.declare_dram_parameter("wgT_blk", [HCC * ICHP * 128, 128], I32,
                                        isOutput=False)
    wuT_blk = nc.declare_dram_parameter("wuT_blk", [HCC * ICHP * 128, 128], I32,
                                        isOutput=False)
    wdT_sl = nc.declare_dram_parameter("wdT_sl", [IPC * 128, D], I32, isOutput=False)
    sg_in = nc.declare_dram_parameter("sg_in", [1, IPAD], F32, isOutput=False)
    su_in = nc.declare_dram_parameter("su_in", [1, IPAD], F32, isOutput=False)
    sd_in = nc.declare_dram_parameter("sd_in", [1, D], F32, isOutput=False)
    cid_in = nc.declare_dram_parameter("cid_in", [1, 1], I32, isOutput=False)
    mask_in = nc.declare_dram_parameter("mask_in", [128, 4 * 512], BF16, isOutput=False)
    out = nc.declare_dram_parameter("out", [ROWS, D], F32, isOutput=True)

    def idram(name, shape, dt_):
        if dbg:
            return nc.declare_dram_parameter(name, shape, dt_, isOutput=True)
        return nc.dram_tensor(name, shape, dt_)

    xn_tm = idram("xn_tm", [T, D], BF16)
    xn_fm = idram("xn_fm", [D, T], BF16)
    wqT_bnc = nc.dram_tensor("wqT_bnc", [D, OSL], BF16)
    wkT_bnc = nc.dram_tensor("wkT_bnc", [D, OSL], BF16)
    wvT_bnc = nc.dram_tensor("wvT_bnc", [D, OSL], BF16)
    woT_loc = nc.dram_tensor("woT_loc", [DPC, D], BF16)
    woT_all = nc.dram_tensor("woT_all", [D, D], BF16, addr_space="Shared")
    wgT_loc = nc.dram_tensor("wgT_loc", [HCC * ICHP * 128, 128], BF16)
    wuT_loc = nc.dram_tensor("wuT_loc", [HCC * ICHP * 128, 128], BF16)
    wdT_loc = nc.dram_tensor("wdT_loc", [IPC * 128, D], BF16)
    wgT_all = nc.dram_tensor("wgT_all", [DCH * ICHP * 128, 128], BF16, addr_space="Shared")
    wuT_all = nc.dram_tensor("wuT_all", [DCH * ICHP * 128, 128], BF16, addr_space="Shared")
    wdT_all = nc.dram_tensor("wdT_all", [IPAD, D], BF16, addr_space="Shared")
    v_dram = idram("v_dram", [HPC * T, 128], BF16)
    o_loc = nc.dram_tensor("o_loc", [OSL, T], BF16)
    o_all = nc.dram_tensor("o_all", [D, T], BF16, addr_space="Shared")
    o_my = idram("o_my", [D, ROWS], BF16)
    x1_d = idram("x1_d", [ROWS, D], F32)
    xn1_tm = idram("xn1_tm", [ROWS, D], BF16)

    sem_n = {}
    sem_h = {}

    with contextlib.ExitStack() as st0:
        E = st0.enter_context
        block = E(nc.Block())

        def sget(name):
            if name not in sem_h:
                sem_h[name] = E(nc.semaphore(name))
                sem_n[name] = 0
            return sem_h[name]

        def sinc(instr, name, k=1):
            instr.then_inc(sget(name), k)
            sem_n[name] += k
            return sem_n[name]

        def sval(name):
            sget(name)
            return sem_n[name]

        def swait(eng, name, v):
            if v > 0:
                eng.wait_ge(sget(name), v)

        def swaitm(eng, mark):
            if mark is not None:
                swait(eng, mark[0], mark[1])

        ones_r = E(nc.sbuf_tensor("ones_r", [1, 128], F32))
        ones_c = E(nc.sbuf_tensor("ones_c", [128, 1], F32))
        ones_s = E(nc.sbuf_tensor("ones_s", [1, 1], F32))
        bqc = E(nc.sbuf_tensor("bqc", [128, HPC], F32))
        bkc = E(nc.sbuf_tensor("bkc", [128, HPC], F32))
        bv_bc = E(nc.sbuf_tensor("bv_bc", [128, OSL], F32))
        sgc = E(nc.sbuf_tensor("sgc", [128, ICH], F32))
        suc = E(nc.sbuf_tensor("suc", [128, ICH], F32))
        masks = E(nc.sbuf_tensor("masks", [128, QSC * 512], BF16))
        cid_sb = E(nc.sbuf_tensor("cid_sb", [1, 1], I32))

        @block.sync
        def _(sync):
            sinc(sync.dma_start(out=cid_sb[:], in_=cid_in[:, :]), "cid", 16)

        @block.sync
        def _(sync):
            sinc(sync.dma_start(out=masks[:], in_=mask_in[:, :]), "mskl", 16)

        @block.gpsimd
        def _(g):
            sinc(g.memset(ones_r[:], 1.0), "p0g")
            sinc(g.memset(ones_c[:], 1.0), "p0g")
            sinc(g.memset(ones_s[:], 1.0), "p0g")
        p0g_all = sval("p0g")

        jobs = [
            (bq_sl[:, :], OSL, "col", bqc),
            (bk_sl[:, :], OSL, "col", bkc),
            (bv_sl[:, :], OSL, "bc", bv_bc),
            (sg_in[0:1, 0:I], I, "col", sgc),
            (su_in[0:1, 0:I], I, "col", suc),
        ]
        maxw = max(j[1] for j in jobs)
        chunks_before = [0]
        for (_src, w, kind, _d) in jobs:
            chunks_before.append(chunks_before[-1] + (w // 128 if kind == "col" else -(-w // 512)))

        with nc.sbuf_tensor("rowbuf", [1, maxw], F32) as rowbuf, \
             nc.psum_tensor("pbc", [128, 512], F32) as pbc, \
             nc.psum_tensor("pbc2", [128, 512], F32) as pbc2:
            pb = [pbc, pbc2]

            @block.scalar
            def _(sc):
                for ji, (src, w, kind, dest) in enumerate(jobs):
                    swait(sc, "p0pe", chunks_before[ji])
                    sinc(sc.dma_start(out=rowbuf[0:1, 0:w], in_=src), "p0row", 16)

            @block.tensor
            def _(t):
                kg = 0
                swait(t, "p0g", 3)
                for ji, (src, w, kind, dest) in enumerate(jobs):
                    nch = w // 128 if kind == "col" else -(-w // 512)
                    swait(t, "p0row", 16 * (ji + 1))
                    for k in range(nch):
                        swait(t, "p0ev", max(0, kg - 1))
                        if kind == "col":
                            mm = t.matmul(pb[kg % 2][:, 0:1],
                                          rowbuf[0:1, k * 128:(k + 1) * 128],
                                          ones_s[:], start=True, stop=True)
                        else:
                            lo, hi = k * 512, min(w, k * 512 + 512)
                            mm = t.matmul(pb[kg % 2][:, 0:hi - lo], ones_r[:],
                                          rowbuf[0:1, lo:hi], start=True, stop=True)
                        sinc(mm, "p0pe")
                        kg += 1

            @block.vector
            def _(v):
                kg = 0
                for ji, (src, w, kind, dest) in enumerate(jobs):
                    nch = w // 128 if kind == "col" else -(-w // 512)
                    for k in range(nch):
                        swait(v, "p0pe", kg + 1)
                        if kind == "col":
                            cp = v.tensor_copy(out=dest[:, k:k + 1],
                                               in_=pb[kg % 2][:, 0:1])
                        else:
                            lo, hi = k * 512, min(w, k * 512 + 512)
                            cp = v.tensor_copy(out=dest[:, lo:hi],
                                               in_=pb[kg % 2][:, 0:hi - lo])
                        sinc(cp, "p0ev")
                        kg += 1

        def flat_aps(t, total_elems, nchunks):
            per = total_elems // 128 // nchunks
            return [t.ap().rearrange("a i -> (a i)").rearrange("(p x) -> p x", p=128)[
                :, k * per:(k + 1) * per] for k in range(nchunks)]

        cast_jobs = []
        for t_in, t_out_, nch in ((woT_sl, woT_loc, 8),
                                  (wqT_sl, wqT_bnc, 8), (wkT_sl, wkT_bnc, 8),
                                  (wvT_sl, wvT_bnc, 8)):
            n = 1
            for d_ in t_in.shape:
                n *= d_
            ins_l = flat_aps(t_in, n, nch)
            outs_l = flat_aps(t_out_, n, nch)
            cast_jobs += list(zip(ins_l, outs_l, [n // 128 // nch] * nch))
        cw_qkv_jobs = len(cast_jobs)
        for t_in, t_out_, nch in ((wgT_blk, wgT_loc, 16), (wuT_blk, wuT_loc, 16),
                                  (wdT_sl, wdT_loc, 16)):
            n = 1
            for d_ in t_in.shape:
                n *= d_
            ins_l = flat_aps(t_in, n, nch)
            outs_l = flat_aps(t_out_, n, nch)
            cast_jobs += list(zip(ins_l, outs_l, [n // 128 // nch] * nch))
        cw_wg_jobs = cw_qkv_jobs + 16
        cw_wu_jobs = cw_qkv_jobs + 32
        maxper = max(w for (_i, _o, w) in cast_jobs)

        cnvb = [E(nc.sbuf_tensor(f"cnv{i}", [128, maxper], BF16, side="right"))
                for i in range(2)]

        def cnt_ab(k):
            return (k + 1) // 2, k // 2

        ag_at = {cw_qkv_jobs: ("woT", None),
                 cw_wg_jobs: ("wg", None), cw_wu_jobs: ("wu", None),
                 len(cast_jobs): ("wd", None)}
        ag_tensors = {"woT": (woT_loc, woT_all), "wg": (wgT_loc, wgT_all),
                      "wu": (wuT_loc, wuT_all), "wd": (wdT_loc, wdT_all)}

        @block.gpsimd
        def _(g):
            for kk, (iap, oap, w) in enumerate(cast_jobs):
                sA = "cwA" if kk % 2 == 0 else "cwB"
                sL = "clA" if kk % 2 == 0 else "clB"
                nprev = (kk - 2) // 2 + 1 if kk >= 2 else 0
                swait(g, sA, 16 * nprev)
                sinc(g.dma_start(out=cnvb[kk % 2][:, 0:w], in_=iap), sL, 16)
                swait(g, sL, sval(sL))
                sinc(g.dma_start(out=oap, in_=cnvb[kk % 2][:, 0:w]), sA, 16)
                if kk + 1 in ag_at:
                    a_n, b_n = cnt_ab(kk + 1)
                    swait(g, "cwA", 16 * a_n)
                    swait(g, "cwB", 16 * b_n)
                    loc_t, all_t = ag_tensors[ag_at[kk + 1][0]]
                    sinc(g.collective_compute(
                        "AllGather", ALU.bypass,
                        replica_groups=[list(range(NC))],
                        ins=[loc_t.ap().opt()], outs=[all_t.ap().opt()]), "cc")

        cw_qkv = cnt_ab(cw_qkv_jobs)

        def swait_cw(eng, njobs):
            a_n, b_n = cnt_ab(njobs)
            swait(eng, "cwA", 16 * a_n)
            swait(eng, "cwB", 16 * b_n)

        st_LN = E(contextlib.ExitStack())
        EL = st_LN.enter_context
        lb = {
            "xb": [EL(nc.sbuf_tensor(f"xb{i}", [128, D], F32, side="right")) for i in range(2)],
            "junk_d": [EL(nc.sbuf_tensor(f"junkd{i}", [128, D], FP8, side="right")) for i in range(2)],
            "junk_a": [EL(nc.sbuf_tensor(f"junka{i}", [128, D], FP8, side="right")) for i in range(2)],
            "sum": [EL(nc.sbuf_tensor(f"lsum{i}", [128, 1], F32, side="right")) for i in range(2)],
            "sq": [EL(nc.sbuf_tensor(f"lsq{i}", [128, 1], F32, side="right")) for i in range(2)],
            "m2": [EL(nc.sbuf_tensor(f"lm2{i}", [128, 1], F32, side="right")) for i in range(2)],
            "var": [EL(nc.sbuf_tensor(f"lvar{i}", [128, 1], F32, side="right")) for i in range(2)],
            "lnv": [EL(nc.sbuf_tensor(f"llnv{i}", [128, 1], F32, side="right")) for i in range(2)],
            "var2": [EL(nc.sbuf_tensor(f"lvar2{i}", [128, 1], F32, side="right")) for i in range(2)],
            "rstd": [EL(nc.sbuf_tensor(f"lrstd{i}", [128, 1], F32, side="right")) for i in range(2)],
            "negm": [EL(nc.sbuf_tensor(f"lnegm{i}", [128, 1], F32, side="right")) for i in range(2)],
            "xnb": [EL(nc.sbuf_tensor(f"lxnb{i}", [128, D], BF16, side="right")) for i in range(2)],
        }
        xfg = [EL(nc.sbuf_tensor(f"xfg{i}", [128, 4 * 512], BF16,
                                 is_transpose_dest=True, side="right"))
               for i in range(2)]
        xfw8 = [xfg[i // 4][:, (i % 4) * 512:(i % 4 + 1) * 512] for i in range(8)]

        def emit_ln_tile(trow, t, src, dst, gate_marks=()):
            pfx = "L"
            q = t % 2

            @block.sync
            def _(sync):
                swait(sync, pfx + "an", t - 1)
                for m in gate_marks:
                    swaitm(sync, m)
                sinc(sync.dma_start(out=lb["xb"][q][:],
                                    in_=src[trow * 128:(trow + 1) * 128, :]),
                     pfx + f"lx{q}", 16)

            @block.vector
            def _(v):
                swait(v, pfx + f"lx{q}", 16 * (t // 2 + 1))
                swait(v, pfx + "d1", t - 1)
                sinc(v.tensor_scalar(out=lb["junk_d"][q][:], in0=lb["xb"][q][:],
                                     scalar1=1.0, scalar2=0.0, op0=ALU.mult,
                                     op1=ALU.add,
                                     accum_out=lb["sum"][q][:]), pfx + "d1")

            @block.scalar
            def _(sc):
                swait(sc, pfx + f"lx{q}", 16 * (t // 2 + 1))
                swait(sc, pfx + "a2", t - 1)
                sinc(sc.activation(out=lb["junk_a"][q][:], in_=lb["xb"][q][:],
                                   func=AF.Square, accum_out=lb["sq"][q][:]),
                     pfx + "a2")

            @block.vector
            def _(v):
                swait(v, pfx + "d1", t + 1)
                swait(v, pfx + "dv1", t - 1)
                sinc(v.tensor_mul(lb["m2"][q][:], lb["sum"][q][:], lb["sum"][q][:]),
                     pfx + "dv1")
                swait(v, pfx + "dv1", t + 1)
                swait(v, pfx + "a2", t + 1)
                swait(v, pfx + "dv2", t - 1)
                sinc(v.scalar_tensor_tensor(out=lb["var"][q][:], in0=lb["m2"][q][:],
                                            scalar=-INV_D, in1=lb["sq"][q][:],
                                            op0=ALU.mult, op1=ALU.add), pfx + "dv2")
                swait(v, pfx + "dv2", t + 1)
                swait(v, pfx + "dv2b", t - 1)
                sinc(v.tensor_scalar(out=lb["var2"][q][:], in0=lb["var"][q][:],
                                     scalar1=INV_D, scalar2=EPS,
                                     op0=ALU.mult, op1=ALU.add), pfx + "dv2b")

            @block.scalar
            def _(sc):
                swait(sc, pfx + "dv2b", t + 1)
                swait(sc, pfx + "aln", t - 1)
                sinc(sc.activation(out=lb["lnv"][q][:], in_=lb["var2"][q][:],
                                   func=AF.Ln), pfx + "aln")
                swait(sc, pfx + "aln", t + 1)
                swait(sc, pfx + "aex", t - 1)
                sinc(sc.activation(out=lb["rstd"][q][:], in_=lb["lnv"][q][:],
                                   func=AF.Exp, scale=-0.5), pfx + "aex")

            @block.vector
            def _(v):
                swait(v, pfx + "aex", t + 1)
                swait(v, pfx + "dv3", t - 1)
                sinc(v.tensor_scalar(out=lb["negm"][q][:], in0=lb["sum"][q][:],
                                     scalar1=lb["rstd"][q][:], scalar2=-INV_D,
                                     op0=ALU.mult, op1=ALU.mult), pfx + "dv3")

            @block.scalar
            def _(sc):
                swait(sc, pfx + "dv3", t + 1)
                if t >= 2:
                    swait(sc, pfx + f"xnw{q}", 16 * ((t - 2) // 2 + 1))
                sinc(sc.activation(out=lb["xnb"][q][:], in_=lb["xb"][q][:],
                                   func=AF.Identity, scale=lb["rstd"][q][:],
                                   bias=lb["negm"][q][:]), pfx + "an")

            @block.sync
            def _(sync):
                swait(sync, pfx + "an", t + 1)
                sinc(sync.dma_start(out=dst[trow * 128:(trow + 1) * 128, :],
                                    in_=lb["xnb"][q][:]), pfx + f"xnw{q}", 16)

        xf_last = {0: 0, 1: 0}
        xf_k = [0]

        def emit_xnfm_strip(s):
            @block.sync
            def _(sync):
                swait(sync, "Lxnw0", 16 * (2 * s + 2))
                swait(sync, "Lxnw1", 16 * (2 * s + 2))
                for dcg in range(0, DCH, 4):
                    grp = (xf_k[0] // 4) % 2
                    for j in range(4):
                        sl4 = grp * 4 + j
                        xf_k[0] += 1
                        swait(sync, f"xfw{grp}", xf_last[grp])
                        sinc(sync.dma_start(
                            out=xfw8[sl4][:, :],
                            in_=xn_tm[s * 512:(s + 1) * 512,
                                      (dcg + j) * 128:(dcg + j + 1) * 128],
                            transpose=True), f"xfl{grp}", 16)
                    swait(sync, f"xfl{grp}", sval(f"xfl{grp}"))
                    xf_last[grp] = sinc(sync.dma_start(
                        out=xn_fm[dcg * 128:(dcg + 4) * 128,
                                  s * 512:(s + 1) * 512].rearrange(
                            "(c p) t -> p c t", p=128),
                        in_=xfg[grp].ap().rearrange("p (c t) -> p c t", c=4)),
                        f"xfw{grp}", 16)

        st_A = E(contextlib.ExitStack())
        EA = st_A.enter_context
        stp = [EA(nc.sbuf_tensor(f"stp{i}", [128, DCH * AG1], BF16)) for i in range(2)]
        K_h = EA(nc.sbuf_tensor("K_h", [128, T], BF16))
        Q_h = EA(nc.sbuf_tensor("Q_h", [128, T], BF16))
        wqT_sb = EA(nc.sbuf_tensor("wqT_sb", [128, DCH * 128], BF16,
                                   is_transpose_dest=True))
        wkT_sb = EA(nc.sbuf_tensor("wkT_sb", [128, DCH * 128], BF16,
                                   is_transpose_dest=True))
        v_sb = [EA(nc.sbuf_tensor(f"v_sb{i}", [128, OSL], BF16)) for i in range(2)]

        strip_free = []
        a1v = {"v": 0}
        a2st = {"s": 0, "chunk": 0, "ptmark": [], "lm_mark": [], "on_mark": [],
                "av_mark": [], "lacp_prev": None, "psl_free": []}

        if dbg:
            bnc_dbg = nc.declare_dram_parameter("bnc_dbg", [128, D], BF16,
                                                isOutput=True)

        def emit_wqkT_load(h, prev_head_mark):
            @block.scalar
            def _(sc):
                swait_cw(sc, cw_qkv_jobs)
                if prev_head_mark is not None:
                    swaitm(sc, prev_head_mark)
                for wsrc, dst in ((wqT_bnc, wqT_sb), (wkT_bnc, wkT_sb)):
                    for cg in range(0, DCH, 8):
                        sinc(sc.dma_start(
                            out=dst[:, cg * 128:(cg + 8) * 128].rearrange(
                                "p (c o) -> p c o", c=8),
                            in_=wsrc[cg * 128:(cg + 8) * 128,
                                     h * 128:(h + 1) * 128].rearrange(
                                "(c p) o -> p c o", p=128)), "wqkx", 16)

        a1ev = {"K": 0, "Q": 0, "V": 0}

        def emit_a1_group(h, g, gctr, pKap, pQap, pVap=None, wvT_sb=None,
                          head_first_wait=None):
            sl = gctr % 2

            @block.sync
            def _(sc):
                s_idx = g // 2
                ngrp = (s_idx + 1) * (DCH // 4)
                swait(sc, "xfw0", 16 * ((ngrp + 1) // 2))
                swait(sc, "xfw1", 16 * (ngrp // 2))
                if gctr >= 2:
                    swaitm(sc, strip_free[gctr - 2])
                for cg in range(0, DCH, 8):
                    sinc(sc.dma_start(
                        out=stp[sl][:, cg * AG1:(cg + 8) * AG1].rearrange(
                            "p (c t) -> p c t", c=8),
                        in_=xn_fm[cg * 128:(cg + 8) * 128, :].rearrange(
                            "(c p) t -> p c t", p=128)[
                            :, :, g * AG1:(g + 1) * AG1]), f"stl{sl}", 16)

            @block.tensor
            def _(t):
                swait(t, f"stl{sl}", 16 * (DCH // 8) * (gctr // 2 + 1))
                if g == 0:
                    swait(t, "wqkx", sval("wqkx"))
                    if head_first_wait is not None:
                        for m in head_first_wait:
                            swaitm(t, m)
                swait(t, "evA1", a1ev["K"])
                for dc in range(DCH):
                    mmk = t.matmul(pKap, wkT_sb[:, dc * 128:(dc + 1) * 128],
                                   stp[sl][:, dc * AG1:(dc + 1) * AG1],
                                   start=(dc == 0), stop=(dc == DCH - 1))
                sinc(mmk, "pA1")

            @block.vector
            def _(v):
                swait(v, "pA1", sval("pA1"))
                a1ev["K"] = sinc(
                    v.tensor_scalar(out=K_h[:, g * AG1:(g + 1) * AG1], in0=pKap,
                                    scalar1=bkc[:, h:h + 1], scalar2=None,
                                    op0=ALU.add), "evA1")

            @block.tensor
            def _(t):
                swait(t, "evA1", a1ev["Q"])
                for dc in range(DCH):
                    mmq = t.matmul(pQap, wqT_sb[:, dc * 128:(dc + 1) * 128],
                                   stp[sl][:, dc * AG1:(dc + 1) * AG1],
                                   start=(dc == 0), stop=(dc == DCH - 1))
                sinc(mmq, "pA1")

            @block.vector
            def _(v):
                swait(v, "pA1", sval("pA1"))
                a1ev["Q"] = sinc(
                    v.tensor_scalar(out=Q_h[:, g * AG1:(g + 1) * AG1], in0=pQap,
                                    scalar1=bqc[:, h:h + 1], scalar2=QSCALE,
                                    op0=ALU.add, op1=ALU.mult), "evA1")

            if h == 0:
                for tt in range(AGD):
                    vctr = a1v["v"]
                    a1v["v"] += 1

                    @block.tensor
                    def _(t):
                        swait(t, "evA1", a1ev["V"])
                        if g == 0 and tt == 0:
                            swait(t, "wqkx", sval("wqkx"))
                        for dc in range(DCH):
                            mmv = t.matmul(pVap,
                                           stp[sl][:, dc * AG1 + tt * 128:
                                                   dc * AG1 + (tt + 1) * 128],
                                           wvT_sb[:, dc * OSL:(dc + 1) * OSL],
                                           start=(dc == 0), stop=(dc == DCH - 1))
                        sinc(mmv, "pA1")

                    @block.vector
                    def _(v):
                        swait(v, "pA1", sval("pA1"))
                        swait(v, f"vw{vctr % 2}", 16 * HPC * (vctr // 2))
                        a1ev["V"] = sinc(
                            v.tensor_add(v_sb[vctr % 2][:, :], pVap, bv_bc[:, :]),
                            "evA1")

                    @block.scalar
                    def _(sc):
                        swait(sc, "evA1", a1ev["V"])
                        row = g * AG1 + tt * 128
                        for hh in range(HPC):
                            sinc(sc.dma_start(
                                out=v_dram[hh * T + row:hh * T + row + 128, :],
                                in_=v_sb[vctr % 2][:, hh * 128:(hh + 1) * 128]),
                                f"vw{vctr % 2}", 16)

            strip_free.append(("pA1", sval("pA1")))

        def emit_vh_load(h):
            @block.scalar
            def _(sc):
                swait(sc, "vw0", 16 * HPC * (T // 256))
                swait(sc, "vw1", 16 * HPC * (T // 256))
                if a2st["av_mark"]:
                    swaitm(sc, a2st["av_mark"][-1])
                KT = T // 128
                for kg in range(0, KT, 8):
                    sinc(sc.dma_start(
                        out=vbuf[:, kg * 128:(kg + 8) * 128].rearrange(
                            "p (k f) -> p k f", k=8),
                        in_=v_dram[h * T + kg * 128:h * T + (kg + 8) * 128, :]
                        .rearrange("(k p) f -> p k f", p=128)), "vhl", 16)

        def emit_a2_strip(h, s, use_pool, psS, psO, psL, psBc):
            kmax = QSC * (s + 1)
            sctr = a2st["s"]
            a2st["s"] += 1

            for kc in range(kmax):
                diag = kc >= QSC * s
                j = kc - QSC * s
                eng_pool = use_pool and (kc % 2 == 1)
                cglob = a2st["chunk"]
                a2st["chunk"] += 1

                @block.tensor
                def _(t):
                    if s == 0 and kc == 0:
                        swait(t, "p0g", p0g_all)
                        swait(t, "mskl", 16)
                        swait(t, "evA1", sval("evA1"))
                    swait(t, "exp", max(0, sval("st") - 1))
                    sinc(t.matmul(psS[kc % 2][:, :], K_h[:, kc * 128:(kc + 1) * 128],
                                  Q_h[:, s * 512:(s + 1) * 512],
                                  start=True, stop=True), "st")

                @block.scalar
                def _(sc):
                    swait(sc, "st", sval("st"))
                    if cglob >= 3:
                        for m in a2st["ptmark"][cglob - 3]:
                            swaitm(sc, m)
                    sinc(sc.activation(out=pt[cglob % 3][:, :],
                                       in_=psS[kc % 2][:, :], func=AF.Exp), "exp")

                if diag:
                    @block.vector
                    def _(v):
                        if sval("msk") == 0:
                            swait(v, "mskl", 16)
                        swait(v, "exp", sval("exp"))
                        swait(v, "msk", max(0, sval("msk") - 1))
                        sinc(v.tensor_mul(pt[cglob % 3][:, :], pt[cglob % 3][:, :],
                                          masks[:, j * 512:(j + 1) * 512]), "msk")
                ready = ("msk", sval("msk")) if diag else ("exp", sval("exp"))

                if eng_pool:
                    @block.gpsimd
                    def _(g):
                        swaitm(g, ready)
                        if kc == 1:
                            swaitm(g, a2st["lacp_prev"])
                            op = g.tensor_copy(out=accP[:, :], in_=pt[cglob % 3][:, :])
                        else:
                            swait(g, "lacP", sval("lacP"))
                            op = g.tensor_add(accP[:, :], accP[:, :],
                                              pt[cglob % 3][:, :])
                        sinc(op, "lacP")
                    lmark = ("lacP", sval("lacP"))
                else:
                    @block.vector
                    def _(v):
                        swaitm(v, ready)
                        if kc == 0:
                            if a2st["lm_mark"]:
                                swaitm(v, a2st["lm_mark"][-1])
                            op = v.tensor_copy(out=accD[:, :], in_=pt[cglob % 3][:, :])
                        else:
                            swait(v, "lacD", sval("lacD"))
                            op = v.tensor_add(accD[:, :], accD[:, :],
                                              pt[cglob % 3][:, :])
                        sinc(op, "lacD")
                    lmark = ("lacD", sval("lacD"))

                @block.tensor
                def _(t):
                    swaitm(t, ready)
                    swait(t, "vhl", 16 * (T // 1024) * (h + 1))
                    if kc == 0 and sctr >= 2:
                        swaitm(t, a2st["on_mark"][sctr - 2])
                    sinc(t.matmul(psO[sctr % 2][:, :],
                                  vbuf[:, kc * 128:(kc + 1) * 128],
                                  pt[cglob % 3][:, :],
                                  start=(kc == 0), stop=(kc == kmax - 1)), "av")
                a2st["ptmark"].append([("av", sval("av")), lmark])

            a2st["av_mark"].append(("av", sval("av")))

            if use_pool:
                @block.vector
                def _(v):
                    swait(v, "lacP", sval("lacP"))
                    swait(v, "lacD", sval("lacD"))
                    sinc(v.tensor_add(accD[:, :], accD[:, :], accP[:, :]), "lacD")
                a2st["lacp_prev"] = ("lacD", sval("lacD"))

            @block.tensor
            def _(t):
                swait(t, "lacD", sval("lacD"))
                if sctr >= 2:
                    swaitm(t, a2st["psl_free"][sctr - 2])
                sinc(t.matmul(psL[sctr % 2][0:1, :], ones_c[:, :], accD[:, :],
                              start=True, stop=True), "lm")
            a2st["lm_mark"].append(("lm", sval("lm")))

            @block.scalar
            def _(sc):
                swait(sc, "lm", sval("lm"))
                swait(sc, "lin", max(0, sval("lin") - 2))
                sinc(sc.activation(out=lnl_r[sctr % 2][:, :],
                                   in_=psL[sctr % 2][0:1, :], func=AF.Ln), "lin")
                a2st["psl_free"].append(("lin", sval("lin")))
                swait(sc, "lin", sval("lin"))
                swait(sc, "bcm", sval("bcm"))
                sinc(sc.activation(out=linv_r[sctr % 2][:, :],
                                   in_=lnl_r[sctr % 2][:, :],
                                   func=AF.Exp, scale=-1.0), "lin")

            @block.tensor
            def _(t):
                swait(t, "lin", sval("lin"))
                swait(t, "bce", max(0, sval("bcm")))
                sinc(t.matmul(psBc[:, :], ones_r[:, :], linv_r[sctr % 2][:, :],
                              start=True, stop=True), "bcm")

            @block.vector
            def _(v):
                swait(v, "bcm", sval("bcm"))
                swait(v, "on", sval("on"))
                sinc(v.tensor_copy(out=linv_bc[:, :], in_=psBc[:, :]), "bce")
                swait(v, "bce", sval("bce"))
                swait(v, "av", sval("av"))
                swait(v, f"ow{sctr % 2}", 16 * (sctr // 2))
                sinc(v.tensor_mul(o_sb[sctr % 2][:, :], psO[sctr % 2][:, :],
                                  linv_bc[:, :]), "on")
            a2st["on_mark"].append(("on", sval("on")))

            @block.scalar
            def _(sc):
                swait(sc, "on", sval("on"))
                sinc(sc.dma_start(out=o_loc[h * 128:(h + 1) * 128,
                                            s * 512:(s + 1) * 512],
                                  in_=o_sb[sctr % 2][:, :]), f"ow{sctr % 2}", 16)

        a1_gctr = 0
        with nc.psum_tensor("psK", [128, AG1], F32) as psK, \
             nc.psum_tensor("psQ", [128, AG1], F32) as psQ, \
             nc.psum_tensor("psV", [128, OSL], F32) as psV, \
             nc.sbuf_tensor("wvT_sb", [128, DCH * OSL], BF16,
                            is_transpose_dest=True) as wvT_sb:

            @block.scalar
            def _(sc):
                swait_cw(sc, cw_qkv_jobs)
                for cg in range(0, DCH, 8):
                    sinc(sc.dma_start(
                        out=wvT_sb[:, cg * OSL:(cg + 8) * OSL].rearrange(
                            "p (c o) -> p c o", c=8),
                        in_=wvT_bnc[cg * 128:(cg + 8) * 128, :].rearrange(
                            "(c p) o -> p c o", p=128)), "wqkx", 16)

            emit_wqkT_load(0, None)
            for s in range(NQS):
                for t4 in range(4):
                    emit_ln_tile(4 * s + t4, 4 * s + t4, x_full, xn_tm)
                emit_xnfm_strip(s)
                for gg in range(2):
                    g = 2 * s + gg
                    emit_a1_group(0, g, a1_gctr, psK[:, :], psQ[:, :],
                                  pVap=psV[:, :], wvT_sb=wvT_sb,
                                  head_first_wait=[("p0ev", sval("p0ev"))] if g == 0 else None)
                    a1_gctr += 1

        vbuf = EA(nc.sbuf_tensor("vbuf", [128, T], BF16))
        pt = [EA(nc.sbuf_tensor(f"pt{i}", [128, 512], BF16)) for i in range(3)]
        accD = EA(nc.sbuf_tensor("accD", [128, 512], F32))
        accP = EA(nc.sbuf_tensor("accP", [128, 512], F32))
        linv_r = [EA(nc.sbuf_tensor(f"linv{i}", [1, 512], F32)) for i in range(2)]
        lnl_r = [EA(nc.sbuf_tensor(f"lnl{i}", [1, 512], F32)) for i in range(2)]
        linv_bc = EA(nc.sbuf_tensor("linv_bc", [128, 512], F32))
        o_sb = [EA(nc.sbuf_tensor(f"o_sb{i}", [128, 512], BF16)) for i in range(2)]

        with nc.psum_tensor("psS0", [128, 512], F32) as psS0, \
             nc.psum_tensor("psS1", [128, 512], F32) as psS1, \
             nc.psum_tensor("psO0", [128, 512], F32) as psO0, \
             nc.psum_tensor("psO1", [128, 512], F32) as psO1, \
             nc.psum_tensor("psL0", [1, 512], F32) as psL0, \
             nc.psum_tensor("psL1", [1, 512], F32) as psL1, \
             nc.psum_tensor("psBc", [128, 512], F32) as psBc:
            psS = [psS0, psS1]
            psO = [psO0, psO1]
            psL = [psL0, psL1]
            emit_vh_load(0)
            for s in range(NQS):
                emit_a2_strip(0, s, False, psS, psO, psL, psBc)

            if dbg:
                kq_dbg = nc.declare_dram_parameter("kq_dbg", [2 * OSL, T], BF16,
                                                   isOutput=True)

            def emit_kq_dump(h):
                if not dbg:
                    return

                @block.scalar
                def _(sc):
                    swait(sc, "evA1", sval("evA1"))
                    sinc(sc.dma_start(out=kq_dbg[h * 128:(h + 1) * 128, :],
                                      in_=K_h[:, :]), "kqd", 16)
                    sinc(sc.dma_start(out=kq_dbg[OSL + h * 128:OSL + (h + 1) * 128, :],
                                      in_=Q_h[:, :]), "kqd", 16)

            emit_kq_dump(0)
            for h in range(1, HPC):
                emit_wqkT_load(h, strip_free[-1])
                for g in range(NAG1):
                    emit_a1_group(h, g, a1_gctr, psS0[:, 0:AG1], psS1[:, 0:AG1],
                                  head_first_wait=[("exp", sval("exp"))] if g == 0 else None)
                    a1_gctr += 1
                emit_kq_dump(h)
                emit_vh_load(h)
                for s in range(NQS):
                    emit_a2_strip(h, s, h >= 2, psS, psO, psL, psBc)

        @block.gpsimd
        def _(g):
            swait(g, "ow0", 16 * ((HPC * NQS + 1) // 2))
            swait(g, "ow1", 16 * ((HPC * NQS) // 2))
            sinc(g.collective_compute(
                "AllGather", ALU.bypass, replica_groups=[list(range(NC))],
                ins=[o_loc.ap().opt()], outs=[o_all.ap().opt()]), "cc")

        if dbg:
            o_loc_dbg = nc.declare_dram_parameter("o_loc_dbg", [OSL, T], BF16,
                                                  isOutput=True)

            @block.sync
            def _(sync):
                swait(sync, "cc", 5)
                for hh in range(HPC):
                    sinc(sync.dma_start(
                        out=o_loc_dbg[hh * 128:(hh + 1) * 128, :],
                        in_=o_loc[hh * 128:(hh + 1) * 128, :]), "odbg", 16)
                swait(sync, "odbg", 16 * HPC)

        st_A.close()

        if upto == "A":
            @block.sync
            def _(sync):
                swait(sync, "cc", 5)
                for t in range(TT):
                    sinc(sync.dma_start(out=out[t * 128:(t + 1) * 128, :],
                                        in_=x_my[t * 128:(t + 1) * 128, :]),
                         "outw0", 16)
                swait(sync, "outw0", sval("outw0"))
            return nc, c

        st_O = E(contextlib.ExitStack())
        EO = st_O.enter_context
        bo_bc = EO(nc.sbuf_tensor("bo_bc", [128, D], F32))
        attn_tail0 = [("exp", sval("exp")), ("on", sval("on")),
                      ("lin", sval("lin")), ("bce", sval("bce"))]
        bo_base = {"row": sval("p0row"), "pe": sval("p0pe"), "ev": sval("p0ev")}
        with nc.sbuf_tensor("rowb2", [1, D], F32) as rowb2, \
             nc.psum_tensor("psBo", [128, 512], F32) as psBo:
            @block.scalar
            def _(sc):
                sinc(sc.dma_start(out=rowb2[0:1, :], in_=bo_in[:, :]), "p0row", 16)

            @block.tensor
            def _(t):
                swait(t, "p0row", bo_base["row"] + 16)
                for m in attn_tail0:
                    swaitm(t, m)
                for k in range(DS):
                    swait(t, "p0ev", bo_base["ev"] + k)
                    sinc(t.matmul(psBo[:, :], ones_r[:],
                                  rowb2[0:1, k * 512:(k + 1) * 512],
                                  start=True, stop=True), "p0pe")

            @block.vector
            def _(v):
                for k in range(DS):
                    swait(v, "p0pe", bo_base["pe"] + k + 1)
                    sinc(v.tensor_copy(out=bo_bc[:, k * 512:(k + 1) * 512],
                                       in_=psBo[:, :]), "p0ev")
        bo_ev_done = sval("p0ev")
        att_sb = EO(nc.sbuf_tensor("att_sb", [128, DCH * ROWS], BF16))
        wo_t = [EO(nc.sbuf_tensor(f"wo_t{i}", [128, 512], BF16)) for i in range(2)]
        tmp_o = [EO(nc.sbuf_tensor(f"tmp_o{i}", [128, 512], F32)) for i in range(2)]
        x1sl = [EO(nc.sbuf_tensor(f"x1sl{i}", [128, 512], F32)) for i in range(2)]
        xsl = [EO(nc.sbuf_tensor(f"xsl{i}", [128, 512], F32)) for i in range(2)]

        @block.sync
        def _(sync):
            swait(sync, "cc", 5)
            swait(sync, "cid", 16)
            reg = sync.value_load(cid_sb[0:1, 0:1])
            reg = sync.snap(reg, min_val=0, max_val=NC - 1)
            for b in sync.Switch(reg, NC):
                for rb in range(8):
                    r0 = rb * (D // 8)
                    sync.dma_start(
                        out=o_my[r0:r0 + D // 8, :],
                        in_=o_all[r0:r0 + D // 8, b * ROWS:(b + 1) * ROWS]
                    ).then_inc(sget("omy"), 16)
            sem_n["omy"] = 16 * 8
            swait(sync, "omy", 16 * 8)
            swait(sync, "p0row", sval("p0row"))
            swait(sync, "p0pe", sval("p0pe"))
            for ac in range(DCH):
                sinc(sync.dma_start(out=att_sb[:, ac * ROWS:(ac + 1) * ROWS],
                                    in_=o_my[ac * 128:(ac + 1) * 128, :]), "attl", 16)

        with contextlib.ExitStack() as stO_ps:
            psOt_l = [stO_ps.enter_context(nc.psum_tensor(f"psOt{i}", [128, 512], F32))
                      for i in range(TT)]
            @block.scalar
            def _(sc):
                swait(sc, "cc", 1)
                swait(sc, "p0row", sval("p0row"))
                swait(sc, "p0pe", sval("p0pe"))
                for k in range(DS * DCH):
                    sl = k % 2
                    swait(sc, "opk", max(0, k - 1))
                    ds, ac = divmod(k, DCH)
                    sinc(sc.dma_start(out=wo_t[sl][:, :],
                                      in_=woT_all[ac * 128:(ac + 1) * 128,
                                                  ds * 512:(ds + 1) * 512]),
                         f"wox{sl}", 16)

            for ds in range(DS):
                @block.tensor
                def _(t):
                    for ac in range(DCH):
                        k = ds * DCH + ac
                        sl = k % 2
                        swait(t, f"wox{sl}", 16 * (k // 2 + 1))
                        if ac == 0:
                            swait(t, "attl", 16 * DCH)
                            if ds == 0:
                                swait(t, "p0ev", bo_ev_done)
                            else:
                                swait(t, "oev", 2 * TT * ds)
                        for tt in range(TT):
                            mm = t.matmul(psOt_l[tt][:, :],
                                          att_sb[:, ac * ROWS + tt * 128:
                                                 ac * ROWS + (tt + 1) * 128],
                                          wo_t[sl][:, :],
                                          start=(ac == 0), stop=(ac == DCH - 1))
                        sinc(mm, "opk")

                for tt in range(TT):
                    k2 = ds * TT + tt
                    sl = k2 % 2

                    @block.sync
                    def _(sync):
                        swait(sync, "oev", max(0, 2 * (k2 - 1)))
                        sinc(sync.dma_start(out=xsl[sl][:, :],
                                            in_=x_my[tt * 128:(tt + 1) * 128,
                                                     ds * 512:(ds + 1) * 512]),
                             f"oxl{sl}", 16)

                    @block.vector
                    def _(v):
                        swait(v, "opk", DCH * (ds + 1))
                        swait(v, "oev", max(0, 2 * (k2 - 1)))
                        sinc(v.tensor_add(tmp_o[sl][:, :],
                                          psOt_l[tt][:, :],
                                          bo_bc[:, ds * 512:(ds + 1) * 512]), "oev")
                        swait(v, "oev", sval("oev"))
                        swait(v, f"oxl{sl}", 16 * (k2 // 2 + 1))
                        swait(v, f"x1w{sl}", 16 * (k2 // 2))
                        sinc(v.tensor_add(x1sl[sl][:, :], tmp_o[sl][:, :],
                                          xsl[sl][:, :]), "oev")

                    @block.sync
                    def _(sync):
                        swait(sync, "oev", 2 * (k2 + 1))
                        sinc(sync.dma_start(out=x1_d[tt * 128:(tt + 1) * 128,
                                                     ds * 512:(ds + 1) * 512],
                                            in_=x1sl[sl][:, :]), f"x1w{sl}", 16)

        NT = c["NT"]
        x1_gate = [("x1w0", sval("x1w0")), ("x1w1", sval("x1w1"))]
        for t in range(TT):
            emit_ln_tile(t, NT + t, x1_d, xn1_tm,
                         gate_marks=x1_gate if t == 0 else ())
        ln2_done = (sval("Lxnw0"), sval("Lxnw1"))

        st_O.close()

        if upto == "O":
            @block.sync
            def _(sync):
                swait(sync, "Lxnw0", ln2_done[0])
                swait(sync, "Lxnw1", ln2_done[1])
                for t in range(TT):
                    sinc(sync.dma_start(out=out[t * 128:(t + 1) * 128, :],
                                        in_=x1_d[t * 128:(t + 1) * 128, :]),
                         "outw0", 16)
                swait(sync, "outw0", sval("outw0"))
            return nc, c

        oev_total = sval("oev")
        opk_total = sval("opk")
        st_LN.close()
        st_F = E(contextlib.ExitStack())
        EF = st_F.enter_context
        sd_bc = EF(nc.sbuf_tensor("sd_bc", [128, D], F32))
        sd_base = {"row": sval("p0row"), "pe": sval("p0pe"), "ev": sval("p0ev")}
        with nc.sbuf_tensor("rowb3", [1, D], F32) as rowb3, \
             nc.psum_tensor("psSd", [128, 512], F32) as psSd:
            @block.scalar
            def _(sc):
                sinc(sc.dma_start(out=rowb3[0:1, :], in_=sd_in[:, :]), "p0row", 16)

            @block.tensor
            def _(t):
                swait(t, "p0row", sd_base["row"] + 16)
                swait(t, "oev", oev_total)
                for k in range(DS):
                    swait(t, "p0ev", sd_base["ev"] + k)
                    sinc(t.matmul(psSd[:, :], ones_r[:],
                                  rowb3[0:1, k * 512:(k + 1) * 512],
                                  start=True, stop=True), "p0pe")

            @block.vector
            def _(v):
                for k in range(DS):
                    swait(v, "p0pe", sd_base["pe"] + k + 1)
                    sinc(v.tensor_copy(out=sd_bc[:, k * 512:(k + 1) * 512],
                                       in_=psSd[:, :]), "p0ev")
        sd_ev_done = sval("p0ev")
        xn1_sb = EF(nc.sbuf_tensor("xn1_sb", [128, DCH * TG], BF16,
                                   is_transpose_dest=True))
        hh_sb = EF(nc.sbuf_tensor("hh_sb", [128, ICH * TG], BF16))
        wgs = [EF(nc.sbuf_tensor(f"wgs{i}", [128, DCH * 128], BF16)) for i in range(2)]
        wus = [EF(nc.sbuf_tensor(f"wus{i}", [128, DCH * 128], BF16)) for i in range(2)]
        wds = [EF(nc.sbuf_tensor(f"wds{i}", [128, 512], BF16)) for i in range(2)]
        sgt = [EF(nc.sbuf_tensor(f"sgt{i}", [128, TG], BF16)) for i in range(2)]
        tmp2 = [EF(nc.sbuf_tensor(f"tmp2{i}", [128, 512], F32)) for i in range(2)]
        x1s = [EF(nc.sbuf_tensor(f"x1s{i}", [128, 512], F32)) for i in range(2)]
        x2s = [EF(nc.sbuf_tensor(f"x2s{i}", [128, 512], F32)) for i in range(2)]

        fst = {"ic": 0, "wd": 0, "ev": 0}
        with nc.psum_tensor("psG0", [128, TG], F32) as psG0, \
             nc.psum_tensor("psG1", [128, TG], F32) as psG1, \
             nc.psum_tensor("psU0", [128, TG], F32) as psU0, \
             nc.psum_tensor("psU1", [128, TG], F32) as psU1, \
             nc.psum_tensor("psD", [128, TGT * 512], F32) as psD:
            psG = [psG0, psG1]
            psU = [psU0, psU1]

            for tg in range(NTG):
                @block.scalar
                def _(sc):
                    swait(sc, "Lxnw0", ln2_done[0])
                    swait(sc, "Lxnw1", ln2_done[1])
                    if tg > 0:
                        swait(sc, "pU", sval("pU"))
                    else:
                        swait(sc, "opk", opk_total)
                        swait(sc, "p0row", sval("p0row"))
                        swait(sc, "p0pe", sval("p0pe"))
                    for hc in range(DCH):
                        sinc(sc.dma_start(out=xn1_sb[:, hc * TG:(hc + 1) * TG],
                                          in_=xn1_tm[tg * TG:(tg + 1) * TG,
                                                     hc * 128:(hc + 1) * 128],
                                          transpose=True), "xg1", 16)

                for ic in range(ICH):
                    kk = fst["ic"]
                    fst["ic"] += 1
                    sl = kk % 2

                    @block.sync
                    def _(sync):
                        swait(sync, "pG", max(0, sval("pG") - 1))
                        sinc(sync.dma_start(
                            out=wgs[sl].ap().rearrange("p (c i) -> p c i", c=DCH),
                            in_=wgT_all.ap().rearrange("(c q p) i -> p c q i",
                                                       q=ICHP, p=128)[
                                :, :, ic, :]), f"wgl{sl}", 16)
                        swait(sync, "pU", max(0, sval("pU") - 1))
                        sinc(sync.dma_start(
                            out=wus[sl].ap().rearrange("p (c i) -> p c i", c=DCH),
                            in_=wuT_all.ap().rearrange("(c q p) i -> p c q i",
                                                       q=ICHP, p=128)[
                                :, :, ic, :]), f"wul{sl}", 16)

                    @block.tensor
                    def _(t):
                        swait(t, f"wgl{sl}", 16 * (kk // 2 + 1))
                        swait(t, "xg1", 16 * DCH * (tg + 1))
                        swait(t, "sil", max(0, sval("sil") - 1))
                        if tg == 0 and ic == 0:
                            swait(t, "cc", 4)
                            swait(t, "p0ev", sd_ev_done)
                        for hc in range(DCH):
                            mg = t.matmul(psG[sl][:, :],
                                          wgs[sl][:, hc * 128:(hc + 1) * 128],
                                          xn1_sb[:, hc * TG:(hc + 1) * TG],
                                          start=(hc == 0), stop=(hc == DCH - 1))
                        sinc(mg, "pG")
                        swait(t, f"wul{sl}", 16 * (kk // 2 + 1))
                        swait(t, "hhe", max(0, sval("hhe") - 1))
                        for hc in range(DCH):
                            mu = t.matmul(psU[sl][:, :],
                                          wus[sl][:, hc * 128:(hc + 1) * 128],
                                          xn1_sb[:, hc * TG:(hc + 1) * TG],
                                          start=(hc == 0), stop=(hc == DCH - 1))
                        sinc(mu, "pU")

                    @block.scalar
                    def _(sc):
                        swait(sc, "pG", sval("pG"))
                        swait(sc, "hhe", max(0, sval("hhe") - 1))
                        sinc(sc.activation(out=sgt[sl][:, :], in_=psG[sl][:, :],
                                           func=AF.Silu,
                                           scale=sgc[:, ic:ic + 1]), "sil")

                    @block.vector
                    def _(v):
                        swait(v, "pU", sval("pU"))
                        swait(v, "sil", sval("sil"))
                        if tg > 0 and ic == 0:
                            swait(v, "pD", sval("pD"))
                        sinc(v.scalar_tensor_tensor(
                            out=hh_sb[:, ic * TG:(ic + 1) * TG],
                            in0=psU[sl][:, :], scalar=suc[:, ic:ic + 1],
                            in1=sgt[sl][:, :], op0=ALU.mult, op1=ALU.mult), "hhe")

                for hs in range(DS):
                    for ic in range(ICH):
                        kd = fst["wd"]
                        fst["wd"] += 1
                        sl = kd % 2

                        @block.sync
                        def _(sync):
                            swait(sync, "pD", max(0, sval("pD") - 1))
                            sinc(sync.dma_start(out=wds[sl][:, :],
                                                in_=wdT_all[ic * 128:(ic + 1) * 128,
                                                            hs * 512:(hs + 1) * 512]),
                                 f"wdl{sl}", 16)

                        @block.tensor
                        def _(t):
                            swait(t, f"wdl{sl}", 16 * (kd // 2 + 1))
                            if ic == 0:
                                swait(t, "hhe", ICH * (tg + 1))
                                swait(t, "x2e", sval("x2e"))
                            for tt in range(TGT):
                                md = t.matmul(psD[:, tt * 512:tt * 512 + 512],
                                              hh_sb[:, ic * TG + tt * 128:
                                                    ic * TG + (tt + 1) * 128],
                                              wds[sl][:, :],
                                              start=(ic == 0), stop=(ic == ICH - 1))
                            sinc(md, "pD")

                    for tt in range(TGT):
                        ke = fst["ev"]
                        fst["ev"] += 1
                        sl = ke % 2
                        row = tg * TG + tt * 128

                        @block.sync
                        def _(sync):
                            swait(sync, "x2e", max(0, 2 * (ke - 1)))
                            swait(sync, "x1w0", sval("x1w0"))
                            swait(sync, "x1w1", sval("x1w1"))
                            sinc(sync.dma_start(out=x1s[sl][:, :],
                                                in_=x1_d[row:row + 128,
                                                         hs * 512:(hs + 1) * 512]),
                                 f"fx1{sl}", 16)

                        @block.vector
                        def _(v):
                            swait(v, "pD", sval("pD"))
                            swait(v, "x2e", max(0, 2 * (ke - 1)))
                            sinc(v.tensor_mul(tmp2[sl][:, :],
                                              psD[:, tt * 512:tt * 512 + 512],
                                              sd_bc[:, hs * 512:(hs + 1) * 512]),
                                 "x2e")
                            swait(v, "x2e", sval("x2e"))
                            swait(v, f"fx1{sl}", 16 * (ke // 2 + 1))
                            swait(v, f"outw{sl}", 16 * (ke // 2))
                            sinc(v.tensor_add(x2s[sl][:, :], tmp2[sl][:, :],
                                              x1s[sl][:, :]), "x2e")

                        @block.sync
                        def _(sync):
                            swait(sync, "x2e", 2 * (ke + 1))
                            sinc(sync.dma_start(out=out[row:row + 128,
                                                        hs * 512:(hs + 1) * 512],
                                                in_=x2s[sl][:, :]), f"outw{sl}", 16)

        @block.sync
        def _(sync):
            swait(sync, "outw0", sval("outw0"))
            swait(sync, "outw1", sval("outw1"))

        st_F.close()

    return nc, c


def _shard_inputs(c, inputs):
    T, D, I = c["T"], c["D"], c["I"]
    ROWS, OSL, DPC = c["ROWS"], c["OSL"], c["DPC"]
    IPC, IPAD, HCC, ICHP = c["IPC"], c["IPAD"], c["HCC"], c["ICHP"]
    x = np.ascontiguousarray(np.asarray(inputs["L0_attn_x"], np.float32).reshape(T, D))
    wq = np.asarray(inputs["L0_attn_Wq"], np.float32)
    wk = np.asarray(inputs["L0_attn_Wk"], np.float32)
    wv = np.asarray(inputs["L0_attn_Wv"], np.float32)
    bq = np.asarray(inputs["L0_attn_bq"], np.float32).reshape(1, D)
    bk = np.asarray(inputs["L0_attn_bk"], np.float32).reshape(1, D)
    bv = np.asarray(inputs["L0_attn_bv"], np.float32).reshape(1, D)
    wo = np.asarray(inputs["L0_attn_Wo"], np.float32)
    bo = np.asarray(inputs["L0_attn_bo"], np.float32).reshape(1, D)
    wg = np.asarray(inputs["L0_ffn_w_gate"], np.int32)
    wu = np.asarray(inputs["L0_ffn_w_up"], np.int32)
    wd = np.asarray(inputs["L0_ffn_w_down"], np.int32)
    sg = np.zeros((1, IPAD), np.float32)
    sg[0, :I] = np.asarray(inputs["L0_ffn_s_gate"], np.float32)
    su = np.zeros((1, IPAD), np.float32)
    su[0, :I] = np.asarray(inputs["L0_ffn_s_up"], np.float32)
    sd = np.asarray(inputs["L0_ffn_s_down"], np.float32).reshape(1, D)

    def blk_gate(w, q):
        a = np.zeros((IPAD, DPC), np.int32)
        a[:I, :] = w[:, q * DPC:(q + 1) * DPC]
        a = a.reshape(ICHP, 128, HCC, 128)
        a = a.transpose(2, 0, 3, 1)
        return np.ascontiguousarray(a.reshape(HCC * ICHP * 128, 128))

    def wdT_slice(q):
        a = np.zeros((IPC * 128, D), np.int32)
        lo = q * IPC * 128
        hi = min(I, (q + 1) * IPC * 128)
        if hi > lo:
            a[0:hi - lo, :] = wd[:, lo:hi].T
        return np.ascontiguousarray(a)

    import ml_dtypes
    p_ = np.arange(128)[:, None]
    f_ = np.arange(512)[None, :]
    mask = np.concatenate(
        [(f_ - p_ - 128 * j >= 0) for j in range(4)],
        axis=1).astype(ml_dtypes.bfloat16)
    in_maps = []
    for q in range(NC):
        in_maps.append({
            "x_full": x,
            "x_my": np.ascontiguousarray(x[q * ROWS:(q + 1) * ROWS, :]),
            "wqT_sl": np.ascontiguousarray(wq[q * OSL:(q + 1) * OSL, :].T),
            "wkT_sl": np.ascontiguousarray(wk[q * OSL:(q + 1) * OSL, :].T),
            "wvT_sl": np.ascontiguousarray(wv[q * OSL:(q + 1) * OSL, :].T),
            "bq_sl": np.ascontiguousarray(bq[:, q * OSL:(q + 1) * OSL]),
            "bk_sl": np.ascontiguousarray(bk[:, q * OSL:(q + 1) * OSL]),
            "bv_sl": np.ascontiguousarray(bv[:, q * OSL:(q + 1) * OSL]),
            "woT_sl": np.ascontiguousarray(wo[:, q * DPC:(q + 1) * DPC].T),
            "bo_in": bo,
            "wgT_blk": blk_gate(wg, q),
            "wuT_blk": blk_gate(wu, q),
            "wdT_sl": wdT_slice(q),
            "sg_in": sg,
            "su_in": su,
            "sd_in": sd,
            "cid_in": np.array([[q]], np.int32),
            "mask_in": mask,
        })
    return in_maps


def kernel(**inputs):
    T, D, HPC, I = 8192, 4096, 4, 11008
    nc, c = build(T, D, HPC, I)
    in_maps = _shard_inputs(c, inputs)
    res = run_bass_kernel_spmd(nc, in_maps, core_ids=list(range(NC)))
    shards = [np.asarray(res.results[q]["out"]) for q in range(NC)]
    return np.concatenate(shards, axis=0).reshape(1, T, D).astype(np.float32)


# version 08 (final)
# speedup vs baseline: 1.0758x; 1.0190x over previous
import numpy as np
import contextlib
import concourse.bass as bass
import concourse.mybir as mybir
from concourse.bass_utils import run_bass_kernel_spmd

F32 = mybir.dt.float32
FP8 = mybir.dt.float8e4
BF16 = mybir.dt.bfloat16
I32 = mybir.dt.int32
AF = mybir.ActivationFunctionType
ALU = mybir.AluOpType

NC = 8


def derived(T, D, HPC, I):
    c = dict(T=T, D=D, HPC=HPC, I=I)
    c["ROWS"] = T // NC
    c["NT"] = T // 128
    c["DCH"] = D // 128
    c["DS"] = D // 512
    c["DPC"] = D // NC
    c["HCC"] = c["DPC"] // 128
    c["ICH"] = I // 128
    c["IPC"] = -(-c["ICH"] // NC)
    c["IPAD"] = c["IPC"] * 128 * NC
    c["NIB"] = -(-I // 512)
    c["ICHP"] = c["IPC"] * NC
    c["QS"] = 512
    c["NQS"] = T // 512
    c["QSC"] = 4
    c["AG1"] = 256
    c["NAG1"] = T // 256
    c["AGD"] = 2
    c["TT"] = c["ROWS"] // 128
    c["TG"] = min(512, c["ROWS"])
    c["NTG"] = c["ROWS"] // c["TG"]
    c["TGT"] = c["TG"] // 128
    c["OSL"] = HPC * 128
    assert T % 512 == 0 and T % (NC * 128) == 0 and D % 512 == 0 and I % 128 == 0
    assert c["TT"] <= 8
    return c


def build(T, D, HPC, I, dbg=False, upto=None):
    c = derived(T, D, HPC, I)
    ROWS, DCH, DS, DPC, HCC = c["ROWS"], c["DCH"], c["DS"], c["DPC"], c["HCC"]
    ICH, IPC, IPAD, NIB = c["ICH"], c["IPC"], c["IPAD"], c["NIB"]
    ICHP = c["ICHP"]
    NQS, QSC, AG1, NAG1, AGD = c["NQS"], c["QSC"], c["AG1"], c["NAG1"], c["AGD"]
    TT, TG, NTG, TGT, OSL = c["TT"], c["TG"], c["NTG"], c["TGT"], c["OSL"]
    INV_D = 1.0 / D
    QSCALE = float(1.0 / np.sqrt(128.0))
    EPS = 1e-5

    nc = bass.Bass("TRN2", target_bir_lowering=False, debug=False)

    x_full = nc.declare_dram_parameter("x_full", [T, D], F32, isOutput=False)
    x_my = nc.declare_dram_parameter("x_my", [ROWS, D], F32, isOutput=False)
    wqT_sl = nc.declare_dram_parameter("wqT_sl", [D, OSL], F32, isOutput=False)
    wkT_sl = nc.declare_dram_parameter("wkT_sl", [D, OSL], F32, isOutput=False)
    wvT_sl = nc.declare_dram_parameter("wvT_sl", [D, OSL], F32, isOutput=False)
    bq_sl = nc.declare_dram_parameter("bq_sl", [1, OSL], F32, isOutput=False)
    bk_sl = nc.declare_dram_parameter("bk_sl", [1, OSL], F32, isOutput=False)
    bv_sl = nc.declare_dram_parameter("bv_sl", [1, OSL], F32, isOutput=False)
    woT_sl = nc.declare_dram_parameter("woT_sl", [DPC, D], F32, isOutput=False)
    bo_in = nc.declare_dram_parameter("bo_in", [1, D], F32, isOutput=False)
    wgT_blk = nc.declare_dram_parameter("wgT_blk", [HCC * ICHP * 128, 128], I32,
                                        isOutput=False)
    wuT_blk = nc.declare_dram_parameter("wuT_blk", [HCC * ICHP * 128, 128], I32,
                                        isOutput=False)
    wdT_sl = nc.declare_dram_parameter("wdT_sl", [IPC * 128, D], I32, isOutput=False)
    sg_in = nc.declare_dram_parameter("sg_in", [1, IPAD], F32, isOutput=False)
    su_in = nc.declare_dram_parameter("su_in", [1, IPAD], F32, isOutput=False)
    sd_in = nc.declare_dram_parameter("sd_in", [1, D], F32, isOutput=False)
    cid_in = nc.declare_dram_parameter("cid_in", [1, 1], I32, isOutput=False)
    mask_in = nc.declare_dram_parameter("mask_in", [128, 4 * 512], BF16, isOutput=False)
    out = nc.declare_dram_parameter("out", [ROWS, D], F32, isOutput=True)

    def idram(name, shape, dt_):
        if dbg:
            return nc.declare_dram_parameter(name, shape, dt_, isOutput=True)
        return nc.dram_tensor(name, shape, dt_)

    xn_tm = idram("xn_tm", [T, D], BF16)
    xn_fm = idram("xn_fm", [D, T], BF16)
    wqT_bnc = nc.dram_tensor("wqT_bnc", [D, OSL], BF16)
    wkT_bnc = nc.dram_tensor("wkT_bnc", [D, OSL], BF16)
    wvT_bnc = nc.dram_tensor("wvT_bnc", [D, OSL], BF16)
    woT_loc = nc.dram_tensor("woT_loc", [DPC, D], BF16)
    woT_all = nc.dram_tensor("woT_all", [D, D], BF16, addr_space="Shared")
    wgT_loc = nc.dram_tensor("wgT_loc", [HCC * ICHP * 128, 128], BF16)
    wuT_loc = nc.dram_tensor("wuT_loc", [HCC * ICHP * 128, 128], BF16)
    wdT_loc = nc.dram_tensor("wdT_loc", [IPC * 128, D], BF16)
    wgT_all = nc.dram_tensor("wgT_all", [DCH * ICHP * 128, 128], BF16, addr_space="Shared")
    wuT_all = nc.dram_tensor("wuT_all", [DCH * ICHP * 128, 128], BF16, addr_space="Shared")
    wdT_all = nc.dram_tensor("wdT_all", [IPAD, D], BF16, addr_space="Shared")
    v_dram = idram("v_dram", [HPC * T, 128], BF16)
    o_loc = nc.dram_tensor("o_loc", [OSL, T], BF16)
    o_all = nc.dram_tensor("o_all", [D, T], BF16, addr_space="Shared")
    o_my = idram("o_my", [D, ROWS], BF16)
    x1_d = idram("x1_d", [ROWS, D], F32)
    xn1_tm = idram("xn1_tm", [ROWS, D], BF16)

    sem_n = {}
    sem_h = {}

    with contextlib.ExitStack() as st0:
        E = st0.enter_context
        block = E(nc.Block())

        def sget(name):
            if name not in sem_h:
                sem_h[name] = E(nc.semaphore(name))
                sem_n[name] = 0
            return sem_h[name]

        def sinc(instr, name, k=1):
            instr.then_inc(sget(name), k)
            sem_n[name] += k
            return sem_n[name]

        def sval(name):
            sget(name)
            return sem_n[name]

        def swait(eng, name, v):
            if v > 0:
                eng.wait_ge(sget(name), v)

        def swaitm(eng, mark):
            if mark is not None:
                swait(eng, mark[0], mark[1])

        ones_r = E(nc.sbuf_tensor("ones_r", [1, 128], F32))
        ones_kb = E(nc.sbuf_tensor("ones_kb", [1, 128], BF16))
        bvr_b = E(nc.sbuf_tensor("bvr_b", [1, OSL], BF16))
        ones_c = E(nc.sbuf_tensor("ones_c", [128, 1], F32))
        ones_s = E(nc.sbuf_tensor("ones_s", [1, 1], F32))
        bqc = E(nc.sbuf_tensor("bqc", [128, HPC], F32))
        bkc = E(nc.sbuf_tensor("bkc", [128, HPC], F32))
        sgc = E(nc.sbuf_tensor("sgc", [128, ICH], F32))
        suc = E(nc.sbuf_tensor("suc", [128, ICH], F32))
        masks = E(nc.sbuf_tensor("masks", [128, QSC * 512], BF16))
        cid_sb = E(nc.sbuf_tensor("cid_sb", [1, 1], I32))

        @block.sync
        def _(sync):
            sinc(sync.dma_start(out=cid_sb[:], in_=cid_in[:, :]), "cid", 16)

        @block.sync
        def _(sync):
            sinc(sync.dma_start(out=masks[:], in_=mask_in[:, :]), "mskl", 16)

        @block.gpsimd
        def _(g):
            sinc(g.memset(ones_r[:], 1.0), "p0g")
            sinc(g.memset(ones_c[:], 1.0), "p0g")
            sinc(g.memset(ones_s[:], 1.0), "p0g")
            sinc(g.memset(ones_kb[:], 1.0), "p0g")
            sinc(g.dma_start(out=bvr_b[:], in_=bv_sl[:, :]), "clA", 16)
        p0g_all = sval("p0g")
        bv_mark = ("clA", 32)

        jobs = [
            (bq_sl[:, :], OSL, "col", bqc),
            (bk_sl[:, :], OSL, "col", bkc),
            (sg_in[0:1, 0:I], I, "col", sgc),
            (su_in[0:1, 0:I], I, "col", suc),
        ]
        maxw = max(j[1] for j in jobs)
        chunks_before = [0]
        for (_src, w, kind, _d) in jobs:
            chunks_before.append(chunks_before[-1] + (w // 128 if kind == "col" else -(-w // 512)))

        with nc.sbuf_tensor("rowbuf", [1, maxw], F32) as rowbuf, \
             nc.psum_tensor("pbc", [128, 512], F32) as pbc, \
             nc.psum_tensor("pbc2", [128, 512], F32) as pbc2:
            pb = [pbc, pbc2]

            @block.scalar
            def _(sc):
                for ji, (src, w, kind, dest) in enumerate(jobs):
                    swait(sc, "p0pe", chunks_before[ji])
                    sinc(sc.dma_start(out=rowbuf[0:1, 0:w], in_=src), "p0row", 16)

            @block.tensor
            def _(t):
                kg = 0
                swait(t, "p0g", 3)
                for ji, (src, w, kind, dest) in enumerate(jobs):
                    nch = w // 128 if kind == "col" else -(-w // 512)
                    swait(t, "p0row", 16 * (ji + 1))
                    for k in range(nch):
                        swait(t, "p0ev", max(0, kg - 1))
                        if kind == "col":
                            mm = t.matmul(pb[kg % 2][:, 0:1],
                                          rowbuf[0:1, k * 128:(k + 1) * 128],
                                          ones_s[:], start=True, stop=True)
                        else:
                            lo, hi = k * 512, min(w, k * 512 + 512)
                            mm = t.matmul(pb[kg % 2][:, 0:hi - lo], ones_r[:],
                                          rowbuf[0:1, lo:hi], start=True, stop=True)
                        sinc(mm, "p0pe")
                        kg += 1

            @block.vector
            def _(v):
                kg = 0
                for ji, (src, w, kind, dest) in enumerate(jobs):
                    nch = w // 128 if kind == "col" else -(-w // 512)
                    for k in range(nch):
                        swait(v, "p0pe", kg + 1)
                        if kind == "col":
                            cp = v.tensor_copy(out=dest[:, k:k + 1],
                                               in_=pb[kg % 2][:, 0:1])
                        else:
                            lo, hi = k * 512, min(w, k * 512 + 512)
                            cp = v.tensor_copy(out=dest[:, lo:hi],
                                               in_=pb[kg % 2][:, 0:hi - lo])
                        sinc(cp, "p0ev")
                        kg += 1

        def flat_aps(t, total_elems, nchunks):
            per = total_elems // 128 // nchunks
            return [t.ap().rearrange("a i -> (a i)").rearrange("(p x) -> p x", p=128)[
                :, k * per:(k + 1) * per] for k in range(nchunks)]

        cast_jobs = []
        for t_in, t_out_, nch in ((woT_sl, woT_loc, 8),
                                  (wqT_sl, wqT_bnc, 8), (wkT_sl, wkT_bnc, 8),
                                  (wvT_sl, wvT_bnc, 8)):
            n = 1
            for d_ in t_in.shape:
                n *= d_
            ins_l = flat_aps(t_in, n, nch)
            outs_l = flat_aps(t_out_, n, nch)
            cast_jobs += list(zip(ins_l, outs_l, [n // 128 // nch] * nch))
        cw_qkv_jobs = len(cast_jobs)
        for t_in, t_out_, nch in ((wgT_blk, wgT_loc, 16), (wuT_blk, wuT_loc, 16),
                                  (wdT_sl, wdT_loc, 16)):
            n = 1
            for d_ in t_in.shape:
                n *= d_
            ins_l = flat_aps(t_in, n, nch)
            outs_l = flat_aps(t_out_, n, nch)
            cast_jobs += list(zip(ins_l, outs_l, [n // 128 // nch] * nch))
        cw_wg_jobs = cw_qkv_jobs + 16
        cw_wu_jobs = cw_qkv_jobs + 32
        maxper = max(w for (_i, _o, w) in cast_jobs)

        cnvb = [E(nc.sbuf_tensor(f"cnv{i}", [128, maxper], BF16, side="right"))
                for i in range(2)]

        def cnt_ab(k):
            return (k + 1) // 2, k // 2

        ag_at = {cw_qkv_jobs: ("woT", None),
                 cw_wg_jobs: ("wg", None), cw_wu_jobs: ("wu", None),
                 len(cast_jobs): ("wd", None)}
        ag_tensors = {"woT": (woT_loc, woT_all), "wg": (wgT_loc, wgT_all),
                      "wu": (wuT_loc, wuT_all), "wd": (wdT_loc, wdT_all)}

        @block.gpsimd
        def _(g):
            for kk, (iap, oap, w) in enumerate(cast_jobs):
                sA = "cwA" if kk % 2 == 0 else "cwB"
                sL = "clA" if kk % 2 == 0 else "clB"
                nprev = (kk - 2) // 2 + 1 if kk >= 2 else 0
                swait(g, sA, 16 * nprev)
                sinc(g.dma_start(out=cnvb[kk % 2][:, 0:w], in_=iap), sL, 16)
                swait(g, sL, sval(sL))
                sinc(g.dma_start(out=oap, in_=cnvb[kk % 2][:, 0:w]), sA, 16)
                if kk + 1 in ag_at:
                    a_n, b_n = cnt_ab(kk + 1)
                    swait(g, "cwA", 16 * a_n)
                    swait(g, "cwB", 16 * b_n)
                    loc_t, all_t = ag_tensors[ag_at[kk + 1][0]]
                    sinc(g.collective_compute(
                        "AllGather", ALU.bypass,
                        replica_groups=[list(range(NC))],
                        ins=[loc_t.ap().opt()], outs=[all_t.ap().opt()]), "cc")

        cw_qkv = cnt_ab(cw_qkv_jobs)

        def swait_cw(eng, njobs):
            a_n, b_n = cnt_ab(njobs)
            swait(eng, "cwA", 16 * a_n)
            swait(eng, "cwB", 16 * b_n)

        st_LN = E(contextlib.ExitStack())
        EL = st_LN.enter_context
        lb = {
            "xb": [EL(nc.sbuf_tensor(f"xb{i}", [128, D], F32, side="right")) for i in range(2)],
            "junk_d": [EL(nc.sbuf_tensor(f"junkd{i}", [128, D], FP8, side="right")) for i in range(2)],
            "junk_a": [EL(nc.sbuf_tensor(f"junka{i}", [128, D], FP8, side="right")) for i in range(2)],
            "sum": [EL(nc.sbuf_tensor(f"lsum{i}", [128, 1], F32, side="right")) for i in range(2)],
            "sq": [EL(nc.sbuf_tensor(f"lsq{i}", [128, 1], F32, side="right")) for i in range(2)],
            "m2": [EL(nc.sbuf_tensor(f"lm2{i}", [128, 1], F32, side="right")) for i in range(2)],
            "var": [EL(nc.sbuf_tensor(f"lvar{i}", [128, 1], F32, side="right")) for i in range(2)],
            "lnv": [EL(nc.sbuf_tensor(f"llnv{i}", [128, 1], F32, side="right")) for i in range(2)],
            "var2": [EL(nc.sbuf_tensor(f"lvar2{i}", [128, 1], F32, side="right")) for i in range(2)],
            "rstd": [EL(nc.sbuf_tensor(f"lrstd{i}", [128, 1], F32, side="right")) for i in range(2)],
            "negm": [EL(nc.sbuf_tensor(f"lnegm{i}", [128, 1], F32, side="right")) for i in range(2)],
            "xnb": [EL(nc.sbuf_tensor(f"lxnb{i}", [128, D], BF16, side="right")) for i in range(2)],
        }
        xfg = [EL(nc.sbuf_tensor(f"xfg{i}", [128, 4 * 512], BF16,
                                 is_transpose_dest=True, side="right"))
               for i in range(2)]
        xfw8 = [xfg[i // 4][:, (i % 4) * 512:(i % 4 + 1) * 512] for i in range(8)]

        def emit_ln_tile(trow, t, src, dst, gate_marks=()):
            pfx = "L"
            q = t % 2

            @block.sync
            def _(sync):
                swait(sync, pfx + "an", t - 1)
                for m in gate_marks:
                    swaitm(sync, m)
                sinc(sync.dma_start(out=lb["xb"][q][:],
                                    in_=src[trow * 128:(trow + 1) * 128, :]),
                     pfx + f"lx{q}", 16)

            @block.vector
            def _(v):
                swait(v, pfx + f"lx{q}", 16 * (t // 2 + 1))
                swait(v, pfx + "d1", t - 1)
                sinc(v.tensor_scalar(out=lb["junk_d"][q][:], in0=lb["xb"][q][:],
                                     scalar1=1.0, scalar2=0.0, op0=ALU.mult,
                                     op1=ALU.add,
                                     accum_out=lb["sum"][q][:]), pfx + "d1")

            @block.scalar
            def _(sc):
                swait(sc, pfx + f"lx{q}", 16 * (t // 2 + 1))
                swait(sc, pfx + "a2", t - 1)
                sinc(sc.activation(out=lb["junk_a"][q][:], in_=lb["xb"][q][:],
                                   func=AF.Square, accum_out=lb["sq"][q][:]),
                     pfx + "a2")

            @block.vector
            def _(v):
                swait(v, pfx + "d1", t + 1)
                swait(v, pfx + "dv1", t - 1)
                sinc(v.tensor_mul(lb["m2"][q][:], lb["sum"][q][:], lb["sum"][q][:]),
                     pfx + "dv1")
                swait(v, pfx + "dv1", t + 1)
                swait(v, pfx + "a2", t + 1)
                swait(v, pfx + "dv2", t - 1)
                sinc(v.scalar_tensor_tensor(out=lb["var"][q][:], in0=lb["m2"][q][:],
                                            scalar=-INV_D, in1=lb["sq"][q][:],
                                            op0=ALU.mult, op1=ALU.add), pfx + "dv2")
                swait(v, pfx + "dv2", t + 1)
                swait(v, pfx + "dv2b", t - 1)
                sinc(v.tensor_scalar(out=lb["var2"][q][:], in0=lb["var"][q][:],
                                     scalar1=INV_D, scalar2=EPS,
                                     op0=ALU.mult, op1=ALU.add), pfx + "dv2b")

            @block.scalar
            def _(sc):
                swait(sc, pfx + "dv2b", t + 1)
                swait(sc, pfx + "aln", t - 1)
                sinc(sc.activation(out=lb["lnv"][q][:], in_=lb["var2"][q][:],
                                   func=AF.Ln), pfx + "aln")
                swait(sc, pfx + "aln", t + 1)
                swait(sc, pfx + "aex", t - 1)
                sinc(sc.activation(out=lb["rstd"][q][:], in_=lb["lnv"][q][:],
                                   func=AF.Exp, scale=-0.5), pfx + "aex")

            @block.vector
            def _(v):
                swait(v, pfx + "aex", t + 1)
                swait(v, pfx + "dv3", t - 1)
                sinc(v.tensor_scalar(out=lb["negm"][q][:], in0=lb["sum"][q][:],
                                     scalar1=lb["rstd"][q][:], scalar2=-INV_D,
                                     op0=ALU.mult, op1=ALU.mult), pfx + "dv3")

            @block.scalar
            def _(sc):
                swait(sc, pfx + "dv3", t + 1)
                if t >= 2:
                    swait(sc, pfx + f"xnw{q}", 16 * ((t - 2) // 2 + 1))
                sinc(sc.activation(out=lb["xnb"][q][:], in_=lb["xb"][q][:],
                                   func=AF.Identity, scale=lb["rstd"][q][:],
                                   bias=lb["negm"][q][:]), pfx + "an")

            @block.sync
            def _(sync):
                swait(sync, pfx + "an", t + 1)
                sinc(sync.dma_start(out=dst[trow * 128:(trow + 1) * 128, :],
                                    in_=lb["xnb"][q][:]), pfx + f"xnw{q}", 16)

        xf_last = {0: 0, 1: 0}
        xf_k = [0]

        def emit_xnfm_strip(s):
            @block.sync
            def _(sync):
                swait(sync, "Lxnw0", 16 * (2 * s + 2))
                swait(sync, "Lxnw1", 16 * (2 * s + 2))
                for dcg in range(0, DCH, 4):
                    grp = (xf_k[0] // 4) % 2
                    for j in range(4):
                        sl4 = grp * 4 + j
                        xf_k[0] += 1
                        swait(sync, f"xfw{grp}", xf_last[grp])
                        sinc(sync.dma_start(
                            out=xfw8[sl4][:, :],
                            in_=xn_tm[s * 512:(s + 1) * 512,
                                      (dcg + j) * 128:(dcg + j + 1) * 128],
                            transpose=True), f"xfl{grp}", 16)
                    swait(sync, f"xfl{grp}", sval(f"xfl{grp}"))
                    xf_last[grp] = sinc(sync.dma_start(
                        out=xn_fm[dcg * 128:(dcg + 4) * 128,
                                  s * 512:(s + 1) * 512].rearrange(
                            "(c p) t -> p c t", p=128),
                        in_=xfg[grp].ap().rearrange("p (c t) -> p c t", c=4)),
                        f"xfw{grp}", 16)

        st_A = E(contextlib.ExitStack())
        EA = st_A.enter_context
        stp = [EA(nc.sbuf_tensor(f"stp{i}", [128, DCH * AG1], BF16)) for i in range(2)]
        K_h = EA(nc.sbuf_tensor("K_h", [128, T], BF16))
        Q_h = EA(nc.sbuf_tensor("Q_h", [128, T], BF16))
        wqT_sb = EA(nc.sbuf_tensor("wqT_sb", [128, DCH * 128], BF16,
                                   is_transpose_dest=True))
        wkT_sb = EA(nc.sbuf_tensor("wkT_sb", [128, DCH * 128], BF16,
                                   is_transpose_dest=True))
        v_sb = [EA(nc.sbuf_tensor(f"v_sb{i}", [128, OSL], BF16)) for i in range(2)]

        strip_free = []
        a1v = {"v": 0}
        a2st = {"s": 0, "chunk": 0, "ptmark": [], "lm_mark": [], "on_mark": [],
                "av_mark": [], "lacp_prev": None, "psl_free": [], "vbias": None}

        if dbg:
            bnc_dbg = nc.declare_dram_parameter("bnc_dbg", [128, D], BF16,
                                                isOutput=True)

        def emit_wqkT_load(h, prev_head_mark):
            @block.scalar
            def _(sc):
                swait_cw(sc, cw_qkv_jobs)
                if prev_head_mark is not None:
                    swaitm(sc, prev_head_mark)
                for wsrc, dst in ((wqT_bnc, wqT_sb), (wkT_bnc, wkT_sb)):
                    for cg in range(0, DCH, 8):
                        sinc(sc.dma_start(
                            out=dst[:, cg * 128:(cg + 8) * 128].rearrange(
                                "p (c o) -> p c o", c=8),
                            in_=wsrc[cg * 128:(cg + 8) * 128,
                                     h * 128:(h + 1) * 128].rearrange(
                                "(c p) o -> p c o", p=128)), "wqkx", 16)

        a1ev = {"K0": 0, "K1": 0, "Q0": 0, "Q1": 0,
                "V0": 0, "V1": 0, "V2": 0, "V3": 0}
        a1_pend = []

        def flush_a1_evacs():
            while a1_pend:
                a1_pend.pop(0)()

        def emit_a1_group(h, g, gctr, wvT_sb=None, head_first_wait=None):
            sl = gctr % 2
            kb = psS0[:, sl * AG1:(sl + 1) * AG1]
            qb = psS1[:, sl * AG1:(sl + 1) * AG1]

            @block.sync
            def _(sc):
                s_idx = g // 2
                ngrp = (s_idx + 1) * (DCH // 4)
                swait(sc, "xfw0", 16 * ((ngrp + 1) // 2))
                swait(sc, "xfw1", 16 * (ngrp // 2))
                if gctr >= 2:
                    swaitm(sc, strip_free[gctr - 2])
                for cg in range(0, DCH, 8):
                    sinc(sc.dma_start(
                        out=stp[sl][:, cg * AG1:(cg + 8) * AG1].rearrange(
                            "p (c t) -> p c t", c=8),
                        in_=xn_fm[cg * 128:(cg + 8) * 128, :].rearrange(
                            "(c p) t -> p c t", p=128)[
                            :, :, g * AG1:(g + 1) * AG1]), f"stl{sl}", 16)

            @block.tensor
            def _(t):
                swait(t, f"stl{sl}", 16 * (DCH // 8) * (gctr // 2 + 1))
                if g == 0:
                    swait(t, "wqkx", sval("wqkx"))
                    if head_first_wait is not None:
                        for m in head_first_wait:
                            swaitm(t, m)
                swait(t, "evA1", a1ev[f"K{sl}"])
                for dc in range(DCH):
                    mmk = t.matmul(kb, wkT_sb[:, dc * 128:(dc + 1) * 128],
                                   stp[sl][:, dc * AG1:(dc + 1) * AG1],
                                   start=(dc == 0), stop=(dc == DCH - 1))
                sinc(mmk, "pA1")
            kmark = ("pA1", sval("pA1"))

            @block.tensor
            def _(t):
                swait(t, "evA1", a1ev[f"Q{sl}"])
                for dc in range(DCH):
                    mmq = t.matmul(qb, wqT_sb[:, dc * 128:(dc + 1) * 128],
                                   stp[sl][:, dc * AG1:(dc + 1) * AG1],
                                   start=(dc == 0), stop=(dc == DCH - 1))
                sinc(mmq, "pA1")
            qmark = ("pA1", sval("pA1"))

            vjobs = []
            if h == 0:
                vbanks = [psV[:, 0:OSL], psO0[:, 0:OSL],
                          psO1[:, 0:OSL], psBc[:, 0:OSL]]
                for tt in range(AGD):
                    vctr = a1v["v"]
                    a1v["v"] += 1
                    vb = vbanks[vctr % 4]

                    @block.tensor
                    def _(t, vb=vb, vctr=vctr, tt=tt):
                        swait(t, "evA1", a1ev[f"V{vctr % 4}"])
                        if g == 0 and tt == 0:
                            swait(t, "wqkx", sval("wqkx"))
                            swait(t, "p0g", p0g_all)
                            swaitm(t, bv_mark)
                        for dc in range(DCH):
                            t.matmul(vb,
                                     stp[sl][:, dc * AG1 + tt * 128:
                                             dc * AG1 + (tt + 1) * 128],
                                     wvT_sb[:, dc * OSL:(dc + 1) * OSL],
                                     start=(dc == 0), stop=False)
                        mmv = t.matmul(vb, ones_kb[:], bvr_b[:],
                                       start=False, stop=True)
                        sinc(mmv, "pA1")
                    vjobs.append((vctr, tt, vb, ("pA1", sval("pA1"))))

            strip_free.append(("pA1", sval("pA1")))

            def evacs():
                @block.scalar
                def _(sc):
                    swaitm(sc, kmark)
                    a1ev[f"K{sl}"] = sinc(
                        sc.activation(out=K_h[:, g * AG1:(g + 1) * AG1], in_=kb,
                                      func=AF.Identity, bias=bkc[:, h:h + 1]),
                        "evA1")
                    swaitm(sc, qmark)
                    a1ev[f"Q{sl}"] = sinc(
                        sc.activation(out=Q_h[:, g * AG1:(g + 1) * AG1], in_=qb,
                                      func=AF.Identity, bias=bqc[:, h:h + 1]),
                        "evA1")
                    for vctr, tt, vb, vmark in vjobs:
                        swaitm(sc, vmark)
                        swait(sc, f"vw{vctr % 2}", 16 * HPC * (vctr // 2))
                        a1ev[f"V{vctr % 4}"] = sinc(
                            sc.activation(out=v_sb[vctr % 2][:, :], in_=vb,
                                          func=AF.Identity), "evA1")
                        swait(sc, "evA1", a1ev[f"V{vctr % 4}"])
                        row = g * AG1 + tt * 128
                        for hh in range(HPC):
                            sinc(sc.dma_start(
                                out=v_dram[hh * T + row:hh * T + row + 128, :],
                                in_=v_sb[vctr % 2][:, hh * 128:(hh + 1) * 128]),
                                f"vw{vctr % 2}", 16)
            a1_pend.append(evacs)

        def emit_vh_load(h):
            @block.scalar
            def _(sc):
                swait(sc, "vw0", 16 * HPC * (T // 256))
                swait(sc, "vw1", 16 * HPC * (T // 256))
                if a2st["av_mark"]:
                    swaitm(sc, a2st["av_mark"][-1])
                KT = T // 128
                for kg in range(0, KT, 8):
                    sinc(sc.dma_start(
                        out=vbuf[:, kg * 128:(kg + 8) * 128].rearrange(
                            "p (k f) -> p k f", k=8),
                        in_=v_dram[h * T + kg * 128:h * T + (kg + 8) * 128, :]
                        .rearrange("(k p) f -> p k f", p=128)), "vhl", 16)


        def emit_a2_strip(h, s, use_pool, psS, psO, psL, psBc):
            kmax = QSC * (s + 1)
            sctr = a2st["s"]
            a2st["s"] += 1

            for kc in range(kmax):
                diag = kc >= QSC * s
                j = kc - QSC * s
                eng_pool = use_pool and (kc % 2 == 1)
                cglob = a2st["chunk"]
                a2st["chunk"] += 1

                @block.tensor
                def _(t):
                    if s == 0 and kc == 0:
                        swait(t, "p0g", p0g_all)
                        swait(t, "mskl", 16)
                        swait(t, "evA1", sval("evA1"))
                    swait(t, "exp", max(0, sval("st") - 1))
                    sinc(t.matmul(psS[kc % 2][:, :], K_h[:, kc * 128:(kc + 1) * 128],
                                  Q_h[:, s * 512:(s + 1) * 512],
                                  start=True, stop=True), "st")

                @block.scalar
                def _(sc):
                    swait(sc, "st", sval("st"))
                    if cglob >= 3:
                        for m in a2st["ptmark"][cglob - 3]:
                            swaitm(sc, m)
                    sinc(sc.activation(out=pt[cglob % 3][:, :],
                                       in_=psS[kc % 2][:, :], func=AF.Exp,
                                       scale=QSCALE), "exp")

                if diag:
                    @block.vector
                    def _(v):
                        if sval("msk") == 0:
                            swait(v, "mskl", 16)
                        swait(v, "exp", sval("exp"))
                        swait(v, "msk", max(0, sval("msk") - 1))
                        sinc(v.tensor_mul(pt[cglob % 3][:, :], pt[cglob % 3][:, :],
                                          masks[:, j * 512:(j + 1) * 512]), "msk")
                ready = ("msk", sval("msk")) if diag else ("exp", sval("exp"))

                if eng_pool:
                    @block.gpsimd
                    def _(g):
                        swaitm(g, ready)
                        if kc == 1:
                            swaitm(g, a2st["lacp_prev"])
                            op = g.tensor_copy(out=accP[:, :], in_=pt[cglob % 3][:, :])
                        else:
                            swait(g, "lacP", sval("lacP"))
                            op = g.tensor_add(accP[:, :], accP[:, :],
                                              pt[cglob % 3][:, :])
                        sinc(op, "lacP")
                    lmark = ("lacP", sval("lacP"))
                else:
                    @block.vector
                    def _(v):
                        swaitm(v, ready)
                        if kc == 0:
                            if a2st["lm_mark"]:
                                swaitm(v, a2st["lm_mark"][-1])
                            op = v.tensor_copy(out=accD[:, :], in_=pt[cglob % 3][:, :])
                        else:
                            swait(v, "lacD", sval("lacD"))
                            op = v.tensor_add(accD[:, :], accD[:, :],
                                              pt[cglob % 3][:, :])
                        sinc(op, "lacD")
                    lmark = ("lacD", sval("lacD"))

                @block.tensor
                def _(t):
                    swaitm(t, ready)
                    swait(t, "vhl", 16 * (T // 1024) * (h + 1))
                    if kc == 0 and sctr >= 2:
                        swaitm(t, a2st["on_mark"][sctr - 2])
                    sinc(t.matmul(psO[sctr % 2][:, :],
                                  vbuf[:, kc * 128:(kc + 1) * 128],
                                  pt[cglob % 3][:, :],
                                  start=(kc == 0), stop=(kc == kmax - 1)), "av")
                a2st["ptmark"].append([("av", sval("av")), lmark])

            a2st["av_mark"].append(("av", sval("av")))

            if use_pool:
                @block.vector
                def _(v):
                    swait(v, "lacP", sval("lacP"))
                    swait(v, "lacD", sval("lacD"))
                    sinc(v.tensor_add(accD[:, :], accD[:, :], accP[:, :]), "lacD")
                a2st["lacp_prev"] = ("lacD", sval("lacD"))

            @block.tensor
            def _(t):
                swait(t, "lacD", sval("lacD"))
                if sctr >= 2:
                    swaitm(t, a2st["psl_free"][sctr - 2])
                sinc(t.matmul(psL[sctr % 2][0:1, :], ones_c[:, :], accD[:, :],
                              start=True, stop=True), "lm")
            a2st["lm_mark"].append(("lm", sval("lm")))

            @block.scalar
            def _(sc):
                swait(sc, "lm", sval("lm"))
                swait(sc, "lin", max(0, sval("lin") - 2))
                sinc(sc.activation(out=lnl_r[sctr % 2][:, :],
                                   in_=psL[sctr % 2][0:1, :], func=AF.Ln), "lin")
                a2st["psl_free"].append(("lin", sval("lin")))
                swait(sc, "lin", sval("lin"))
                swait(sc, "bcm", sval("bcm"))
                sinc(sc.activation(out=linv_r[sctr % 2][:, :],
                                   in_=lnl_r[sctr % 2][:, :],
                                   func=AF.Exp, scale=-1.0), "lin")

            @block.tensor
            def _(t):
                swait(t, "lin", sval("lin"))
                swait(t, "bce", max(0, sval("bcm")))
                sinc(t.matmul(psBc[:, :], ones_r[:, :], linv_r[sctr % 2][:, :],
                              start=True, stop=True), "bcm")

            @block.vector
            def _(v):
                swait(v, "bcm", sval("bcm"))
                swait(v, "on", sval("on"))
                sinc(v.tensor_copy(out=linv_bc[:, :], in_=psBc[:, :]), "bce")
                swait(v, "bce", sval("bce"))
                swait(v, "av", sval("av"))
                swait(v, f"ow{sctr % 2}", 16 * (sctr // 2))
                sinc(v.tensor_mul(o_sb[sctr % 2][:, :], psO[sctr % 2][:, :],
                                  linv_bc[:, :]), "on")
            a2st["on_mark"].append(("on", sval("on")))

            @block.scalar
            def _(sc):
                swait(sc, "on", sval("on"))
                sinc(sc.dma_start(out=o_loc[h * 128:(h + 1) * 128,
                                            s * 512:(s + 1) * 512],
                                  in_=o_sb[sctr % 2][:, :]), f"ow{sctr % 2}", 16)

        a1_gctr = 0
        st_PS = E(contextlib.ExitStack())
        psS0 = st_PS.enter_context(nc.psum_tensor("psS0", [128, 512], F32))
        psS1 = st_PS.enter_context(nc.psum_tensor("psS1", [128, 512], F32))
        psV = st_PS.enter_context(nc.psum_tensor("psV", [128, OSL], F32))
        psO0 = st_PS.enter_context(nc.psum_tensor("psO0", [128, 512], F32))
        psO1 = st_PS.enter_context(nc.psum_tensor("psO1", [128, 512], F32))
        psL0 = st_PS.enter_context(nc.psum_tensor("psL0", [1, 512], F32))
        psL1 = st_PS.enter_context(nc.psum_tensor("psL1", [1, 512], F32))
        psBc = st_PS.enter_context(nc.psum_tensor("psBc", [128, 512], F32))
        with nc.sbuf_tensor("wvT_sb", [128, DCH * OSL], BF16,
                            is_transpose_dest=True) as wvT_sb:

            @block.scalar
            def _(sc):
                swait_cw(sc, cw_qkv_jobs)
                for cg in range(0, DCH, 8):
                    sinc(sc.dma_start(
                        out=wvT_sb[:, cg * OSL:(cg + 8) * OSL].rearrange(
                            "p (c o) -> p c o", c=8),
                        in_=wvT_bnc[cg * 128:(cg + 8) * 128, :].rearrange(
                            "(c p) o -> p c o", p=128)), "wqkx", 16)

            emit_wqkT_load(0, None)
            for s in range(NQS):
                for t4 in range(4):
                    emit_ln_tile(4 * s + t4, 4 * s + t4, x_full, xn_tm)
                emit_xnfm_strip(s)
                flush_a1_evacs()
                for gg in range(2):
                    g = 2 * s + gg
                    emit_a1_group(0, g, a1_gctr, wvT_sb=wvT_sb,
                                  head_first_wait=[("p0ev", sval("p0ev"))] if g == 0 else None)
                    a1_gctr += 1
            flush_a1_evacs()

        vbuf = EA(nc.sbuf_tensor("vbuf", [128, T], BF16))
        pt = [EA(nc.sbuf_tensor(f"pt{i}", [128, 512], BF16)) for i in range(3)]
        accD = EA(nc.sbuf_tensor("accD", [128, 512], F32))
        accP = EA(nc.sbuf_tensor("accP", [128, 512], F32))
        linv_r = [EA(nc.sbuf_tensor(f"linv{i}", [1, 512], F32)) for i in range(2)]
        lnl_r = [EA(nc.sbuf_tensor(f"lnl{i}", [1, 512], F32)) for i in range(2)]
        linv_bc = EA(nc.sbuf_tensor("linv_bc", [128, 512], F32))
        o_sb = [EA(nc.sbuf_tensor(f"o_sb{i}", [128, 512], BF16)) for i in range(2)]

        psS = [psS0, psS1]
        psO = [psO0, psO1]
        psL = [psL0, psL1]
        emit_vh_load(0)
        for s in range(NQS):
            emit_a2_strip(0, s, False, psS, psO, psL, psBc)

        if dbg:
            kq_dbg = nc.declare_dram_parameter("kq_dbg", [2 * OSL, T], BF16,
                                               isOutput=True)

        def emit_kq_dump(h):
            if not dbg:
                return

            @block.scalar
            def _(sc):
                swait(sc, "evA1", sval("evA1"))
                sinc(sc.dma_start(out=kq_dbg[h * 128:(h + 1) * 128, :],
                                  in_=K_h[:, :]), "kqd", 16)
                sinc(sc.dma_start(out=kq_dbg[OSL + h * 128:OSL + (h + 1) * 128, :],
                                  in_=Q_h[:, :]), "kqd", 16)

        emit_kq_dump(0)
        for h in range(1, HPC):
            emit_wqkT_load(h, strip_free[-1])
            for g in range(NAG1):
                flush_a1_evacs()
                emit_a1_group(h, g, a1_gctr,
                              head_first_wait=[("exp", sval("exp"))] if g == 0 else None)
                a1_gctr += 1
            flush_a1_evacs()
            emit_kq_dump(h)
            emit_vh_load(h)
            for s in range(NQS):
                emit_a2_strip(h, s, h >= 2, psS, psO, psL, psBc)

        st_PS.close()

        @block.gpsimd
        def _(g):
            swait(g, "ow0", 16 * ((HPC * NQS + 1) // 2))
            swait(g, "ow1", 16 * ((HPC * NQS) // 2))
            sinc(g.collective_compute(
                "AllGather", ALU.bypass, replica_groups=[list(range(NC))],
                ins=[o_loc.ap().opt()], outs=[o_all.ap().opt()]), "cc")

        if dbg:
            o_loc_dbg = nc.declare_dram_parameter("o_loc_dbg", [OSL, T], BF16,
                                                  isOutput=True)

            @block.sync
            def _(sync):
                swait(sync, "cc", 5)
                for hh in range(HPC):
                    sinc(sync.dma_start(
                        out=o_loc_dbg[hh * 128:(hh + 1) * 128, :],
                        in_=o_loc[hh * 128:(hh + 1) * 128, :]), "odbg", 16)
                swait(sync, "odbg", 16 * HPC)

        st_A.close()

        if upto == "A":
            @block.sync
            def _(sync):
                swait(sync, "cc", 5)
                for t in range(TT):
                    sinc(sync.dma_start(out=out[t * 128:(t + 1) * 128, :],
                                        in_=x_my[t * 128:(t + 1) * 128, :]),
                         "outw0", 16)
                swait(sync, "outw0", sval("outw0"))
            return nc, c

        st_O = E(contextlib.ExitStack())
        EO = st_O.enter_context
        bo_bc = EO(nc.sbuf_tensor("bo_bc", [128, D], F32))
        attn_tail0 = [("exp", sval("exp")), ("on", sval("on")),
                      ("lin", sval("lin")), ("bce", sval("bce"))]
        bo_base = {"row": sval("p0row"), "pe": sval("p0pe"), "ev": sval("p0ev")}
        with nc.sbuf_tensor("rowb2", [1, D], F32) as rowb2, \
             nc.psum_tensor("psBo", [128, 512], F32) as psBo:
            @block.scalar
            def _(sc):
                sinc(sc.dma_start(out=rowb2[0:1, :], in_=bo_in[:, :]), "p0row", 16)

            @block.tensor
            def _(t):
                swait(t, "p0row", bo_base["row"] + 16)
                for m in attn_tail0:
                    swaitm(t, m)
                for k in range(DS):
                    swait(t, "p0ev", bo_base["ev"] + k)
                    sinc(t.matmul(psBo[:, :], ones_r[:],
                                  rowb2[0:1, k * 512:(k + 1) * 512],
                                  start=True, stop=True), "p0pe")

            @block.vector
            def _(v):
                for k in range(DS):
                    swait(v, "p0pe", bo_base["pe"] + k + 1)
                    sinc(v.tensor_copy(out=bo_bc[:, k * 512:(k + 1) * 512],
                                       in_=psBo[:, :]), "p0ev")
        bo_ev_done = sval("p0ev")
        att_sb = EO(nc.sbuf_tensor("att_sb", [128, DCH * ROWS], BF16))
        wo_t = [EO(nc.sbuf_tensor(f"wo_t{i}", [128, 512], BF16)) for i in range(2)]
        tmp_o = [EO(nc.sbuf_tensor(f"tmp_o{i}", [128, 512], F32)) for i in range(2)]
        x1sl = [EO(nc.sbuf_tensor(f"x1sl{i}", [128, 512], F32)) for i in range(2)]
        xsl = [EO(nc.sbuf_tensor(f"xsl{i}", [128, 512], F32)) for i in range(2)]

        @block.sync
        def _(sync):
            swait(sync, "cc", 5)
            swait(sync, "cid", 16)
            reg = sync.value_load(cid_sb[0:1, 0:1])
            reg = sync.snap(reg, min_val=0, max_val=NC - 1)
            for b in sync.Switch(reg, NC):
                for rb in range(8):
                    r0 = rb * (D // 8)
                    sync.dma_start(
                        out=o_my[r0:r0 + D // 8, :],
                        in_=o_all[r0:r0 + D // 8, b * ROWS:(b + 1) * ROWS]
                    ).then_inc(sget("omy"), 16)
            sem_n["omy"] = 16 * 8
            swait(sync, "omy", 16 * 8)
            swait(sync, "p0row", sval("p0row"))
            swait(sync, "p0pe", sval("p0pe"))
            for ac in range(DCH):
                sinc(sync.dma_start(out=att_sb[:, ac * ROWS:(ac + 1) * ROWS],
                                    in_=o_my[ac * 128:(ac + 1) * 128, :]), "attl", 16)

        with contextlib.ExitStack() as stO_ps:
            psOt_l = [stO_ps.enter_context(nc.psum_tensor(f"psOt{i}", [128, 512], F32))
                      for i in range(TT)]
            @block.scalar
            def _(sc):
                swait(sc, "cc", 1)
                swait(sc, "p0row", sval("p0row"))
                swait(sc, "p0pe", sval("p0pe"))
                for k in range(DS * DCH):
                    sl = k % 2
                    swait(sc, "opk", max(0, k - 1))
                    ds, ac = divmod(k, DCH)
                    sinc(sc.dma_start(out=wo_t[sl][:, :],
                                      in_=woT_all[ac * 128:(ac + 1) * 128,
                                                  ds * 512:(ds + 1) * 512]),
                         f"wox{sl}", 16)

            for ds in range(DS):
                @block.tensor
                def _(t):
                    for ac in range(DCH):
                        k = ds * DCH + ac
                        sl = k % 2
                        swait(t, f"wox{sl}", 16 * (k // 2 + 1))
                        if ac == 0:
                            swait(t, "attl", 16 * DCH)
                            if ds == 0:
                                swait(t, "p0ev", bo_ev_done)
                            else:
                                swait(t, "oev", 2 * TT * ds)
                        for tt in range(TT):
                            mm = t.matmul(psOt_l[tt][:, :],
                                          att_sb[:, ac * ROWS + tt * 128:
                                                 ac * ROWS + (tt + 1) * 128],
                                          wo_t[sl][:, :],
                                          start=(ac == 0), stop=(ac == DCH - 1))
                        sinc(mm, "opk")

                for tt in range(TT):
                    k2 = ds * TT + tt
                    sl = k2 % 2

                    @block.sync
                    def _(sync):
                        swait(sync, "oev", max(0, 2 * (k2 - 1)))
                        sinc(sync.dma_start(out=xsl[sl][:, :],
                                            in_=x_my[tt * 128:(tt + 1) * 128,
                                                     ds * 512:(ds + 1) * 512]),
                             f"oxl{sl}", 16)

                    @block.vector
                    def _(v):
                        swait(v, "opk", DCH * (ds + 1))
                        swait(v, "oev", max(0, 2 * (k2 - 1)))
                        sinc(v.tensor_add(tmp_o[sl][:, :],
                                          psOt_l[tt][:, :],
                                          bo_bc[:, ds * 512:(ds + 1) * 512]), "oev")
                        swait(v, "oev", sval("oev"))
                        swait(v, f"oxl{sl}", 16 * (k2 // 2 + 1))
                        swait(v, f"x1w{sl}", 16 * (k2 // 2))
                        sinc(v.tensor_add(x1sl[sl][:, :], tmp_o[sl][:, :],
                                          xsl[sl][:, :]), "oev")

                    @block.sync
                    def _(sync):
                        swait(sync, "oev", 2 * (k2 + 1))
                        sinc(sync.dma_start(out=x1_d[tt * 128:(tt + 1) * 128,
                                                     ds * 512:(ds + 1) * 512],
                                            in_=x1sl[sl][:, :]), f"x1w{sl}", 16)

        NT = c["NT"]
        x1_gate = [("x1w0", sval("x1w0")), ("x1w1", sval("x1w1"))]
        for t in range(TT):
            emit_ln_tile(t, NT + t, x1_d, xn1_tm,
                         gate_marks=x1_gate if t == 0 else ())
        ln2_done = (sval("Lxnw0"), sval("Lxnw1"))

        st_O.close()

        if upto == "O":
            @block.sync
            def _(sync):
                swait(sync, "Lxnw0", ln2_done[0])
                swait(sync, "Lxnw1", ln2_done[1])
                for t in range(TT):
                    sinc(sync.dma_start(out=out[t * 128:(t + 1) * 128, :],
                                        in_=x1_d[t * 128:(t + 1) * 128, :]),
                         "outw0", 16)
                swait(sync, "outw0", sval("outw0"))
            return nc, c

        oev_total = sval("oev")
        opk_total = sval("opk")
        st_LN.close()
        st_F = E(contextlib.ExitStack())
        EF = st_F.enter_context
        sd_bc = EF(nc.sbuf_tensor("sd_bc", [128, D], F32))
        sd_base = {"row": sval("p0row"), "pe": sval("p0pe"), "ev": sval("p0ev")}
        with nc.sbuf_tensor("rowb3", [1, D], F32) as rowb3, \
             nc.psum_tensor("psSd", [128, 512], F32) as psSd:
            @block.scalar
            def _(sc):
                sinc(sc.dma_start(out=rowb3[0:1, :], in_=sd_in[:, :]), "p0row", 16)

            @block.tensor
            def _(t):
                swait(t, "p0row", sd_base["row"] + 16)
                swait(t, "oev", oev_total)
                for k in range(DS):
                    swait(t, "p0ev", sd_base["ev"] + k)
                    sinc(t.matmul(psSd[:, :], ones_r[:],
                                  rowb3[0:1, k * 512:(k + 1) * 512],
                                  start=True, stop=True), "p0pe")

            @block.vector
            def _(v):
                for k in range(DS):
                    swait(v, "p0pe", sd_base["pe"] + k + 1)
                    sinc(v.tensor_copy(out=sd_bc[:, k * 512:(k + 1) * 512],
                                       in_=psSd[:, :]), "p0ev")
        sd_ev_done = sval("p0ev")
        xn1_sb = EF(nc.sbuf_tensor("xn1_sb", [128, DCH * TG], BF16,
                                   is_transpose_dest=True))
        hh_sb = EF(nc.sbuf_tensor("hh_sb", [128, ICH * TG], BF16))
        wgs = [EF(nc.sbuf_tensor(f"wgs{i}", [128, DCH * 128], BF16)) for i in range(2)]
        wus = [EF(nc.sbuf_tensor(f"wus{i}", [128, DCH * 128], BF16)) for i in range(2)]
        wds = [EF(nc.sbuf_tensor(f"wds{i}", [128, 512], BF16)) for i in range(2)]
        sgt = [EF(nc.sbuf_tensor(f"sgt{i}", [128, TG], BF16)) for i in range(2)]
        tmp2 = [EF(nc.sbuf_tensor(f"tmp2{i}", [128, 512], F32)) for i in range(2)]
        x1s = [EF(nc.sbuf_tensor(f"x1s{i}", [128, 512], F32)) for i in range(2)]
        x2s = [EF(nc.sbuf_tensor(f"x2s{i}", [128, 512], F32)) for i in range(2)]

        fst = {"ic": 0, "wd": 0, "ev": 0}
        with nc.psum_tensor("psG0", [128, TG], F32) as psG0, \
             nc.psum_tensor("psG1", [128, TG], F32) as psG1, \
             nc.psum_tensor("psU0", [128, TG], F32) as psU0, \
             nc.psum_tensor("psU1", [128, TG], F32) as psU1, \
             nc.psum_tensor("psD", [128, TGT * 512], F32) as psD:
            psG = [psG0, psG1]
            psU = [psU0, psU1]

            for tg in range(NTG):
                @block.scalar
                def _(sc):
                    swait(sc, "Lxnw0", ln2_done[0])
                    swait(sc, "Lxnw1", ln2_done[1])
                    if tg > 0:
                        swait(sc, "pU", sval("pU"))
                    else:
                        swait(sc, "opk", opk_total)
                        swait(sc, "p0row", sval("p0row"))
                        swait(sc, "p0pe", sval("p0pe"))
                    for hc in range(DCH):
                        sinc(sc.dma_start(out=xn1_sb[:, hc * TG:(hc + 1) * TG],
                                          in_=xn1_tm[tg * TG:(tg + 1) * TG,
                                                     hc * 128:(hc + 1) * 128],
                                          transpose=True), "xg1", 16)

                for ic in range(ICH):
                    kk = fst["ic"]
                    fst["ic"] += 1
                    sl = kk % 2

                    @block.sync
                    def _(sync):
                        swait(sync, "pG", max(0, sval("pG") - 1))
                        sinc(sync.dma_start(
                            out=wgs[sl].ap().rearrange("p (c i) -> p c i", c=DCH),
                            in_=wgT_all.ap().rearrange("(c q p) i -> p c q i",
                                                       q=ICHP, p=128)[
                                :, :, ic, :]), f"wgl{sl}", 16)
                        swait(sync, "pU", max(0, sval("pU") - 1))
                        sinc(sync.dma_start(
                            out=wus[sl].ap().rearrange("p (c i) -> p c i", c=DCH),
                            in_=wuT_all.ap().rearrange("(c q p) i -> p c q i",
                                                       q=ICHP, p=128)[
                                :, :, ic, :]), f"wul{sl}", 16)

                    @block.tensor
                    def _(t):
                        swait(t, f"wgl{sl}", 16 * (kk // 2 + 1))
                        swait(t, "xg1", 16 * DCH * (tg + 1))
                        swait(t, "sil", max(0, sval("sil") - 1))
                        if tg == 0 and ic == 0:
                            swait(t, "cc", 4)
                            swait(t, "p0ev", sd_ev_done)
                        for hc in range(DCH):
                            mg = t.matmul(psG[sl][:, :],
                                          wgs[sl][:, hc * 128:(hc + 1) * 128],
                                          xn1_sb[:, hc * TG:(hc + 1) * TG],
                                          start=(hc == 0), stop=(hc == DCH - 1))
                        sinc(mg, "pG")
                        swait(t, f"wul{sl}", 16 * (kk // 2 + 1))
                        swait(t, "hhe", max(0, sval("hhe") - 1))
                        for hc in range(DCH):
                            mu = t.matmul(psU[sl][:, :],
                                          wus[sl][:, hc * 128:(hc + 1) * 128],
                                          xn1_sb[:, hc * TG:(hc + 1) * TG],
                                          start=(hc == 0), stop=(hc == DCH - 1))
                        sinc(mu, "pU")

                    @block.scalar
                    def _(sc):
                        swait(sc, "pG", sval("pG"))
                        swait(sc, "hhe", max(0, sval("hhe") - 1))
                        sinc(sc.activation(out=sgt[sl][:, :], in_=psG[sl][:, :],
                                           func=AF.Silu,
                                           scale=sgc[:, ic:ic + 1]), "sil")

                    @block.vector
                    def _(v):
                        swait(v, "pU", sval("pU"))
                        swait(v, "sil", sval("sil"))
                        if tg > 0 and ic == 0:
                            swait(v, "pD", sval("pD"))
                        sinc(v.scalar_tensor_tensor(
                            out=hh_sb[:, ic * TG:(ic + 1) * TG],
                            in0=psU[sl][:, :], scalar=suc[:, ic:ic + 1],
                            in1=sgt[sl][:, :], op0=ALU.mult, op1=ALU.mult), "hhe")

                for hs in range(DS):
                    for ic in range(ICH):
                        kd = fst["wd"]
                        fst["wd"] += 1
                        sl = kd % 2

                        @block.sync
                        def _(sync):
                            swait(sync, "pD", max(0, sval("pD") - 1))
                            sinc(sync.dma_start(out=wds[sl][:, :],
                                                in_=wdT_all[ic * 128:(ic + 1) * 128,
                                                            hs * 512:(hs + 1) * 512]),
                                 f"wdl{sl}", 16)

                        @block.tensor
                        def _(t):
                            swait(t, f"wdl{sl}", 16 * (kd // 2 + 1))
                            if ic == 0:
                                swait(t, "hhe", ICH * (tg + 1))
                                swait(t, "x2e", sval("x2e"))
                            for tt in range(TGT):
                                md = t.matmul(psD[:, tt * 512:tt * 512 + 512],
                                              hh_sb[:, ic * TG + tt * 128:
                                                    ic * TG + (tt + 1) * 128],
                                              wds[sl][:, :],
                                              start=(ic == 0), stop=(ic == ICH - 1))
                            sinc(md, "pD")

                    for tt in range(TGT):
                        ke = fst["ev"]
                        fst["ev"] += 1
                        sl = ke % 2
                        row = tg * TG + tt * 128

                        @block.sync
                        def _(sync):
                            swait(sync, "x2e", max(0, 2 * (ke - 1)))
                            swait(sync, "x1w0", sval("x1w0"))
                            swait(sync, "x1w1", sval("x1w1"))
                            sinc(sync.dma_start(out=x1s[sl][:, :],
                                                in_=x1_d[row:row + 128,
                                                         hs * 512:(hs + 1) * 512]),
                                 f"fx1{sl}", 16)

                        @block.vector
                        def _(v):
                            swait(v, "pD", sval("pD"))
                            swait(v, "x2e", max(0, 2 * (ke - 1)))
                            sinc(v.tensor_mul(tmp2[sl][:, :],
                                              psD[:, tt * 512:tt * 512 + 512],
                                              sd_bc[:, hs * 512:(hs + 1) * 512]),
                                 "x2e")
                            swait(v, "x2e", sval("x2e"))
                            swait(v, f"fx1{sl}", 16 * (ke // 2 + 1))
                            swait(v, f"outw{sl}", 16 * (ke // 2))
                            sinc(v.tensor_add(x2s[sl][:, :], tmp2[sl][:, :],
                                              x1s[sl][:, :]), "x2e")

                        @block.sync
                        def _(sync):
                            swait(sync, "x2e", 2 * (ke + 1))
                            sinc(sync.dma_start(out=out[row:row + 128,
                                                        hs * 512:(hs + 1) * 512],
                                                in_=x2s[sl][:, :]), f"outw{sl}", 16)

        @block.sync
        def _(sync):
            swait(sync, "outw0", sval("outw0"))
            swait(sync, "outw1", sval("outw1"))

        st_F.close()

    return nc, c


def _shard_inputs(c, inputs):
    T, D, I = c["T"], c["D"], c["I"]
    ROWS, OSL, DPC = c["ROWS"], c["OSL"], c["DPC"]
    IPC, IPAD, HCC, ICHP = c["IPC"], c["IPAD"], c["HCC"], c["ICHP"]
    x = np.ascontiguousarray(np.asarray(inputs["L0_attn_x"], np.float32).reshape(T, D))
    wq = np.asarray(inputs["L0_attn_Wq"], np.float32)
    wk = np.asarray(inputs["L0_attn_Wk"], np.float32)
    wv = np.asarray(inputs["L0_attn_Wv"], np.float32)
    bq = np.asarray(inputs["L0_attn_bq"], np.float32).reshape(1, D)
    bk = np.asarray(inputs["L0_attn_bk"], np.float32).reshape(1, D)
    bv = np.asarray(inputs["L0_attn_bv"], np.float32).reshape(1, D)
    wo = np.asarray(inputs["L0_attn_Wo"], np.float32)
    bo = np.asarray(inputs["L0_attn_bo"], np.float32).reshape(1, D)
    wg = np.asarray(inputs["L0_ffn_w_gate"], np.int32)
    wu = np.asarray(inputs["L0_ffn_w_up"], np.int32)
    wd = np.asarray(inputs["L0_ffn_w_down"], np.int32)
    sg = np.zeros((1, IPAD), np.float32)
    sg[0, :I] = np.asarray(inputs["L0_ffn_s_gate"], np.float32)
    su = np.zeros((1, IPAD), np.float32)
    su[0, :I] = np.asarray(inputs["L0_ffn_s_up"], np.float32)
    sd = np.asarray(inputs["L0_ffn_s_down"], np.float32).reshape(1, D)

    def blk_gate(w, q):
        a = np.zeros((IPAD, DPC), np.int32)
        a[:I, :] = w[:, q * DPC:(q + 1) * DPC]
        a = a.reshape(ICHP, 128, HCC, 128)
        a = a.transpose(2, 0, 3, 1)
        return np.ascontiguousarray(a.reshape(HCC * ICHP * 128, 128))

    def wdT_slice(q):
        a = np.zeros((IPC * 128, D), np.int32)
        lo = q * IPC * 128
        hi = min(I, (q + 1) * IPC * 128)
        if hi > lo:
            a[0:hi - lo, :] = wd[:, lo:hi].T
        return np.ascontiguousarray(a)

    import ml_dtypes
    p_ = np.arange(128)[:, None]
    f_ = np.arange(512)[None, :]
    mask = np.concatenate(
        [(f_ - p_ - 128 * j >= 0) for j in range(4)],
        axis=1).astype(ml_dtypes.bfloat16)
    in_maps = []
    for q in range(NC):
        in_maps.append({
            "x_full": x,
            "x_my": np.ascontiguousarray(x[q * ROWS:(q + 1) * ROWS, :]),
            "wqT_sl": np.ascontiguousarray(wq[q * OSL:(q + 1) * OSL, :].T),
            "wkT_sl": np.ascontiguousarray(wk[q * OSL:(q + 1) * OSL, :].T),
            "wvT_sl": np.ascontiguousarray(wv[q * OSL:(q + 1) * OSL, :].T),
            "bq_sl": np.ascontiguousarray(bq[:, q * OSL:(q + 1) * OSL]),
            "bk_sl": np.ascontiguousarray(bk[:, q * OSL:(q + 1) * OSL]),
            "bv_sl": np.ascontiguousarray(bv[:, q * OSL:(q + 1) * OSL]),
            "woT_sl": np.ascontiguousarray(wo[:, q * DPC:(q + 1) * DPC].T),
            "bo_in": bo,
            "wgT_blk": blk_gate(wg, q),
            "wuT_blk": blk_gate(wu, q),
            "wdT_sl": wdT_slice(q),
            "sg_in": sg,
            "su_in": su,
            "sd_in": sd,
            "cid_in": np.array([[q]], np.int32),
            "mask_in": mask,
        })
    return in_maps


def kernel(**inputs):
    T, D, HPC, I = 8192, 4096, 4, 11008
    nc, c = build(T, D, HPC, I)
    in_maps = _shard_inputs(c, inputs)
    res = run_bass_kernel_spmd(nc, in_maps, core_ids=list(range(NC)))
    shards = [np.asarray(res.results[q]["out"]) for q in range(NC)]
    return np.concatenate(shards, axis=0).reshape(1, T, D).astype(np.float32)
